# version 02
# baseline (speedup 1.0000x reference)
import os
os.environ.setdefault("NEURON_SCRATCHPAD_PAGE_SIZE", "640")
import contextlib
import numpy as np
import ml_dtypes
import concourse.bacc as bacc_mod
import concourse.mybir as mybir
from concourse.bass import ds
from concourse.bass_utils import run_bass_kernel_spmd

F32 = mybir.dt.float32
I32 = mybir.dt.int32
BF16 = mybir.dt.bfloat16
AF = mybir.ActivationFunctionType
ALU = mybir.AluOpType

BF16 = mybir.dt.bfloat16
I32 = mybir.dt.int32
AF = mybir.ActivationFunctionType
ALU = mybir.AluOpType

TT = 512
PT = 128
D = 1024
DF = 4096
NH = 16
LN_EPS = 1e-5
HE_FFN = 64
HE_ATT = 128
WF = TT + 2 * HE_FFN
WA = TT + 2 * HE_ATT
MAGIC = 12582912.0
GN = [(1.0, 0.0),
      (0.7847558856010437, 2.4108107510301124e-08),
      (0.3021286129951477, 1.8317987127147717e-09),
      (0.060451213270425797, 3.7214936687846034e-10),
      (0.005342473741620779, 0.0)]
GD = [(1.0, 0.0),
      (1.582640528678894, -4.984901735838321e-08),
      (1.0648928880691528, -2.0261421429523807e-09),
      (0.38475489616394043, 9.13366449140085e-09),
      (0.07580475509166718, 0.0),
      (0.00669455248862505, 0.0)]

INNER = 8
B = 8
SL = 1024
NMG = 4 * SL // 128
NMW = SL // 128
NT = NMG + NMW
CHA = 64
TCA = CHA * B
GSC = NMG * 8

def _wrap(eng):
    class W:
        def __getattr__(self, name):
            fn = getattr(eng, name)
            def call(*a, **k):
                inst = fn(*a, **k)
                eng.drain()
                return inst
            return call
    return W()


class P2:
    def __init__(self, nc, t, ntok):
        self.nc = nc
        self.t = t
        self.ntok = ntok
        self.TP = ntok + 2 * PT
        self.V = _wrap(nc.vector)
        self.S = _wrap(nc.scalar)
        self.scope = None
        self.rG = nc.gpsimd.alloc_register("p2rG")
        self.rV = nc.vector.alloc_register("p2rV")
        self.rGO = nc.gpsimd.alloc_register("p2rGO")
        self.rSO = nc.sync.alloc_register("p2rSO")
        self.rT = nc.tensor.alloc_register("p2rT")
        self.rVo = nc.vector.alloc_register("p2rVo")
        self.rS = nc.scalar.alloc_register("p2rS")
        self.tDP = nc.tensor.alloc_register("p2tDP")
        self.tDPq = nc.tensor.alloc_register("p2tDPq")
        self.vPDq = nc.vector.alloc_register("p2vPDq")
        self.tAP = nc.tensor.alloc_register("p2tAP")
        self.vPD = nc.vector.alloc_register("p2vPD")
        self.vAD = nc.vector.alloc_register("p2vAD")
        self.sPA = nc.scalar.alloc_register("p2sPA")
        self.sDA = nc.scalar.alloc_register("p2sDA")
        self.rY = nc.sync.alloc_register("p2rY")
        self.rY2 = nc.sync.alloc_register("p2rY2")
        self.vSD = nc.vector.alloc_register("p2vSD")
        nc.gpsimd.reg_mov(self.rGO, 0)
        nc.sync.reg_mov(self.rSO, 0)
        self.oG = None
        self.oS = None
        self._stage_regs = []


    def bar(self):
        self.nc.all_engine_barrier()

    def edges_init(self):
        nc, t = self.nc, self.t
        for nm in ("eDP", "ePD", "ePA", "eAP", "eAD", "eDA", "eDS", "eDPq",
                   "ePDq"):
            t[nm] = nc.alloc_semaphore(name="p2" + nm)
        for eng, reg in ((nc.tensor, self.tDP), (nc.tensor, self.tAP),
                         (nc.vector, self.vPD), (nc.vector, self.vAD),
                         (nc.scalar, self.sPA), (nc.scalar, self.sDA),
                         (nc.sync, self.rY), (nc.tensor, self.tDPq),
                         (nc.vector, self.vPDq), (nc.vector, self.vSD)):
            eng.reg_mov(reg, 0)

    def ewait(self, eng, reg, sem, k=1):
        if k:
            eng.reg_add(reg, reg, k)
        eng.wait_ge(sem, reg)

    def sb(self, name, shape, dtype=F32):
        uname = f"{self.stage_name}_{name}"
        self.t[name] = self.scope.enter_context(self.nc.sbuf_tensor(uname, shape, dtype))
        return self.t[name]

    def flush_regs(self):
        for eng in (self.nc.tensor, self.nc.vector, self.nc.scalar,
                    self.nc.gpsimd, self.nc.sync):
            try:
                eng.get_value_cache().barrier()
            except Exception:
                pass

    @contextlib.contextmanager
    def tloop(self, ntile):
        nc = self.nc
        nc.gpsimd.reg_mov(self.rGO, 0)
        nc.sync.reg_mov(self.rSO, 0)
        mx = max(0, self.ntok - TT)
        with nc.Fori(0, ntile) as i:
            self.oG = nc.gpsimd.snap(self.rGO, min_val=0, max_val=mx)
            self.oS = nc.sync.snap(self.rSO, min_val=0, max_val=mx)
            for ov in (self.oG, self.oS):
                try:
                    self._stage_regs.append(ov.val)
                except Exception:
                    pass
            yield i
            nc.gpsimd.reg_add(self.rGO, self.rGO, TT)
            nc.sync.reg_add(self.rSO, self.rSO, TT)

    @contextlib.contextmanager
    def regscope(self):
        nc = self.nc
        prev_stage_regs = self._stage_regs
        self._stage_regs = []
        allocated = []
        orig = nc.alloc_registers
        orig64 = nc.alloc_registers64
        def wrap(*a, **k):
            hs = orig(*a, **k)
            try:
                for h in hs:
                    allocated.append(h)
            except Exception:
                pass
            return hs
        def wrap64(*a, **k):
            hs = orig64(*a, **k)
            try:
                for h in hs:
                    allocated.append(h.lo)
                    allocated.append(h.hi)
            except Exception:
                pass
            return hs
        nc.alloc_registers = wrap
        nc.alloc_registers64 = wrap64
        engs = (nc.tensor, nc.vector, nc.scalar, nc.gpsimd, nc.sync)
        eorig = []
        for e in engs:
            o1, o64 = e.alloc_register, e.alloc_register64
            eorig.append((e, o1, o64))
            def mk1(o1):
                def f(*a, **k):
                    h = o1(*a, **k)
                    allocated.append(h)
                    return h
                return f
            def mk64(o64):
                def f(*a, **k):
                    h = o64(*a, **k)
                    allocated.append(h.lo)
                    allocated.append(h.hi)
                    return h
                return f
            e.alloc_register = mk1(o1)
            e.alloc_register64 = mk64(o64)
        try:
            yield
        finally:
            nc.alloc_registers = orig
            nc.alloc_registers64 = orig64
            for e, o1, o64 in eorig:
                e.alloc_register = o1
                e.alloc_register64 = o64
            for h in allocated + self._stage_regs:
                try:
                    nc.free_register(h)
                except Exception:
                    pass
            self._stage_regs = prev_stage_regs
            self.flush_regs()

    @contextlib.contextmanager
    def stage(self, name):
        old = self.scope
        olds = getattr(self, "stage_name", "")
        self.stage_name = name
        self.scope = contextlib.ExitStack()
        with self.regscope():
            yield
            self.bar()
        self.scope.close()
        self.scope = old
        self.stage_name = olds

    def mmN(self, psum, off, lhsT, rhs, W, start, stop, inc=None):
        a = 0
        while a < W:
            b = min(W, ((off + a) // 512 + 1) * 512 - off)
            inst = self.nc.tensor.matmul(psum[:, off + a:off + b], lhsT,
                                         rhs[:, a:b], start=start, stop=stop)
            if inc is not None and b == W:
                inst.then_inc(inc, 1)
            a = b

    def chunk_ap(self, dram, ti, halo=0, nk=8):
        return dram.rearrange("p (k t) -> p k t", t=self.TP)[
            :, 0:nk, ds(ti * TT + PT - halo, TT + 2 * halo)]

    def chg(self, dram, halo=0, nk=8):
        return dram.rearrange("p (k t) -> p k t", t=self.TP)[
            :, 0:nk, PT - halo:][:, :, ds(self.oG, TT + 2 * halo)]

    def chs(self, dram, nk=8):
        return dram.rearrange("p (k t) -> p k t", t=self.TP)[
            :, 0:nk, PT:][:, :, ds(self.oS, TT)]

    def flatg(self, dram, t_extent, nk):
        return dram.rearrange("p (k t) -> p k t", t=t_extent)[
            :, 0:nk, :][:, :, ds(self.oG, TT)]

    def flats(self, dram, t_extent, nk):
        return dram.rearrange("p (k t) -> p k t", t=t_extent)[
            :, 0:nk, :][:, :, ds(self.oS, TT)]

    def ln_stats(self, xv, nk, W, zero_mean=False):
        nc, V, S, t = self.nc, self.V, self.S, self.t
        psB = t["psB"]
        psS = t["psA"][0:1, :]
        sqb, rowb = t["sqb"], t["rowb"]
        onescf, ones128 = t["onescf"], t["ones128"]
        for r in range(0, nk, 4):
            re = min(r + 4, nk)
            for k in range(r, re):
                S.activation(out=sqb[:, (k - r) * W:(k - r + 1) * W],
                             in_=xv[:, k, 0:W], func=AF.Square)
            self.bar()
            for k in range(r, re):
                if not zero_mean:
                    self.mmN(psS, 0, onescf[:], xv[:, k, 0:W], W,
                             start=(k == 0), stop=(k == nk - 1))
                self.mmN(psS, 1024, onescf[:], sqb[:, (k - r) * W:(k - r + 1) * W],
                         W, start=(k == 0), stop=(k == nk - 1))
            self.bar()
        MU, VAR, NMU = 0, W, 2 * W
        if not zero_mean:
            V.tensor_scalar_mul(rowb[:, MU:MU + W], psS[:, 0:W], 1.0 / D)
            V.tensor_scalar_mul(rowb[:, NMU:NMU + W], rowb[:, MU:MU + W], -1.0)
        V.tensor_scalar_mul(rowb[:, VAR:VAR + W], psS[:, 1024:1024 + W], 1.0 / D)
        if not zero_mean:
            V.tensor_tensor(rowb[:, MU:MU + W], rowb[:, MU:MU + W],
                            rowb[:, MU:MU + W], ALU.mult)
            V.tensor_tensor(rowb[:, VAR:VAR + W], rowb[:, VAR:VAR + W],
                            rowb[:, MU:MU + W], ALU.subtract)
        self.bar()
        S.activation(out=rowb[:, VAR:VAR + W], in_=rowb[:, VAR:VAR + W],
                     func=AF.Ln, bias=t["epsb"][:, 0:1])
        S.activation(out=rowb[:, VAR:VAR + W], in_=rowb[:, VAR:VAR + W],
                     func=AF.Exp, scale=-0.5)
        self.bar()
        if not zero_mean:
            self.mmN(psB, 0, t["ones128f"][:], rowb[:, NMU:NMU + W], W,
                     start=True, stop=True)
        self.mmN(psB, 768, t["ones128f"][:], rowb[:, VAR:VAR + W], W,
                 start=True, stop=True)
        self.bar()
        if not zero_mean:
            V.tensor_copy(t["nmuB"][:, 0:W], psB[:, 0:W])
        V.tensor_copy(t["invsB"][:, 0:W], psB[:, 768:768 + W])
        self.bar()

    def act_quant(self, sv, nk, W, wscale_ap):
        nc, V, S, t = self.nc, self.V, self.S, self.t
        psB = t["psB"]
        fold, ident = t["fold"], t["ident"]
        nchunk = W // 128
        S.activation(out=fold[:, 0:W], in_=sv[:, 0, 0:W], func=AF.Abs)
        for k in range(1, nk):
            S.activation(out=t["qtmp"][:, 0:W], in_=sv[:, k, 0:W], func=AF.Abs)
            self.bar()
            V.tensor_tensor(fold[:, 0:W], fold[:, 0:W], t["qtmp"][:, 0:W], ALU.max)
            self.bar()
        for q in range(nchunk):
            nc.tensor.transpose(psB[:, 768 + q * 128:768 + (q + 1) * 128],
                                fold[:, q * 128:(q + 1) * 128], ident[:])
        self.bar()
        for q in range(nchunk):
            V.tensor_reduce(t["colcl"][:, q:q + 1],
                            psB[:, 768 + q * 128:768 + (q + 1) * 128],
                            mybir.AxisListType.X, ALU.max)
        self.bar()
        V.tensor_scalar_max(t["colcl"][:, 0:nchunk], t["colcl"][:, 0:nchunk], 1e-5)
        self.bar()
        S.activation(out=t["cl2"][:, 0:nchunk], in_=t["colcl"][:, 0:nchunk], func=AF.Ln)
        S.activation(out=t["cl2"][:, 0:nchunk], in_=t["cl2"][:, 0:nchunk],
                     func=AF.Exp, scale=-1.0)
        self.bar()
        V.tensor_scalar_mul(t["cl2"][:, 0:nchunk], t["cl2"][:, 0:nchunk], 127.0)
        V.tensor_single_scalar(t["colcl"][:, 0:nchunk], t["colcl"][:, 0:nchunk],
                               wscale_ap, ALU.mult)
        V.tensor_scalar_mul(t["colcl"][:, 0:nchunk], t["colcl"][:, 0:nchunk],
                            1.0 / 127.0)
        self.bar()
        for q in range(nchunk):
            nc.tensor.matmul(psB[0:1, q * 128:(q + 1) * 128],
                             t["cl2"][:, q:q + 1], t["ident"][:],
                             start=True, stop=True)
            nc.tensor.matmul(psB[0:1, 768 + q * 128:768 + (q + 1) * 128],
                             t["colcl"][:, q:q + 1], t["ident"][:],
                             start=True, stop=True)
        self.bar()
        V.tensor_copy(t["qrow"][:, 0:W], psB[0:1, 0:W])
        V.tensor_copy(t["qrow"][:, 768:768 + W], psB[0:1, 768:768 + W])
        self.bar()
        self.mmN(psB, 0, t["ones128f"][:], t["qrow"][:, 0:W], W, start=True, stop=True)
        self.mmN(psB, 768, t["ones128f"][:], t["qrow"][:, 768:768 + W], W,
                 start=True, stop=True)
        self.bar()
        V.tensor_copy(t["sxb"][:, 0:W], psB[:, 0:W])
        V.tensor_copy(t["oscb"][:, 0:W], psB[:, 768:768 + W])
        self.bar()

    def rint_hop(self, W):
        V, t = self.V, self.t
        kint = t["kint"]
        V.tensor_copy(kint[:, 0:W], t["qtmp"][:, 0:W])
        V.tensor_copy(t["qtmp2"][:, 0:W], kint[:, 0:W])
        V.tensor_tensor(t["qtmp"][:, 0:W], t["qtmp"][:, 0:W],
                        t["qtmp2"][:, 0:W], ALU.subtract)
        V.tensor_scalar(t["qtmp"][:, 0:W], t["qtmp"][:, 0:W], 2.0, None, ALU.mult)
        V.tensor_copy(kint[:, 0:W], t["qtmp"][:, 0:W])
        V.tensor_copy(t["qtmp"][:, 0:W], kint[:, 0:W])
        V.tensor_tensor(t["qtmp2"][:, 0:W], t["qtmp2"][:, 0:W],
                        t["qtmp"][:, 0:W], ALU.add)

    def quantize(self, sv, kt, nk, W):
        V, t = self.V, self.t
        for k in range(nk):
            V.tensor_tensor(t["qtmp"][:, 0:W], sv[:, k, 0:W], t["sxb"][:, 0:W],
                            ALU.mult)
            V.tensor_scalar(t["qtmp"][:, 0:W], t["qtmp"][:, 0:W], 127.0, -127.0,
                            ALU.min, ALU.max)
            V.tensor_scalar(t["qtmp"][:, 0:W], t["qtmp"][:, 0:W], MAGIC, -MAGIC,
                            ALU.add, ALU.add)
            self.rint_hop(W)
            V.tensor_copy(kt[:, k * W:(k + 1) * W], t["qtmp2"][:, 0:W])
        self.bar()

    def quantize2(self, sv, ktv, nk, W):
        V, t = self.V, self.t
        for k in range(nk):
            V.tensor_tensor(t["qtmp"][:, 0:W], sv[:, k, 0:W], t["sxb"][:, 0:W],
                            ALU.mult)
            V.tensor_scalar(t["qtmp"][:, 0:W], t["qtmp"][:, 0:W], 127.0, -127.0,
                            ALU.min, ALU.max)
            V.tensor_scalar(t["qtmp"][:, 0:W], t["qtmp"][:, 0:W], MAGIC, -MAGIC,
                            ALU.add, ALU.add)
            self.rint_hop(W)
            V.tensor_copy(ktv[:, k, 0:W], t["qtmp2"][:, 0:W])
        self.bar()


def emit_wprep(p, wsrc, wqd, sc_key, nk, nm, tag):
    nc, t, V, S = p.nc, p.t, p.V, p.S
    ld, st = t["ld_w" + tag], t["st_w" + tag]
    M = nm * 128
    with p.stage("wprep" + tag):
        wtmp = p.sb("wtmp", [128, M])
        wtmp2 = p.sb("wtmp2", [128, M])
        wint = p.sb("wint", [128, M], I32)
        wbf = p.sb("wbf", [128, M], BF16)
        acc = p.sb("wacc", [128, 2])
        sc2 = p.sb("wsc2", [1, 2])
        swcol = p.sb("swcol", [128, 1])
        nc.vector.memset(acc[:, 0:1], 0.0)
        nc.vector.drain()
        p.bar()
        for k in range(nk):
            nc.gpsimd.dma_start(out=wtmp[:], in_=wsrc[k * 128:(k + 1) * 128, :]
                                ).then_inc(ld, 16)
            nc.vector.wait_ge(ld, 16 * (k + 1))
            p.bar()
            V.tensor_reduce(acc[:, 1:2], wtmp[:], mybir.AxisListType.X,
                            ALU.add, apply_absolute_value=True)
            V.tensor_tensor(acc[:, 0:1], acc[:, 0:1], acc[:, 1:2], ALU.add)
            p.bar()
        p.bar()
        nc.tensor.matmul(t["psB"][0:1, 0:1], acc[:, 0:1], t["onescf"][:],
                         start=True, stop=True)
        p.bar()
        V.tensor_scalar_mul(sc2[:, 0:1], t["psB"][0:1, 0:1], 1.0 / (nk * 128 * M))
        V.tensor_scalar_max(sc2[:, 0:1], sc2[:, 0:1], 1e-5)
        V.tensor_copy(t[sc_key][0:1, 0:1], sc2[:, 0:1])
        p.bar()
        S.activation(out=sc2[:, 1:2], in_=sc2[:, 0:1], func=AF.Ln)
        S.activation(out=sc2[:, 1:2], in_=sc2[:, 1:2], func=AF.Exp, scale=-1.0)
        p.bar()
        nc.tensor.matmul(t["psB"][:, 0:1], t["ones128f"][:], sc2[:, 1:2],
                         start=True, stop=True)
        nc.tensor.matmul(t["psB"][:, 512:513], t["ones128f"][:], sc2[:, 0:1],
                         start=True, stop=True)
        p.bar()
        V.tensor_copy(swcol[:], t["psB"][:, 0:1])
        V.tensor_copy(t[sc_key + "c"][:, 0:1], t["psB"][:, 512:513])
        p.bar()
        for k in range(nk):
            nc.gpsimd.dma_start(out=wtmp[:], in_=wsrc[k * 128:(k + 1) * 128, :]
                                ).then_inc(ld, 16)
            nc.vector.wait_ge(ld, 16 * (nk + k + 1))
            p.bar()
            V.tensor_single_scalar(wtmp[:], wtmp[:], swcol[:, 0:1], ALU.mult)
            V.tensor_scalar(wtmp[:], wtmp[:], 1.0, -1.0, ALU.min, ALU.max)
            V.tensor_scalar(wtmp[:], wtmp[:], MAGIC, -MAGIC, ALU.add, ALU.add)
            V.tensor_copy(wint[:], wtmp[:])
            V.tensor_copy(wtmp2[:], wint[:])
            V.tensor_tensor(wtmp[:], wtmp[:], wtmp2[:], ALU.subtract)
            V.tensor_scalar(wtmp[:], wtmp[:], 2.0, None, ALU.mult)
            V.tensor_copy(wint[:], wtmp[:])
            V.tensor_copy(wtmp[:], wint[:])
            V.tensor_tensor(wtmp2[:], wtmp2[:], wtmp[:], ALU.add)
            V.tensor_copy(wbf[:], wtmp2[:])
            p.bar()
            nc.sync.dma_start(
                out=wqd.rearrange("p (m k c) -> p m k c", k=nk, c=128)[:, :, k, :],
                in_=wbf.rearrange("p (m c) -> p m c", c=128)).then_inc(st, 16)
            nc.sync.wait_ge(st, 16 * (k + 1))
        p.bar()


def emit_s1(p):
    nc, t, V, S = p.nc, p.t, p.V, p.S
    ntile = p.ntok // TT
    ld, st = t["ld_s1"], t["st_s1"]
    psB = t["psB"]
    W = TT
    BCB = 768
    with p.stage("s1"):
        xt = p.sb("s1x", [128, 8 * W])
        ht = p.sb("s1h", [128, 8 * W])
        yt = p.sb("s1y", [128, 8 * W])
        stg = p.sb("s1stg", [128, 8 * W])
        nc.gpsimd.reg_mov(p.rG, 0)
        nc.vector.reg_mov(p.rV, 32)
        with p.tloop(ntile) as i:
            nc.gpsimd.wait_ge(st, p.rG)
            nc.gpsimd.reg_add(p.rG, p.rG, 16)
            nc.gpsimd.dma_start(out=xt.rearrange("p (k t) -> p k t", t=W),
                                in_=t["xb"].rearrange("(k p) t -> p k t", p=128)[
                                    :, :, ds(p.oG, TT)]).then_inc(ld, 16)
            nc.gpsimd.dma_start(out=ht.rearrange("p (k t) -> p k t", t=W),
                                in_=p.flatg(t["hd"], p.ntok, 8)).then_inc(ld, 16)
            nc.vector.wait_ge(ld, p.rV)
            nc.vector.reg_add(p.rV, p.rV, 32)
            p.bar()
            xv = xt.rearrange("p (k t) -> p k t", t=W)
            yv = yt.rearrange("p (k t) -> p k t", t=W)
            p.ln_stats(xv, 8, W)
            for k in range(8):
                V.tensor_tensor(t["qtmp"][:, 0:W], xv[:, k, :], t["nmuB"][:, 0:W], ALU.add)
                V.tensor_tensor(t["qtmp"][:, 0:W], t["qtmp"][:, 0:W],
                                t["invsB"][:, 0:W], ALU.mult)
                V.tensor_tensor(yv[:, k, :], t["qtmp"][:, 0:W],
                                ht[:, k * W:(k + 1) * W], ALU.add)
            p.bar()
            p.ln_stats(yv, 8, W)
            for k in range(8):
                V.tensor_tensor(t["qtmp"][:, 0:W], yv[:, k, :], t["nmuB"][:, 0:W], ALU.add)
                V.tensor_tensor(t["qtmp"][:, 0:W], t["qtmp"][:, 0:W],
                                t["invsB"][:, 0:W], ALU.mult)
                V.tensor_scalar(t["qtmp"][:, 0:W], t["qtmp"][:, 0:W],
                                t["gammab"][:, k:k + 1], t["betab"][:, k:k + 1],
                                ALU.mult, ALU.add)
                V.tensor_tensor(stg[:, k * W:(k + 1) * W], xv[:, k, :],
                                t["qtmp"][:, 0:W], ALU.add)
            p.bar()
            nc.sync.dma_start(out=p.chs(t["x1d"]),
                              in_=stg.rearrange("p (k t) -> p k t", t=W)
                              ).then_inc(st, 16)
        nc.sync.wait_ge(st, 16 * ntile)


def emit_ffn1(p, ind_key, tag):
    nc, t, V, S = p.nc, p.t, p.V, p.S
    ntile = p.ntok // TT
    ld, lw, st = t["ld_" + tag], t["lw_" + tag], t["st_" + tag]
    W = WF
    WC = TT
    psA, psB = t["psA"], t["psB"]
    BCB = 768
    with p.stage(tag):
        w1q = p.sb("w1qs", [128, 8 * DF], BF16)
        xt = p.sb("f1x", [128, 8 * W])
        s2t = xt
        k1 = p.sb("f1k", [128, 8 * W], BF16)
        hmid = p.sb("f1h", [128, 32 * W], BF16)
        hmv = hmid.rearrange("p (m t) -> p m t", t=W)
        oscrow = p.sb("f1osc", [1, WC])
        nc.gpsimd.dma_start(out=w1q[:], in_=t["w1qd"][:]).then_inc(lw, 16)
        nc.tensor.wait_ge(lw, 16)
        p.bar()
        nc.gpsimd.reg_mov(p.rG, 0)
        nc.vector.reg_mov(p.rV, 16)
        with p.tloop(ntile) as i:
            nc.gpsimd.wait_ge(st, p.rG)
            nc.gpsimd.reg_add(p.rG, p.rG, 32)
            nc.gpsimd.dma_start(out=xt.rearrange("p (k t) -> p k t", t=W),
                                in_=p.chg(t[ind_key], halo=HE_FFN)).then_inc(ld, 16)
            nc.vector.wait_ge(ld, p.rV)
            nc.vector.reg_add(p.rV, p.rV, 16)
            p.bar()
            xv = xt.rearrange("p (k t) -> p k t", t=W)
            sv = xv
            p.ln_stats(xv, 8, W)
            for k in range(8):
                V.tensor_tensor(t["qtmp"][:, 0:W], xv[:, k, :], t["nmuB"][:, 0:W], ALU.add)
                V.tensor_tensor(sv[:, k, :], t["qtmp"][:, 0:W],
                                t["invsB"][:, 0:W], ALU.mult)
            p.bar()
            p.act_quant(sv, 8, W, t["w1scc"][:, 0:1])
            p.quantize(sv, k1, 8, W)
            for m in range(32):
                pp = m % 2
                for k in range(8):
                    p.mmN(psA, pp * 1024,
                          w1q[:, (m * 8 + k) * 128:(m * 8 + k + 1) * 128],
                          k1[:, k * W:(k + 1) * W], W,
                          start=(k == 0), stop=(k == 7))
                if m % 2 == 1:
                    p.bar()
                    for mm in (m - 1, m):
                        V.tensor_tensor(hmid[:, mm * W:(mm + 1) * W],
                                        psA[:, (mm % 2) * 1024:(mm % 2) * 1024 + W],
                                        t["oscb"][:, 0:W], ALU.mult)
                    p.bar()
            CV = W - 2
            sqb = t["sqb"]
            for m0 in range(0, 32, 4):
                for m in range(m0, m0 + 4):
                    sl = sqb[:, (m % 4) * 768:(m % 4) * 768 + CV]
                    V.tensor_scalar(sl, hmid[:, m * W:m * W + CV],
                                    t["convw"][:, 3 * m:3 * m + 1],
                                    t["convb"][:, m:m + 1], ALU.mult, ALU.add)
                    V.scalar_tensor_tensor(sl, hmid[:, m * W + 1:m * W + 1 + CV],
                                           t["convw"][:, 3 * m + 1:3 * m + 2],
                                           sl, ALU.mult, ALU.add)
                    V.scalar_tensor_tensor(sl, hmid[:, m * W + 2:m * W + 2 + CV],
                                           t["convw"][:, 3 * m + 2:3 * m + 3],
                                           sl, ALU.mult, ALU.add)
                p.bar()
                for m in range(m0, m0 + 4):
                    S.activation(out=hmv[:, m, 0:WC],
                                 in_=sqb[:, (m % 4) * 768 + HE_FFN - 1:
                                         (m % 4) * 768 + HE_FFN - 1 + WC],
                                 func=AF.Silu)
                p.bar()
            p.act_quant(hmv[:, :, 0:WC], 32, WC, t["w2scc"][:, 0:1])
            p.quantize2(hmv[:, :, 0:WC], hmv[:, :, 0:WC], 32, WC)
            V.tensor_copy(oscrow[:, 0:WC], t["qrow"][:, 768:768 + WC])
            p.bar()
            nc.sync.dma_start(
                out=p.flats(t["k2d"], p.ntok, 32),
                in_=hmv[:, :, 0:WC]).then_inc(st, 16)
            nc.sync.dma_start(
                out=t["oscd"][:, ds(p.oS, WC)], in_=oscrow[:, 0:WC]
                ).then_inc(st, 16)
        nc.sync.wait_ge(st, 32 * ntile)


def emit_ffn2(p, ind_key, outd_key, tag):
    nc, t, V, S = p.nc, p.t, p.V, p.S
    ntile = p.ntok // TT
    ld, lw, st = t["ld_" + tag], t["lw_" + tag], t["st_" + tag]
    WC = TT
    psA, psB = t["psA"], t["psB"]
    with p.stage(tag):
        w2q = p.sb("w2qs", [128, 32 * D], BF16)
        k2 = p.sb("f2k", [128, 32 * WC], BF16)
        xc = p.sb("f2x", [128, 8 * WC])
        stg = p.sb("f2stg", [128, 8 * WC])
        oscrow = p.sb("f2osc", [1, WC])
        nc.gpsimd.dma_start(out=w2q[:], in_=t["w2qd"][:]).then_inc(lw, 16)
        nc.tensor.wait_ge(lw, 16)
        p.bar()
        nc.gpsimd.reg_mov(p.rG, 0)
        nc.vector.reg_mov(p.rV, 48)
        with p.tloop(ntile) as i:
            nc.gpsimd.wait_ge(st, p.rG)
            nc.gpsimd.reg_add(p.rG, p.rG, 16)
            nc.gpsimd.dma_start(out=k2.rearrange("p (k t) -> p k t", t=WC),
                                in_=p.flatg(t["k2d"], p.ntok, 32)).then_inc(ld, 16)
            nc.gpsimd.dma_start(out=xc.rearrange("p (k t) -> p k t", t=WC),
                                in_=p.chg(t[ind_key])).then_inc(ld, 16)
            nc.gpsimd.dma_start(out=oscrow[:, 0:WC],
                                in_=t["oscd"][:, ds(p.oG, WC)]).then_inc(ld, 16)
            nc.vector.wait_ge(ld, p.rV)
            nc.vector.reg_add(p.rV, p.rV, 48)
            p.bar()
            p.mmN(psB, 0, t["ones128f"][:], oscrow[:, 0:WC], WC, start=True, stop=True)
            p.bar()
            V.tensor_copy(t["oscb"][:, 0:WC], psB[:, 0:WC])
            p.bar()
            for m in range(8):
                pp = m % 2
                for k in range(32):
                    p.mmN(psA, pp * 1024,
                          w2q[:, (m * 32 + k) * 128:(m * 32 + k + 1) * 128],
                          k2[:, k * WC:(k + 1) * WC], WC,
                          start=(k == 0), stop=(k == 31))
                p.bar()
                V.tensor_tensor(t["qtmp"][:, 0:WC], psA[:, pp * 1024:pp * 1024 + WC],
                                t["oscb"][:, 0:WC], ALU.mult)
                V.tensor_tensor(stg[:, m * WC:(m + 1) * WC], t["qtmp"][:, 0:WC],
                                xc[:, m * WC:(m + 1) * WC], ALU.add)
                p.bar()
            if outd_key == "outd":
                nc.sync.dma_start(out=p.flats(t[outd_key], p.ntok, 8),
                                  in_=stg.rearrange("p (k t) -> p k t", t=WC)
                                  ).then_inc(st, 16)
            else:
                nc.sync.dma_start(out=p.chs(t[outd_key]),
                                  in_=stg.rearrange("p (k t) -> p k t", t=WC)
                                  ).then_inc(st, 16)
        nc.sync.wait_ge(st, 16 * ntile)


def emit_s3a(p):
    nc, t, V, S = p.nc, p.t, p.V, p.S
    ntile = p.ntok // TT
    ld, lw, st = t["ld_s3a"], t["lw_s3a"], t["st_s3a"]
    W = WA
    WC = TT
    psA, psB, psD = t["psA"], t["psB"], t["psD"]
    eDP, ePD, ePA, eAP = t["eDP"], t["ePD"], t["ePA"], t["eAP"]
    eAD, eDA, eDS = t["eAD"], t["eDA"], t["eDS"]
    eDPq, ePDq = t["eDPq"], t["ePDq"]
    with p.stage("s3a"):
        qkvw = p.sb("qkvws", [128, 8 * 3 * D], BF16)
        xt = p.sb("a_x", [128, 8 * W])
        xnt = p.sb("a_xn", [128, 8 * W], BF16)
        qkvT = p.sb("a_qkv", [128, 24 * W], BF16)
        kst = p.sb("a_kst", [128, W], BF16)
        vst = p.sb("a_vst", [128, W], BF16)
        qst = p.sb("a_qst", [128, 512], BF16)
        vT = p.sb("a_vT", [128, 768], BF16)
        expS = p.sb("a_exp", [128, 4 * 384], BF16)
        rsA = p.sb("a_rsA", [1, 512])
        rsB = p.sb("a_rsB", [1, 512])
        rn = p.sb("a_rn", [64, 512])
        attnT = p.sb("a_at", [64, 2 * WC], BF16)
        nc.gpsimd.dma_start(out=qkvw[:], in_=t["qkvwd"][:]).then_inc(lw, 16)
        nc.tensor.wait_ge(lw, 16)
        p.bar()
        nc.gpsimd.reg_mov(p.rG, 0)
        nc.vector.reg_mov(p.rV, 16)
        nc.vector.sem_inc(eDA, 1)
        nc.tensor.sem_inc(ePA, 1)
        with p.tloop(ntile) as i:
            nc.gpsimd.wait_ge(st, p.rG)
            nc.gpsimd.reg_add(p.rG, p.rG, 128)
            nc.sync.reg_mov(p.rY2, 0)
            nc.gpsimd.dma_start(out=xt.rearrange("p (k t) -> p k t", t=W),
                                in_=p.chg(t["x2d"], halo=HE_ATT)).then_inc(ld, 16)
            nc.vector.wait_ge(ld, p.rV)
            nc.vector.reg_add(p.rV, p.rV, 16)
            p.bar()
            xv = xt.rearrange("p (k t) -> p k t", t=W)
            p.ln_stats(xv, 8, W)
            for k in range(8):
                V.tensor_tensor(t["qtmp"][:, 0:W], xv[:, k, :], t["nmuB"][:, 0:W], ALU.add)
                V.tensor_tensor(xv[:, k, :], t["qtmp"][:, 0:W],
                                t["invsB"][:, 0:W], ALU.mult)
            p.bar()
            p.ln_stats(xv, 8, W, zero_mean=True)
            for k in range(8):
                V.tensor_tensor(t["qtmp"][:, 0:W], xv[:, k, :],
                                t["invsB"][:, 0:W], ALU.mult)
                V.tensor_scalar(xnt[:, k * W:(k + 1) * W], t["qtmp"][:, 0:W],
                                t["ln1wb"][:, k:k + 1], t["ln1bb"][:, k:k + 1],
                                ALU.mult, ALU.add)
            p.bar()
            for m in range(24):
                pp = m % 2
                if pp == 0 and m > 0:
                    p.ewait(nc.tensor, p.tDPq, eDPq)
                for k in range(8):
                    p.mmN(psA, pp * 1024,
                          qkvw[:, (m * 8 + k) * 128:(m * 8 + k + 1) * 128],
                          xnt[:, k * W:(k + 1) * W], W,
                          start=(k == 0), stop=(k == 7),
                          inc=(ePDq if (k == 7 and pp == 1) else None))
                if pp == 1:
                    p.ewait(nc.vector, p.vPDq, ePDq)
                    for mm2 in (m - 1, m):
                        sc = 0.125 if mm2 < 8 else 1.0
                        inst = V.tensor_scalar(
                            qkvT[:, mm2 * W:(mm2 + 1) * W],
                            psA[:, (mm2 % 2) * 1024:(mm2 % 2) * 1024 + W],
                            t["qkvbb"][:, mm2:mm2 + 1], sc, ALU.add, ALU.mult)
                    if m < 23:
                        inst.then_inc(eDPq, 1)
            nc.vector.reg_mov(p.rVo, 0)
            expv = expS.rearrange("p (q c) -> p q c", c=384)
            psAs = psA.rearrange("p (q c) -> p q c", c=512)
            with nc.Fori(0, NH // 2) as hp:
                oV = nc.vector.snap(p.rVo, min_val=0, max_val=7 * W)
                oY2 = nc.sync.snap(p.rY2, min_val=0, max_val=14)
                p._stage_regs += [oV.val, oY2.val]
                p.ewait(nc.vector, p.vPD, ePD, k=0)
                V.tensor_copy(kst[:], qkvT[:, ds(oV + 8 * W, W)])
                V.tensor_copy(vst[:], qkvT[:, ds(oV + 16 * W, W)])
                inst = V.tensor_copy(qst[:], qkvT[:, ds(oV + 128, 512)])
                inst.then_inc(eDP, 1)
                for b6 in (0, 1):
                    base = b6 * 64
                    p.ewait(nc.tensor, p.tDP, eDP, k=1)
                    for b in range(6):
                        nc.tensor.transpose(
                            psD[:, b * 64:(b + 1) * 64],
                            vst[base:base + 64, b * 128:(b + 1) * 128],
                            t["identb"][base:base + 64, base:base + 64])
                    nc.tensor.drain()
                    nc.tensor.sem_inc(ePD, 1)
                    p.ewait(nc.vector, p.vPD, ePD, k=1)
                    inst = V.tensor_copy(vT[:, b6 * 384:(b6 + 1) * 384],
                                         psD[:, 0:384])
                    if b6 == 0:
                        inst.then_inc(eDP, 1)
                for b6 in (0, 1):
                    base = b6 * 64
                    rs = rsA if b6 == 0 else rsB
                    p.ewait(nc.tensor, p.tAP, eAP, k=0)
                    for qb in range(4):
                        for b in range(3):
                            inst = nc.tensor.matmul(
                                psA[:, qb * 512 + b * 128:qb * 512 + (b + 1) * 128],
                                kst[base:base + 64, (qb + b) * 128:(qb + b + 1) * 128],
                                qst[base:base + 64, qb * 128:(qb + 1) * 128],
                                start=True, stop=True)
                    inst.then_inc(ePA, 1)
                    p.ewait(nc.scalar, p.sPA, ePA, k=2)
                    p.ewait(nc.scalar, p.sDA, eDA)
                    inst = S.activation(out=expv[:, 0:4, 0:384], in_=psAs[:, 0:4, 0:384],
                                        func=AF.Exp)
                    inst.then_inc(eAD, 1)
                    nc.scalar.sem_inc(eAP, 1)
                    p.ewait(nc.vector, p.vAD, eAD)
                    with nc.vector.If(i == 0):
                        nc.vector.memset(expS[:, 0:128], 0.0)
                        nc.vector.drain()
                    with nc.vector.If(i == ntile - 1):
                        nc.vector.memset(expS[:, 3 * 384 + 256:4 * 384], 0.0)
                        nc.vector.drain()
                    nc.vector.sem_inc(eDP, 1)
                    nc.vector.sem_inc(eDA, 1)
                    p.ewait(nc.tensor, p.tDP, eDP)
                    p.ewait(nc.tensor, p.tAP, eAP, k=0)
                    for qb in range(4):
                        for b in range(3):
                            inst = nc.tensor.matmul(
                                psA[0:1, qb * 512 + 384:qb * 512 + 512],
                                t["onescol"][:],
                                expS[:, qb * 384 + b * 128:qb * 384 + (b + 1) * 128],
                                start=(b == 0), stop=(b == 2))
                    inst.then_inc(ePA, 1)
                    for qb in range(4):
                        for b in range(3):
                            inst = nc.tensor.matmul(
                                psB[0:64, qb * 128:(qb + 1) * 128],
                                vT[:, (b6 * 6 + qb + b) * 64:(b6 * 6 + qb + b + 1) * 64],
                                expS[:, qb * 384 + b * 128:qb * 384 + (b + 1) * 128],
                                start=(b == 0), stop=(b == 2))
                    inst.then_inc(ePA, 1)
                    p.ewait(nc.scalar, p.sPA, ePA)
                    S.activation(out=rs.rearrange("o (q c) -> o q c", c=128),
                                 in_=psAs[0:1, 0:4, 384:512], func=AF.Ln)
                    inst = S.activation(out=rs[:, 0:512], in_=rs[:, 0:512],
                                        func=AF.Exp, scale=-1.0)
                    inst.then_inc(eAP, 1)
                    nc.scalar.sem_inc(eAP, 1)
                    p.ewait(nc.tensor, p.tAP, eAP, k=3)
                    inst = nc.tensor.matmul(psB[0:64, 512:1024], t["ones64f"][:],
                                            rs[:, 0:512], start=True, stop=True)
                    inst.then_inc(ePD, 1)
                    p.ewait(nc.vector, p.vPD, ePD)
                    V.tensor_copy(rn[:, 0:512], psB[0:64, 512:1024])
                    if b6 == 0:
                        nc.vector.wait_ge(st, p.vSD)
                        nc.vector.reg_add(p.vSD, p.vSD, 16)
                    inst = V.tensor_tensor(attnT[0:64, b6 * WC:(b6 + 1) * WC],
                                           psB[0:64, 0:512], rn[:, 0:512], ALU.mult)
                    if b6 == 1:
                        nc.vector.sem_inc(eDS, 1)
                p.ewait(nc.sync, p.rY, eDS)
                nc.sync.dma_start(
                    out=t["attd"].rearrange("p (k t) -> p k t", t=p.ntok)[
                        :, ds(oY2, 2), ds(p.oS, TT)],
                    in_=attnT.rearrange("p (c t) -> p c t", t=WC)
                    ).then_inc(st, 16)
                nc.vector.reg_add(p.rVo, p.rVo, W)
                nc.sync.reg_add(p.rY2, p.rY2, 2)
        nc.sync.wait_ge(st, 128 * ntile)


def emit_s3a2(p):
    nc, t, V, S = p.nc, p.t, p.V, p.S
    ntile = p.ntok // TT
    ld, lw, st = t["ld_s3a2"], t["lw_s3a2"], t["st_s3a2"]
    WC = TT
    psA = t["psA"]
    with p.stage("s3a2"):
        projw = p.sb("projws", [64, 16 * D], BF16)
        attnT = p.sb("p_at", [64, 16 * WC], BF16)
        x2c = p.sb("p_x2", [128, 8 * WC])
        s3c = p.sb("p_s3", [128, 8 * WC])
        stg = p.sb("p_stg", [128, 8 * WC])
        nc.gpsimd.dma_start(out=projw[:], in_=t["projwd"][:]).then_inc(lw, 16)
        nc.tensor.wait_ge(lw, 16)
        p.bar()
        nc.gpsimd.reg_mov(p.rG, 0)
        nc.vector.reg_mov(p.rV, 32)
        with p.tloop(ntile) as i:
            nc.gpsimd.wait_ge(st, p.rG)
            nc.gpsimd.reg_add(p.rG, p.rG, 16)
            nc.gpsimd.dma_start(out=attnT.rearrange("p (c t) -> p c t", t=WC),
                                in_=p.flatg(t["attd"], p.ntok, 16)).then_inc(ld, 16)
            nc.gpsimd.dma_start(out=x2c.rearrange("p (k t) -> p k t", t=WC),
                                in_=p.chg(t["x2d"])).then_inc(ld, 16)
            nc.vector.wait_ge(ld, p.rV)
            nc.vector.reg_add(p.rV, p.rV, 32)
            p.bar()
            x2v = x2c.rearrange("p (k t) -> p k t", t=WC)
            p.ln_stats(x2v, 8, WC)
            for k in range(8):
                V.tensor_tensor(t["qtmp"][:, 0:WC], x2v[:, k, :],
                                t["nmuB"][:, 0:WC], ALU.add)
                V.tensor_tensor(s3c[:, k * WC:(k + 1) * WC], t["qtmp"][:, 0:WC],
                                t["invsB"][:, 0:WC], ALU.mult)
            p.bar()
            for m in range(8):
                pp = m % 2
                for kh in range(16):
                    p.mmN(psA, pp * 1024,
                          projw[:, (m * 16 + kh) * 128:(m * 16 + kh + 1) * 128],
                          attnT[:, kh * WC:(kh + 1) * WC], WC,
                          start=(kh == 0), stop=(kh == 15))
                p.bar()
                V.tensor_scalar(t["qtmp"][:, 0:WC], psA[:, pp * 1024:pp * 1024 + WC],
                                t["projbb"][:, m:m + 1], None, ALU.add)
                V.tensor_tensor(stg[:, m * WC:(m + 1) * WC], t["qtmp"][:, 0:WC],
                                s3c[:, m * WC:(m + 1) * WC], ALU.add)
                p.bar()
            nc.sync.dma_start(out=p.chs(t["xb1d"]),
                              in_=stg.rearrange("p (k t) -> p k t", t=WC)
                              ).then_inc(st, 16)
        nc.sync.wait_ge(st, 16 * ntile)


def emit_s3b1(p):
    nc, t, V, S = p.nc, p.t, p.V, p.S
    ntile = p.ntok // TT
    ld, lw, st = t["ld_s3b1"], t["lw_s3b1"], t["st_s3b1"]
    W = TT
    psA, psB = t["psA"], t["psB"]
    BCB = 768
    with p.stage("s3b1"):
        fc1w = p.sb("fc1ws", [128, 8 * DF], BF16)
        xt = p.sb("b1x", [128, 8 * W])
        y2t = p.sb("b1y", [128, 8 * W], BF16)
        glb = p.sb("b1g", [128, 32 * W], BF16)
        u8 = p.sb("b1u8", [128, 8 * W])
        xa = p.sb("b1xa", [128, W])
        xc = p.sb("b1xc", [128, W])
        nh = p.sb("b1nh", [128, W])
        dh = p.sb("b1dh", [128, W])
        eb = p.sb("b1e", [128, W])
        nc.gpsimd.dma_start(out=fc1w[:], in_=t["fc1wd"][:]).then_inc(lw, 16)
        nc.tensor.wait_ge(lw, 16)
        p.bar()
        nc.gpsimd.reg_mov(p.rG, 0)
        nc.vector.reg_mov(p.rV, 16)
        with p.tloop(ntile) as i:
            nc.gpsimd.wait_ge(st, p.rG)
            nc.gpsimd.reg_add(p.rG, p.rG, 16)
            nc.gpsimd.dma_start(out=xt.rearrange("p (k t) -> p k t", t=W),
                                in_=p.chg(t["xb1d"])).then_inc(ld, 16)
            nc.vector.wait_ge(ld, p.rV)
            nc.vector.reg_add(p.rV, p.rV, 16)
            p.bar()
            xv = xt.rearrange("p (k t) -> p k t", t=W)
            p.ln_stats(xv, 8, W)
            for k in range(8):
                V.tensor_tensor(t["qtmp"][:, 0:W], xv[:, k, :], t["nmuB"][:, 0:W], ALU.add)
                V.tensor_tensor(t["qtmp"][:, 0:W], t["qtmp"][:, 0:W],
                                t["invsB"][:, 0:W], ALU.mult)
                V.tensor_scalar(y2t[:, k * W:(k + 1) * W], t["qtmp"][:, 0:W],
                                t["ln2wb"][:, k:k + 1], t["ln2bb"][:, k:k + 1],
                                ALU.mult, ALU.add)
            p.bar()
            glv = glb.rearrange("p (m t) -> p m t", t=W)
            u8v = u8.rearrange("p (m t) -> p m t", t=W)
            nc.vector.reg_mov(p.rVo, 0)
            nc.scalar.reg_mov(p.rS, 0)
            for g in range(4):
                for m8 in range(8):
                    m = g * 8 + m8
                    for k in range(8):
                        p.mmN(psA, (m8 % 4) * 512,
                              fc1w[:, (m * 8 + k) * 128:(m * 8 + k + 1) * 128],
                              y2t[:, k * W:(k + 1) * W], W,
                              start=(k == 0), stop=(k == 7))
                    if m8 % 4 == 3:
                        p.bar()
                        for q4 in range(4):
                            V.tensor_scalar(u8v[:, m8 - 3 + q4, 0:W],
                                            psA[:, q4 * 512:q4 * 512 + W],
                                            t["fc1bb"][:, m - 3 + q4:m - 2 + q4],
                                            None, ALU.add)
                        p.bar()
                gs = p.regscope(); gs.__enter__()
                with nc.Fori(0, 8) as mi:
                    oD = nc.vector.snap(p.rVo, min_val=g * 8, max_val=g * 8 + 7)
                    oA = nc.scalar.snap(p.rS, min_val=0, max_val=7)
                    p._stage_regs += [oD.val, oA.val]
                    uS = u8v[:, ds(oA, 1), 0:W]
                    uV = u8v[:, ds(oD - g * 8, 1), 0:W]
                    S.activation(out=xa[:], in_=uS, func=AF.Abs)
                    S.activation(out=nh[:], in_=uS, func=AF.Square)
                    S.activation(out=eb[:], in_=nh[:], func=AF.Exp, scale=-0.5)
                    p.bar()
                    V.tensor_scalar(xc[:], xa[:], 16.0, None, ALU.min)
                    V.tensor_scalar(nh[:], xc[:], GN[4][0], GN[3][0], ALU.mult, ALU.add)
                    V.tensor_scalar(nh[:], nh[:], GN[3][1], None, ALU.add)
                    for jj in (2, 1):
                        V.tensor_tensor(nh[:], nh[:], xc[:], ALU.mult)
                        V.tensor_scalar(nh[:], nh[:], GN[jj][0], GN[jj][1], ALU.add, ALU.add)
                    V.tensor_tensor(nh[:], nh[:], xc[:], ALU.mult)
                    V.tensor_scalar(nh[:], nh[:], 1.0, None, ALU.add)
                    V.tensor_scalar(dh[:], xc[:], GD[5][0], GD[4][0], ALU.mult, ALU.add)
                    for jj in (3, 2, 1):
                        V.tensor_tensor(dh[:], dh[:], xc[:], ALU.mult)
                        V.tensor_scalar(dh[:], dh[:], GD[jj][0], GD[jj][1], ALU.add, ALU.add)
                    V.tensor_tensor(dh[:], dh[:], xc[:], ALU.mult)
                    V.tensor_scalar(dh[:], dh[:], 1.0, None, ALU.add)
                    V.reciprocal(xc[:], dh[:])
                    V.tensor_tensor(nh[:], nh[:], xc[:], ALU.mult)
                    V.tensor_tensor(eb[:], eb[:], nh[:], ALU.mult)
                    V.tensor_scalar(eb[:], eb[:], -0.5, 0.5, ALU.mult, ALU.add)
                    V.tensor_tensor(eb[:], eb[:], xa[:], ALU.mult)
                    V.tensor_scalar(nh[:], uV, 0.5, None, ALU.mult)
                    V.tensor_tensor(glv[:, ds(oD, 1), 0:W], nh[:], eb[:], ALU.add)
                    p.bar()
                    nc.vector.reg_add(p.rVo, p.rVo, 1)
                    nc.scalar.reg_add(p.rS, p.rS, 1)
                gs.__exit__(None, None, None)
                nc.scalar.reg_mov(p.rS, 0)
            nc.sync.dma_start(
                out=p.flats(t["gld"], p.ntok, 32),
                in_=glb.rearrange("p (k t) -> p k t", t=W)).then_inc(st, 16)
        nc.sync.wait_ge(st, 16 * ntile)


def emit_s3b2(p):
    nc, t, V, S = p.nc, p.t, p.V, p.S
    ntile = p.ntok // TT
    ld, lw, st = t["ld_s3b2"], t["lw_s3b2"], t["st_s3b2"]
    W = TT
    psA = t["psA"]
    with p.stage("s3b2"):
        fc2w = p.sb("fc2ws", [128, 32 * D], BF16)
        glb = p.sb("b2g", [128, 32 * W], BF16)
        xb1 = p.sb("b2xb1", [128, 8 * W])
        x2 = p.sb("b2x2", [128, 8 * W])
        stg = p.sb("b2stg", [128, 8 * W])
        nc.gpsimd.dma_start(out=fc2w[:], in_=t["fc2wd"][:]).then_inc(lw, 16)
        nc.tensor.wait_ge(lw, 16)
        p.bar()
        nc.gpsimd.reg_mov(p.rG, 0)
        nc.vector.reg_mov(p.rV, 48)
        with p.tloop(ntile) as i:
            nc.gpsimd.wait_ge(st, p.rG)
            nc.gpsimd.reg_add(p.rG, p.rG, 16)
            nc.gpsimd.dma_start(out=glb.rearrange("p (k t) -> p k t", t=W),
                                in_=p.flatg(t["gld"], p.ntok, 32)).then_inc(ld, 16)
            nc.gpsimd.dma_start(out=xb1.rearrange("p (k t) -> p k t", t=W),
                                in_=p.chg(t["xb1d"])).then_inc(ld, 16)
            nc.gpsimd.dma_start(out=x2.rearrange("p (k t) -> p k t", t=W),
                                in_=p.chg(t["x2d"])).then_inc(ld, 16)
            nc.vector.wait_ge(ld, p.rV)
            nc.vector.reg_add(p.rV, p.rV, 48)
            p.bar()
            for m in range(8):
                pp = m % 2
                for k in range(32):
                    p.mmN(psA, pp * 1024,
                          fc2w[:, (m * 32 + k) * 128:(m * 32 + k + 1) * 128],
                          glb[:, k * W:(k + 1) * W], W,
                          start=(k == 0), stop=(k == 31))
                p.bar()
                V.tensor_scalar(t["qtmp"][:, 0:W], psA[:, pp * 1024:pp * 1024 + W],
                                t["fc2bb"][:, m:m + 1], None, ALU.add)
                V.tensor_tensor(t["qtmp"][:, 0:W], t["qtmp"][:, 0:W],
                                xb1[:, m * W:(m + 1) * W], ALU.add)
                V.tensor_tensor(stg[:, m * W:(m + 1) * W], t["qtmp"][:, 0:W],
                                x2[:, m * W:(m + 1) * W], ALU.add)
                p.bar()
            nc.sync.dma_start(out=p.chs(t["x3d"]),
                              in_=stg.rearrange("p (k t) -> p k t", t=W)
                              ).then_inc(st, 16)
        nc.sync.wait_ge(st, 16 * ntile)


def emit_phase2(nc, t, ntok, stages=8):
    p = P2(nc, t, ntok)
    t["p2obj"] = p
    p.edges_init()
    emit_wprep(p, t["w1T"], t["w1qd"], "w1sc", 8, 32, "1")
    emit_wprep(p, t["w2T"], t["w2qd"], "w2sc", 32, 8, "2")
    if stages >= 1:
        emit_s1(p)
    if stages >= 2:
        emit_ffn1(p, "x1d", "s2a")
    if stages >= 3:
        emit_ffn2(p, "x1d", "x2d", "s2b")
    if stages >= 4:
        emit_s3a(p)
    if stages >= 5:
        emit_s3a2(p)
    if stages >= 6:
        emit_s3b1(p)
    if stages >= 7:
        emit_s3b2(p)
    if stages >= 8:
        emit_ffn1(p, "x3d", "s4a")
    if stages >= 9:
        emit_ffn2(p, "x3d", "outd", "s4b")



def emit_phase_a(nc, t, nstep):
    V, S = _wrap(nc.vector), _wrap(nc.scalar)
    ncha = nstep // CHA
    xt, wx, gd, biasb = t["xt"], t["wx"], t["gd"], t["biasb"]
    onescol, ones128 = t["onescol"], t["ones128"]
    wxb, xbuf, sqb, s1b, t8 = t["wxb"], t["xbuf"], t["sqb"], t["s1b"], t["t8"]
    trow, nmub, invsb, gstage = t["trow"], t["nmub"], t["invsb"], t["gstage"]
    psS, psB, psP = t["psS"], t["psB"], t["psP"]
    (a_wl, a_xl, a_sq, a_mm2, a_mmS, a_ew1, a_iv, a_rows, a_mmB, a_s1, a_pg,
     a_gact, a_st, a_pgc) = (
        t["a_wl"], t["a_xl"], t["a_sq"], t["a_mm2"], t["a_mmS"], t["a_ew1"],
        t["a_iv"], t["a_rows"], t["a_mmB"], t["a_s1"], t["a_pg"], t["a_gact"],
        t["a_st"], t["a_pgc"])

    rQ = nc.scalar.alloc_register("rQ")
    rTsq = nc.tensor.alloc_register("rTsq")
    rTg = nc.tensor.alloc_register("rTg")
    rVp = nc.vector.alloc_register("rVp")
    nc.scalar.reg_mov(rQ, 1)
    nc.tensor.reg_mov(rTsq, 1)
    nc.tensor.reg_mov(rTg, 1)
    nc.vector.reg_mov(rVp, 1)
    nc.vector.sem_inc(a_gact, 2)

    for k in range(8):
        nc.gpsimd.dma_start(
            out=wxb.rearrange("p (m k c) -> p m k c", k=8, c=128)[:, :, k, :],
            in_=wx[k * 128:(k + 1) * 128, :].rearrange("p (m c) -> p m c", c=128),
        ).then_inc(a_wl, 16)

    with nc.Fori(0, ncha) as c:
        nc.gpsimd.wait_ge(a_s1, 8 * c)
        nc.gpsimd.wait_ge(a_sq, 8 * c)
        nc.gpsimd.wait_ge(a_pgc, c)
        for k in range(8):
            nc.gpsimd.dma_start(
                out=xbuf[:, k * TCA:(k + 1) * TCA],
                in_=xt[k * 128:(k + 1) * 128, :][:, ds(c * TCA, TCA)],
            ).then_inc(a_xl, 16)
        nc.tensor.wait_ge(a_wl, 128)
        nc.tensor.wait_ge(a_xl, (c + 1) * 128)
        nc.tensor.wait_ge(a_rows, c)
        for k in range(8):
            nc.tensor.matmul(psS[:, 0:TCA], onescol[:], xbuf[:, k * TCA:(k + 1) * TCA],
                             start=(k == 0), stop=(k == 7))
        nc.scalar.wait_ge(a_xl, (c + 1) * 128)
        for k in range(8):
            pq = k % 2
            if k < 2:
                nc.scalar.wait_ge(a_mmS, c)
            else:
                nc.scalar.wait_ge(a_mm2, rQ)
                nc.scalar.reg_add(rQ, rQ, 1)
            S.activation(out=sqb[:, pq * TCA:(pq + 1) * TCA],
                         in_=xbuf[:, k * TCA:(k + 1) * TCA],
                         func=AF.Square).then_inc(a_sq, 1)
            nc.tensor.wait_ge(a_sq, rTsq)
            nc.tensor.reg_add(rTsq, rTsq, 1)
            inst = nc.tensor.matmul(psS[:, TCA:2 * TCA], onescol[:],
                                    sqb[:, pq * TCA:(pq + 1) * TCA],
                                    start=(k == 0), stop=(k == 7))
            inst.then_inc(a_mmS if k == 7 else a_mm2, 1)
        nc.scalar.reg_add(rQ, rQ, 1)
        MU, VAR, NMU = 0, TCA, 2 * TCA
        nc.vector.wait_ge(a_mmS, c + 1)
        V.tensor_scalar_mul(trow[:, MU:MU + TCA], psS[:, 0:TCA], 1.0 / D)
        V.tensor_scalar_mul(trow[:, VAR:VAR + TCA], psS[:, TCA:2 * TCA], 1.0 / D)
        V.tensor_scalar_mul(trow[:, NMU:NMU + TCA], trow[:, MU:MU + TCA], -1.0)
        V.tensor_copy(nmub[:], trow[:, NMU:NMU + TCA])
        V.tensor_tensor(trow[:, MU:MU + TCA], trow[:, MU:MU + TCA],
                        trow[:, MU:MU + TCA], ALU.mult)
        V.tensor_tensor(trow[:, VAR:VAR + TCA], trow[:, VAR:VAR + TCA],
                        trow[:, MU:MU + TCA], ALU.subtract).then_inc(a_ew1, 1)
        nc.scalar.wait_ge(a_ew1, c + 1)
        S.activation(out=trow[:, VAR:VAR + TCA], in_=trow[:, VAR:VAR + TCA],
                     func=AF.Ln, bias=t["epsb"][:, 0:1])
        S.activation(out=trow[:, VAR:VAR + TCA], in_=trow[:, VAR:VAR + TCA],
                     func=AF.Exp, scale=-0.5).then_inc(a_iv, 1)
        nc.vector.wait_ge(a_iv, c + 1)
        V.tensor_copy(invsb[:], trow[:, VAR:VAR + TCA]).then_inc(a_rows, 1)
        nc.tensor.wait_ge(a_rows, c + 1)
        nc.tensor.wait_ge(a_s1, 8 * c)
        nc.tensor.matmul(psB[:, 0:TCA], ones128[:], nmub[:], start=True, stop=True)
        nc.tensor.matmul(psB[:, TCA:2 * TCA], ones128[:], invsb[:],
                         start=True, stop=True).then_inc(a_mmB, 1)
        nc.vector.wait_ge(a_mmB, c + 1)
        for k in range(8):
            V.tensor_tensor(t8[:], xbuf[:, k * TCA:(k + 1) * TCA],
                            psB[:, 0:TCA], ALU.add)
            V.tensor_tensor(s1b[:, k * TCA:(k + 1) * TCA], t8[:],
                            psB[:, TCA:2 * TCA], ALU.mult).then_inc(a_s1, 1)
        nc.tensor.wait_ge(a_s1, 8 * (c + 1))
        for m in range(NMG):
            pp = m % 2
            nc.tensor.wait_ge(a_gact, rTg)
            nc.tensor.reg_add(rTg, rTg, 1)
            for k in range(8):
                inst = nc.tensor.matmul(psP[:, pp * TCA:(pp + 1) * TCA],
                                        wxb[:, (m * 8 + k) * 128:(m * 8 + k + 1) * 128],
                                        s1b[:, k * TCA:(k + 1) * TCA],
                                        start=(k == 0), stop=(k == 7))
            inst.then_inc(a_pgc if m == NMG - 1 else a_pg, 1)
            if m == NMG - 1:
                nc.vector.wait_ge(a_pgc, c + 1)
            else:
                nc.vector.wait_ge(a_pg, rVp)
                nc.vector.reg_add(rVp, rVp, 1)
            if m == 0:
                nc.vector.wait_ge(a_st, 16 * c)
            V.tensor_scalar_add(gstage[:, m * TCA:(m + 1) * TCA],
                                psP[:, pp * TCA:(pp + 1) * TCA],
                                biasb[:, m:m + 1]).then_inc(a_gact, 1)
        nc.sync.wait_ge(a_gact, 2 + (c + 1) * NMG)
        nc.sync.dma_start(
            out=gd.rearrange("p (m t) -> p m t", m=NMG)[:, :, ds(c * TCA, TCA)],
            in_=gstage.rearrange("p (m t) -> p m t", m=NMG),
        ).then_inc(a_st, 16)
    nc.sync.wait_ge(a_st, ncha * 16)



def emit_phase_b(nc, t, nstep, record_hs=True):
    V, S = _wrap(nc.vector), _wrap(nc.scalar)
    nbody = nstep // (2 * INNER)
    wh, weg, gd, begb, hsout = t["wh"], t["weg"], t["gd"], t["begb"], t["hsout"]
    whb, gbuf = t["whb"], t["gbuf"]
    hT, gT, stc, stn = t["hT"], t["gT"], t["stc"], t["stn"]
    zg, zifo, sifo, gf, ef, tmp, tmp2, rn, hf, hsb = (
        t["zg"], t["zifo"], t["sifo"], t["gf"], t["ef"], t["tmp"], t["tmp2"],
        t["rn"], t["hf"], t["hsb"])
    psG, psI, psW = t["psG"], t["psI"], t["psW"]
    (b_wl, b_gl, b_mg, b_mi, b_mw, b_eg, b_ag, b_gt, b_ei, b_en, b_ai, b_ae,
     b_ar, b_ec, b_hs) = (t["b_wl"], t["b_gl"], t["b_mg"], t["b_mi"], t["b_mw"],
                          t["b_eg"], t["b_ag"], t["b_gt"], t["b_ei"], t["b_en"],
                          t["b_ai"], t["b_ae"], t["b_ar"], t["b_ec"], t["b_hs"])

    rT = nc.tensor.alloc_register("rT")
    rV = nc.vector.alloc_register("rV")
    rS = nc.scalar.alloc_register("rS")
    nc.tensor.reg_mov(rT, 1)
    nc.vector.reg_mov(rV, 1)
    nc.scalar.reg_mov(rS, 1)

    for k in range(8):
        nc.gpsimd.dma_start(
            out=whb.rearrange("p (m k c) -> p m k c", k=8, c=128)[:, 0:NMG, k, :],
            in_=wh[k * 128:(k + 1) * 128, :].rearrange("p (m c) -> p m c", c=128),
        ).then_inc(b_wl, 16)
        nc.gpsimd.dma_start(
            out=whb.rearrange("p (m k c) -> p m k c", k=8, c=128)[:, NMG:NT, k, :],
            in_=weg[k * 128:(k + 1) * 128, :].rearrange("p (m c) -> p m c", c=128),
        ).then_inc(b_wl, 16)
    nc.vector.memset(hT[:], 0.0)
    nc.vector.drain()
    nc.vector.memset(stc[:], 0.0)
    nc.vector.drain()
    nc.vector.memset(stn[:], 0.0)
    nc.vector.drain()
    nc.vector.sem_inc(b_ec, 1)
    nc.tensor.wait_ge(b_wl, 256)

    with nc.Fori(0, nbody) as cb:
        nc.gpsimd.wait_ge(b_ei, 16 * cb)
        for H in range(2):
            nc.gpsimd.dma_start(
                out=gbuf[:, H * INNER * GSC:(H + 1) * INNER * GSC].rearrange(
                    "p (m t) -> p m t", m=NMG),
                in_=gd.rearrange("p (m t) -> p m t", m=NMG)[
                    :, :, ds(cb * 2 * INNER * B + H * INNER * B, INNER * B)],
            ).then_inc(b_gl, 16)
        for j in range(2 * INNER):
            H = j // INNER
            pa = j % 2
            gt_ap = gbuf.rearrange("p (h m t) -> p h m t", h=2, m=NMG)[
                :, H, :, (j % INNER) * 8:(j % INNER + 1) * 8]
            nc.tensor.wait_ge(b_ec, rT)
            for m in range(8):
                for k in range(8):
                    inst = nc.tensor.matmul(
                        psG[:, pa * 64 + m * 8:pa * 64 + (m + 1) * 8],
                        whb[:, (m * 8 + k) * 128:(m * 8 + k + 1) * 128],
                        hT[:, k * 8:(k + 1) * 8],
                        start=(k == 0), stop=(k == 7))
            inst.then_inc(b_mg, 1)
            for m in range(8, 32):
                for k in range(8):
                    inst = nc.tensor.matmul(
                        psI[:, pa * 192 + (m - 8) * 8:pa * 192 + (m - 7) * 8],
                        whb[:, (m * 8 + k) * 128:(m * 8 + k + 1) * 128],
                        hT[:, k * 8:(k + 1) * 8],
                        start=(k == 0), stop=(k == 7))
            inst.then_inc(b_mi, 1)
            nc.vector.wait_ge(b_mg, rV)
            if j == 0:
                nc.vector.wait_ge(b_gl, (cb + 1) * 32)
            V.tensor_tensor(
                zg.rearrange("p (m t) -> p m t", m=8),
                psG[:, pa * 64:pa * 64 + 64].rearrange("p (m t) -> p m t", m=8),
                gt_ap[:, 0:8, :], ALU.add).then_inc(b_eg, 1)
            nc.scalar.wait_ge(b_eg, rS)
            S.activation(out=gf[:], in_=zg[:], func=AF.Tanh).then_inc(b_ag, 1)
            nc.vector.wait_ge(b_ag, rV)
            V.tensor_copy(gT[:], gf[:]).then_inc(b_gt, 1)
            nc.tensor.wait_ge(b_gt, rT)
            nc.tensor.reg_add(rT, rT, 1)
            for m in range(NMG, NT):
                mo = m - NMG
                for k in range(8):
                    inst = nc.tensor.matmul(
                        psW[:, pa * 64 + mo * 8:pa * 64 + (mo + 1) * 8],
                        whb[:, (m * 8 + k) * 128:(m * 8 + k + 1) * 128],
                        gT[:, k * 8:(k + 1) * 8],
                        start=(k == 0), stop=(k == 7))
            inst.then_inc(b_mw, 1)
            nc.vector.wait_ge(b_mi, rV)
            V.tensor_tensor(
                zifo.rearrange("p (m t) -> p m t", m=24),
                psI[:, pa * 192:pa * 192 + 192].rearrange("p (m t) -> p m t", m=24),
                gt_ap[:, 8:32, :], ALU.add).then_inc(b_ei, 1)
            nc.scalar.wait_ge(b_ei, rS)
            S.activation(out=sifo[:], in_=zifo[:], func=AF.Sigmoid).then_inc(b_ai, 1)
            nc.scalar.wait_ge(b_mw, rS)
            for mo in range(NMW):
                inst = S.activation(
                    out=ef[:, mo * 8:(mo + 1) * 8],
                    in_=psW[:, pa * 64 + mo * 8:pa * 64 + (mo + 1) * 8],
                    func=AF.Exp, bias=begb[:, mo:mo + 1])
            inst.then_inc(b_ae, 1)
            SI, SF, SO = 0, 64, 128
            nc.vector.wait_ge(b_ai, rV)
            nc.vector.wait_ge(b_ae, rV)
            if record_hs and j == 0:
                nc.vector.wait_ge(b_hs, 32 * cb)
            V.tensor_tensor(stc[:], sifo[:, SF:SF + 64], stc[:], ALU.mult)
            V.tensor_tensor(tmp[:], sifo[:, SI:SI + 64], gf[:], ALU.mult)
            V.tensor_tensor(stc[:], stc[:], tmp[:], ALU.add)
            V.tensor_tensor(stn[:], sifo[:, SF:SF + 64], stn[:], ALU.mult)
            V.tensor_tensor(tmp2[:], sifo[:, SI:SI + 64], ef[:], ALU.mult)
            V.tensor_tensor(stn[:], stn[:], tmp2[:], ALU.add).then_inc(b_en, 1)
            nc.scalar.wait_ge(b_en, rS)
            nc.scalar.reg_add(rS, rS, 1)
            S.activation(out=rn[:], in_=stn[:], func=AF.Ln)
            S.activation(out=rn[:], in_=rn[:], func=AF.Exp,
                         scale=-1.0).then_inc(b_ar, 1)
            nc.vector.wait_ge(b_ar, rV)
            nc.vector.reg_add(rV, rV, 1)
            V.tensor_tensor(hf[:], sifo[:, SO:SO + 64], stc[:], ALU.mult)
            V.tensor_tensor(hf[:], hf[:], rn[:], ALU.mult)
            if record_hs:
                V.tensor_copy(
                    hsb.rearrange("p (m s) -> p m s", s=2 * INNER)[:, :, j],
                    hf.rearrange("p (m b) -> p m b", b=8)[:, :, 0])
            V.tensor_copy(hT[:], hf[:]).then_inc(b_ec, 1)
            if record_hs and (j == INNER - 1 or j == 2 * INNER - 1):
                nc.sync.wait_ge(b_ec, 16 * cb + (H + 1) * INNER + 1)
                nc.sync.dma_start(
                    out=hsout.rearrange("p (m t) -> p m t", m=8)[
                        :, :, ds(16 * cb + H * INNER, INNER)],
                    in_=hsb.rearrange("p (m s) -> p m s", s=2 * INNER)[
                        :, :, H * INNER:(H + 1) * INNER],
                ).then_inc(b_hs, 16)
    if record_hs:
        nc.sync.wait_ge(b_hs, nbody * 32)


def host_tile_lhsT(w):
    K, M = w.shape
    nk, nm = K // 128, M // 128
    out = np.empty((128, nm * nk * 128), np.float32)
    for m in range(nm):
        for k in range(nk):
            out[:, (m * nk + k) * 128:(m * nk + k + 1) * 128] = \
                w[k * 128:(k + 1) * 128, m * 128:(m + 1) * 128]
    return out


def host_tile_lhsT64(w):
    K, M = w.shape
    nk, nm = K // 64, M // 128
    out = np.empty((64, nm * nk * 128), np.float32)
    for m in range(nm):
        for kh in range(nk):
            out[:, (m * nk + kh) * 128:(m * nk + kh + 1) * 128] = \
                w[kh * 64:(kh + 1) * 64, m * 128:(m + 1) * 128]
    return out


def perp(v):
    C = v.shape[0]
    return np.ascontiguousarray(v.reshape(C // 128, 128).T)



def build_full(nstep, ntok, stages=9):
    nc = bacc_mod.Bacc("TRN2", target_bir_lowering=False)
    t = {}
    TP = ntok + 2 * PT
    def din(name, shape):
        t[name] = nc.dram_tensor(name, shape, F32, kind="ExternalInput").ap()
    def dint(name, shape, dtype=F32):
        t[name] = nc.dram_tensor(name, shape, dtype, kind="Internal").ap()
    din("xb", [D, ntok])
    dint("hd", [128, 8 * ntok])
    din("sxt", [D, nstep * B]); din("wx", [D, 4 * SL]); din("wh", [D, 4 * SL])
    din("weg", [D, SL]); din("biasT", [128, NMG]); din("begT", [128, NMW])
    dint("gd", [128, NMG * (nstep + 2 * INNER) * B])
    din("identin", [128, 128])
    din("w1T", [D, DF]); din("w2T", [DF, D])
    din("qkvwin", [128, 24 * 8 * 128]); din("projwin", [64, 8 * 16 * 128])
    din("fc1win", [128, 32 * 8 * 128]); din("fc2win", [128, 8 * 32 * 128])
    din("vecsin", [128, 8 + 8 + 8 + 8 + 8 + 8 + 24 + 8 + 32 + 8 + 96 + 32])
    t["outd"] = nc.dram_tensor("outd", [128, 8 * ntok], F32, kind="ExternalOutput").ap()
    for nmx in ("x1d", "x2d", "x3d", "xb1d"):
        dint(nmx, [128, 8 * TP])
    dint("k2d", [128, 32 * ntok], BF16)
    dint("attd", [64, 16 * ntok], BF16)
    dint("gld", [128, 32 * ntok], BF16)
    dint("oscd", [1, ntok])
    dint("w1qd", [128, 8 * DF], BF16)
    dint("w2qd", [128, 32 * D], BF16)

    ts = {"xt": t["sxt"], "wx": t["wx"], "wh": t["wh"], "weg": t["weg"],
          "biasT": t["biasT"], "begT": t["begT"], "hsout": t["hd"],
          "gd": t["gd"]}
    scan_sems = []
    def ssem(name):
        h = nc.alloc_semaphore(name="sc_" + name)
        ts[name] = h
        scan_sems.append(h)
    esta = contextlib.ExitStack()
    estb = contextlib.ExitStack()
    estc = contextlib.ExitStack()
    def ssb(est_, name, shape, dtype=F32):
        ts[name] = est_.enter_context(nc.sbuf_tensor("sc_" + name, shape, dtype))
    def spsum(est_, name, shape):
        ts[name] = est_.enter_context(nc.psum_tensor("sc_" + name, shape, F32))
    for s in ["s_init", "a_wl", "a_xl", "a_sq", "a_mm2", "a_mmS", "a_ew1",
              "a_iv", "a_rows", "a_mmB", "a_s1", "a_pg", "a_gact", "a_st",
              "a_pgc", "b_wl", "b_gl", "b_mg", "b_mi", "b_mw", "b_eg", "b_ag",
              "b_gt", "b_ei", "b_en", "b_ai", "b_ae", "b_ar", "b_ec", "b_hs"]:
        ssem(s)
    ssb(estc, "ones128", [1, 128], BF16); ssb(estc, "onescol", [128, 1], BF16)
    ssb(estc, "epsb", [1, 1])
    ssb(estc, "biasb", [128, NMG]); ssb(estc, "begb", [128, NMW])
    ssb(esta, "wxb", [128, 8 * 4 * SL], BF16)
    ssb(esta, "xbuf", [128, 8 * TCA], BF16)
    ssb(esta, "sqb", [128, 2 * TCA], BF16)
    ssb(esta, "s1b", [128, 8 * TCA], BF16)
    ssb(esta, "t8", [128, TCA])
    ssb(esta, "trow", [1, 3 * TCA])
    ssb(esta, "nmub", [1, TCA], BF16); ssb(esta, "invsb", [1, TCA], BF16)
    ssb(esta, "gstage", [128, NMG * TCA])
    spsum(esta, "psS", [1, 2 * TCA]); spsum(esta, "psB", [128, 2 * TCA])
    spsum(esta, "psP", [128, 2 * TCA])
    Vs = _wrap(nc.vector)
    nc.vector.memset(ts["epsb"][:], LN_EPS); nc.vector.drain()
    nc.vector.memset(ts["ones128"][:], 1.0); nc.vector.drain()
    nc.vector.memset(ts["onescol"][:], 1.0); nc.vector.drain()
    nc.sync.dma_start(out=ts["biasb"][:], in_=ts["biasT"][:]).then_inc(ts["s_init"], 16)
    nc.sync.dma_start(out=ts["begb"][:], in_=ts["begT"][:]).then_inc(ts["s_init"], 16)
    nc.vector.wait_ge(ts["s_init"], 32)
    nc.all_engine_barrier()
    emit_phase_a(nc, ts, nstep)
    nc.all_engine_barrier()
    esta.close()
    ssb(estb, "whb", [128, 8 * NT * 128], BF16)
    ssb(estb, "gbuf", [128, 2 * INNER * GSC])
    ssb(estb, "hT", [128, 64], BF16); ssb(estb, "gT", [128, 64], BF16)
    ssb(estb, "stc", [128, 64]); ssb(estb, "stn", [128, 64])
    ssb(estb, "zg", [128, 64]); ssb(estb, "zifo", [128, 192])
    ssb(estb, "sifo", [128, 192])
    ssb(estb, "gf", [128, 64]); ssb(estb, "ef", [128, 64])
    ssb(estb, "tmp", [128, 64]); ssb(estb, "tmp2", [128, 64])
    ssb(estb, "rn", [128, 64]); ssb(estb, "hf", [128, 64])
    ssb(estb, "hsb", [128, 2 * INNER * 8])
    spsum(estb, "psG", [128, 2 * 64]); spsum(estb, "psI", [128, 2 * 192])
    spsum(estb, "psW", [128, 2 * 64])
    emit_phase_b(nc, ts, nstep, record_hs=True)
    nc.all_engine_barrier()
    estb.close()
    estc.close()
    nc.clear_and_free_semaphores(scan_sems)
    nc.all_engine_barrier()
    nc.all_engine_barrier()
    est = contextlib.ExitStack()
    def sb(name, shape, dtype=F32):
        t[name] = est.enter_context(nc.sbuf_tensor(name, shape, dtype))
    def psum(name, shape, dtype=F32):
        t[name] = est.enter_context(nc.psum_tensor(name, shape, dtype))
    def sem(name):
        t[name] = est.enter_context(nc.semaphore(name))
    sb("ones128", [1, 128], BF16); sb("ones64", [1, 64], BF16)
    sb("onescol", [128, 1], BF16); sb("onescf", [128, 1]); sb("ones128f", [1, 128]); sb("ones64f", [1, 64]); sb("epsb", [1, 1])
    sb("ident", [128, 128]); sb("identb", [128, 128], BF16)
    sb("sqb", [128, 4 * 768])
    sb("rowb", [1, 3 * 768])
    sb("qtmp", [128, 768]); sb("qtmp2", [128, 768])
    sb("kint", [128, 768], I32)
    sb("fold", [128, 768])
    sb("colcl", [128, 8]); sb("cl2", [128, 8]); sb("clb", [128, 16], BF16)
    sb("qrow", [1, 2 * 768])
    sb("oscb", [128, 768]); sb("sxb", [128, 768])
    sb("nmuB", [128, 768]); sb("invsB", [128, 768])
    sb("zrow", [128, 8 * PT])
    sb("gammab", [128, 8]); sb("betab", [128, 8])
    sb("ln1wb", [128, 8]); sb("ln1bb", [128, 8])
    sb("ln2wb", [128, 8]); sb("ln2bb", [128, 8])
    sb("qkvbb", [128, 24]); sb("projbb", [128, 8])
    sb("fc1bb", [128, 32]); sb("fc2bb", [128, 8])
    sb("convw", [128, 96]); sb("convb", [128, 32])
    sb("w1sc", [1, 1]); sb("w2sc", [1, 1])
    sb("w1scc", [128, 1]); sb("w2scc", [128, 1])
    sb("vecs", [128, 256])
    psum("psA", [128, 2048]); psum("psB", [128, 1536])
    psum("psD", [128, 384], BF16)
    for s in ["s_init", "ld_w1", "st_w1", "ld_w2", "st_w2",
              "ld_s1", "st_s1",
              "ld_s2a", "lw_s2a", "st_s2a", "ld_s2b", "lw_s2b", "st_s2b",
              "ld_s3a", "lw_s3a", "st_s3a", "ld_s3a2", "lw_s3a2", "st_s3a2",
              "ld_s3b1", "lw_s3b1", "st_s3b1", "ld_s3b2", "lw_s3b2", "st_s3b2",
              "ld_s4a", "lw_s4a", "st_s4a", "ld_s4b", "lw_s4b", "st_s4b"]:
        sem(s)
    t["qkvwd"] = t["qkvwin"]; t["projwd"] = t["projwin"]
    t["fc1wd"] = t["fc1win"]; t["fc2wd"] = t["fc2win"]

    si = t["s_init"]
    nc.sync.dma_start(out=t["ident"][:], in_=t["identin"][:]).then_inc(si, 16)
    nc.sync.dma_start(out=t["vecs"][:, 0:248], in_=t["vecsin"][:, 0:248]).then_inc(si, 16)
    nc.vector.wait_ge(si, 32)
    V = _wrap(nc.vector)
    V.tensor_copy(t["identb"][:], t["ident"][:])
    nc.vector.memset(t["ones128"][:], 1.0); nc.vector.drain()
    nc.vector.memset(t["ones64"][:], 1.0); nc.vector.drain()
    nc.vector.memset(t["onescol"][:], 1.0); nc.vector.drain()
    nc.vector.memset(t["onescf"][:], 1.0); nc.vector.drain()
    nc.vector.memset(t["ones128f"][:], 1.0); nc.vector.drain()
    nc.vector.memset(t["ones64f"][:], 1.0); nc.vector.drain()
    nc.vector.memset(t["epsb"][:], LN_EPS); nc.vector.drain()
    nc.vector.memset(t["zrow"][:], 0.0); nc.vector.drain()
    off = 0
    for name, w in [("gammab", 8), ("betab", 8), ("ln1wb", 8), ("ln1bb", 8),
                    ("ln2wb", 8), ("ln2bb", 8), ("qkvbb", 24), ("projbb", 8),
                    ("fc1bb", 32), ("fc2bb", 8), ("convw", 96), ("convb", 32)]:
        V.tensor_copy(t[name][:], t["vecs"][:, off:off + w])
        off += w
    nc.vector.sem_inc(si, 1)
    nc.sync.wait_ge(si, 33)
    for nmx in ("x1d", "x2d", "x3d"):
        dv = t[nmx].rearrange("p (k t) -> p k t", t=ntok + 2 * PT)
        nc.sync.dma_start(out=dv[:, :, 0:PT],
                          in_=t["zrow"].rearrange("p (k t) -> p k t", t=PT)
                          ).then_inc(si, 16)
        nc.sync.dma_start(out=dv[:, :, ntok + PT:ntok + 2 * PT],
                          in_=t["zrow"].rearrange("p (k t) -> p k t", t=PT)
                          ).then_inc(si, 16)
    nc.sync.wait_ge(si, 33 + 96)
    nc.all_engine_barrier()
    emit_phase2(nc, t, ntok, stages)
    nc.all_engine_barrier()
    est.close()
    return nc




NSTEP_FULL = 4096
_CACHE = {}


def _get_nc():
    if "nc" not in _CACHE:
        nc = build_full(NSTEP_FULL, NSTEP_FULL, 9)
        nc.compile()
        _CACHE["nc"] = nc
    return _CACHE["nc"]


def kernel(**inp):
    f32 = np.float32
    g = lambda k: np.ascontiguousarray(np.asarray(inp[k], dtype=f32))
    x = g("L0_attn_a_x")
    Wi, Wf = g("L0_attn_a_Wi"), g("L0_attn_a_Wf")
    Wg, Wo = g("L0_attn_a_Wg"), g("L0_attn_a_Wo")
    Weg = g("L0_attn_a_Weg")
    bi, bf = g("L0_attn_a_bi"), g("L0_attn_a_bf")
    bg, bo = g("L0_attn_a_bg"), g("L0_attn_a_bo")
    beg = g("L0_attn_a_beg")
    wx = np.ascontiguousarray(np.concatenate([Wg[:D], Wi[:D], Wf[:D], Wo[:D]], 1))
    wh = np.ascontiguousarray(np.concatenate([Wg[D:], Wi[D:], Wf[D:], Wo[D:]], 1))
    biasT = np.ascontiguousarray(np.concatenate([bg, bi, bf, bo]).reshape(NMG, 128).T)
    begT = perp(beg)
    Wd = {
        "gamma": g("L0_attn_a_gamma"), "beta": g("L0_attn_a_beta"),
        "ln1_w": g("L0_attn_b_ln1_w"), "ln1_b": g("L0_attn_b_ln1_b"),
        "ln2_w": g("L0_attn_b_ln2_w"), "ln2_b": g("L0_attn_b_ln2_b"),
        "qkv_w": g("L0_attn_b_qkv_w"), "qkv_b": g("L0_attn_b_qkv_b"),
        "proj_w": g("L0_attn_b_proj_w"), "proj_b": g("L0_attn_b_proj_b"),
        "fc1_w": g("L0_attn_b_fc1_w"), "fc1_b": g("L0_attn_b_fc1_b"),
        "fc2_w": g("L0_attn_b_fc2_w"), "fc2_b": g("L0_attn_b_fc2_b"),
        "w1": g("L0_ffn_w1"), "w2": g("L0_ffn_w2"),
        "conv_w": g("L0_ffn_conv_w"), "conv_b": g("L0_ffn_conv_b"),
    }
    vecs = np.concatenate([
        perp(Wd["gamma"]), perp(Wd["beta"]), perp(Wd["ln1_w"]), perp(Wd["ln1_b"]),
        perp(Wd["ln2_w"]), perp(Wd["ln2_b"]), perp(Wd["qkv_b"]), perp(Wd["proj_b"]),
        perp(Wd["fc1_b"]), perp(Wd["fc2_b"]),
        np.ascontiguousarray(Wd["conv_w"].reshape(32, 128, 3).transpose(1, 0, 2)
                             .reshape(128, 96)),
        perp(Wd["conv_b"])], axis=1).astype(np.float32)
    shared = {
        "wx": wx, "wh": wh, "weg": np.ascontiguousarray(Weg),
        "biasT": biasT, "begT": begT,
        "identin": np.eye(128, dtype=f32),
        "w1T": np.ascontiguousarray(Wd["w1"].T),
        "w2T": np.ascontiguousarray(Wd["w2"].T),
        "qkvwin": host_tile_lhsT(Wd["qkv_w"].T),
        "projwin": host_tile_lhsT64(Wd["proj_w"].T),
        "fc1win": host_tile_lhsT(Wd["fc1_w"].T),
        "fc2win": host_tile_lhsT(Wd["fc2_w"].T),
        "vecsin": vecs,
    }
    S = x.shape[1]
    in_maps = []
    for c in range(8):
        perm = [(c + j) % 8 for j in range(8)]
        m = dict(shared)
        m["sxt"] = np.ascontiguousarray(x[perm].transpose(2, 1, 0).reshape(D, S * B))
        m["xb"] = np.ascontiguousarray(x[c].T)
        in_maps.append(m)
    res = run_bass_kernel_spmd(_get_nc(), in_maps, core_ids=list(range(8)))
    out = np.empty((8, S, D), f32)
    for c in range(8):
        out[c] = (res.results[c]["outd"].reshape(128, 8, S)
                  .transpose(2, 1, 0).reshape(S, D))
    return out


# version 03
# speedup vs baseline: 1.0844x; 1.0844x over previous
import os
os.environ.setdefault("NEURON_SCRATCHPAD_PAGE_SIZE", "640")
import contextlib
import numpy as np
import ml_dtypes
import concourse.bacc as bacc_mod
import concourse.mybir as mybir
from concourse.bass import ds
from concourse.bass_utils import run_bass_kernel_spmd

F32 = mybir.dt.float32
I32 = mybir.dt.int32
BF16 = mybir.dt.bfloat16
AF = mybir.ActivationFunctionType
ALU = mybir.AluOpType

BF16 = mybir.dt.bfloat16
I32 = mybir.dt.int32
AF = mybir.ActivationFunctionType
ALU = mybir.AluOpType

TT = 512
PT = 128
D = 1024
DF = 4096
NH = 16
LN_EPS = 1e-5
HE_FFN = 64
HE_ATT = 128
WF = TT + 2 * HE_FFN
WA = TT + 2 * HE_ATT
MAGIC = 12582912.0
GN = [(1.0, 0.0),
      (0.7847558856010437, 2.4108107510301124e-08),
      (0.3021286129951477, 1.8317987127147717e-09),
      (0.060451213270425797, 3.7214936687846034e-10),
      (0.005342473741620779, 0.0)]
GD = [(1.0, 0.0),
      (1.582640528678894, -4.984901735838321e-08),
      (1.0648928880691528, -2.0261421429523807e-09),
      (0.38475489616394043, 9.13366449140085e-09),
      (0.07580475509166718, 0.0),
      (0.00669455248862505, 0.0)]

INNER = 8
B = 8
SL = 1024
NMG = 4 * SL // 128
NMW = SL // 128
NT = NMG + NMW
CHA = 64
TCA = CHA * B
GSC = NMG * 8

def _wrap(eng):
    class W:
        def __getattr__(self, name):
            fn = getattr(eng, name)
            def call(*a, **k):
                inst = fn(*a, **k)
                eng.drain()
                return inst
            return call
    return W()


class P2:
    def __init__(self, nc, t, ntok):
        self.nc = nc
        self.t = t
        self.ntok = ntok
        self.TP = ntok + 2 * PT
        self.V = _wrap(nc.vector)
        self.S = _wrap(nc.scalar)
        self.scope = None
        self.rG = nc.gpsimd.alloc_register("p2rG")
        self.rV = nc.vector.alloc_register("p2rV")
        self.rGO = nc.gpsimd.alloc_register("p2rGO")
        self.rSO = nc.sync.alloc_register("p2rSO")
        self.rT = nc.tensor.alloc_register("p2rT")
        self.rVo = nc.vector.alloc_register("p2rVo")
        self.rS = nc.scalar.alloc_register("p2rS")
        self.tDP = nc.tensor.alloc_register("p2tDP")
        self.tDPq = nc.tensor.alloc_register("p2tDPq")
        self.vPDq = nc.vector.alloc_register("p2vPDq")
        self.tAP = nc.tensor.alloc_register("p2tAP")
        self.vPD = nc.vector.alloc_register("p2vPD")
        self.vAD = nc.vector.alloc_register("p2vAD")
        self.sPA = nc.scalar.alloc_register("p2sPA")
        self.sDA = nc.scalar.alloc_register("p2sDA")
        self.rY = nc.sync.alloc_register("p2rY")
        self.rY2 = nc.sync.alloc_register("p2rY2")
        self.vSD = nc.vector.alloc_register("p2vSD")
        nc.gpsimd.reg_mov(self.rGO, 0)
        nc.sync.reg_mov(self.rSO, 0)
        self.oG = None
        self.oS = None
        self._stage_regs = []


    def bar(self):
        self.nc.all_engine_barrier()

    def edges_init(self):
        nc, t = self.nc, self.t
        for nm in ("eDP", "ePD", "ePA", "eAP", "eAD", "eDA", "eDS", "eDPq",
                   "ePDq"):
            t[nm] = nc.alloc_semaphore(name="p2" + nm)
        for eng, reg in ((nc.tensor, self.tDP), (nc.tensor, self.tAP),
                         (nc.vector, self.vPD), (nc.vector, self.vAD),
                         (nc.scalar, self.sPA), (nc.scalar, self.sDA),
                         (nc.sync, self.rY), (nc.tensor, self.tDPq),
                         (nc.vector, self.vPDq), (nc.vector, self.vSD)):
            eng.reg_mov(reg, 0)

    def ewait(self, eng, reg, sem, k=1):
        if k:
            eng.reg_add(reg, reg, k)
        eng.wait_ge(sem, reg)

    def sb(self, name, shape, dtype=F32):
        uname = f"{self.stage_name}_{name}"
        self.t[name] = self.scope.enter_context(self.nc.sbuf_tensor(uname, shape, dtype))
        return self.t[name]

    def flush_regs(self):
        for eng in (self.nc.tensor, self.nc.vector, self.nc.scalar,
                    self.nc.gpsimd, self.nc.sync):
            try:
                eng.get_value_cache().barrier()
            except Exception:
                pass

    @contextlib.contextmanager
    def tloop(self, ntile):
        nc = self.nc
        nc.gpsimd.reg_mov(self.rGO, 0)
        nc.sync.reg_mov(self.rSO, 0)
        mx = max(0, self.ntok - TT)
        with nc.Fori(0, ntile) as i:
            self.oG = nc.gpsimd.snap(self.rGO, min_val=0, max_val=mx)
            self.oS = nc.sync.snap(self.rSO, min_val=0, max_val=mx)
            for ov in (self.oG, self.oS):
                try:
                    self._stage_regs.append(ov.val)
                except Exception:
                    pass
            yield i
            nc.gpsimd.reg_add(self.rGO, self.rGO, TT)
            nc.sync.reg_add(self.rSO, self.rSO, TT)

    @contextlib.contextmanager
    def regscope(self):
        nc = self.nc
        prev_stage_regs = self._stage_regs
        self._stage_regs = []
        allocated = []
        orig = nc.alloc_registers
        orig64 = nc.alloc_registers64
        def wrap(*a, **k):
            hs = orig(*a, **k)
            try:
                for h in hs:
                    allocated.append(h)
            except Exception:
                pass
            return hs
        def wrap64(*a, **k):
            hs = orig64(*a, **k)
            try:
                for h in hs:
                    allocated.append(h.lo)
                    allocated.append(h.hi)
            except Exception:
                pass
            return hs
        nc.alloc_registers = wrap
        nc.alloc_registers64 = wrap64
        engs = (nc.tensor, nc.vector, nc.scalar, nc.gpsimd, nc.sync)
        eorig = []
        for e in engs:
            o1, o64 = e.alloc_register, e.alloc_register64
            eorig.append((e, o1, o64))
            def mk1(o1):
                def f(*a, **k):
                    h = o1(*a, **k)
                    allocated.append(h)
                    return h
                return f
            def mk64(o64):
                def f(*a, **k):
                    h = o64(*a, **k)
                    allocated.append(h.lo)
                    allocated.append(h.hi)
                    return h
                return f
            e.alloc_register = mk1(o1)
            e.alloc_register64 = mk64(o64)
        try:
            yield
        finally:
            nc.alloc_registers = orig
            nc.alloc_registers64 = orig64
            for e, o1, o64 in eorig:
                e.alloc_register = o1
                e.alloc_register64 = o64
            for h in allocated + self._stage_regs:
                try:
                    nc.free_register(h)
                except Exception:
                    pass
            self._stage_regs = prev_stage_regs
            self.flush_regs()

    @contextlib.contextmanager
    def stage(self, name):
        old = self.scope
        olds = getattr(self, "stage_name", "")
        self.stage_name = name
        self.scope = contextlib.ExitStack()
        with self.regscope():
            yield
            self.bar()
        self.scope.close()
        self.scope = old
        self.stage_name = olds

    def mmN(self, psum, off, lhsT, rhs, W, start, stop, inc=None):
        a = 0
        while a < W:
            b = min(W, ((off + a) // 512 + 1) * 512 - off)
            inst = self.nc.tensor.matmul(psum[:, off + a:off + b], lhsT,
                                         rhs[:, a:b], start=start, stop=stop)
            if inc is not None and b == W:
                inst.then_inc(inc, 1)
            a = b

    def chunk_ap(self, dram, ti, halo=0, nk=8):
        return dram.rearrange("p (k t) -> p k t", t=self.TP)[
            :, 0:nk, ds(ti * TT + PT - halo, TT + 2 * halo)]

    def chg(self, dram, halo=0, nk=8):
        return dram.rearrange("p (k t) -> p k t", t=self.TP)[
            :, 0:nk, PT - halo:][:, :, ds(self.oG, TT + 2 * halo)]

    def chs(self, dram, nk=8):
        return dram.rearrange("p (k t) -> p k t", t=self.TP)[
            :, 0:nk, PT:][:, :, ds(self.oS, TT)]

    def flatg(self, dram, t_extent, nk):
        return dram.rearrange("p (k t) -> p k t", t=t_extent)[
            :, 0:nk, :][:, :, ds(self.oG, TT)]

    def flats(self, dram, t_extent, nk):
        return dram.rearrange("p (k t) -> p k t", t=t_extent)[
            :, 0:nk, :][:, :, ds(self.oS, TT)]

    def ln_stats(self, xv, nk, W, zero_mean=False):
        nc, V, S, t = self.nc, self.V, self.S, self.t
        psB = t["psB"]
        psS = t["psA"][0:1, :]
        sqb, rowb = t["sqb"], t["rowb"]
        onescf, ones128 = t["onescf"], t["ones128"]
        for r in range(0, nk, 4):
            re = min(r + 4, nk)
            for k in range(r, re):
                S.activation(out=sqb[:, (k - r) * W:(k - r + 1) * W],
                             in_=xv[:, k, 0:W], func=AF.Square)
            self.bar()
            for k in range(r, re):
                if not zero_mean:
                    self.mmN(psS, 0, onescf[:], xv[:, k, 0:W], W,
                             start=(k == 0), stop=(k == nk - 1))
                self.mmN(psS, 1024, onescf[:], sqb[:, (k - r) * W:(k - r + 1) * W],
                         W, start=(k == 0), stop=(k == nk - 1))
            self.bar()
        MU, VAR, NMU = 0, W, 2 * W
        if not zero_mean:
            V.tensor_scalar_mul(rowb[:, MU:MU + W], psS[:, 0:W], 1.0 / D)
            V.tensor_scalar_mul(rowb[:, NMU:NMU + W], rowb[:, MU:MU + W], -1.0)
        V.tensor_scalar_mul(rowb[:, VAR:VAR + W], psS[:, 1024:1024 + W], 1.0 / D)
        if not zero_mean:
            V.tensor_tensor(rowb[:, MU:MU + W], rowb[:, MU:MU + W],
                            rowb[:, MU:MU + W], ALU.mult)
            V.tensor_tensor(rowb[:, VAR:VAR + W], rowb[:, VAR:VAR + W],
                            rowb[:, MU:MU + W], ALU.subtract)
        self.bar()
        S.activation(out=rowb[:, VAR:VAR + W], in_=rowb[:, VAR:VAR + W],
                     func=AF.Ln, bias=t["epsb"][:, 0:1])
        S.activation(out=rowb[:, VAR:VAR + W], in_=rowb[:, VAR:VAR + W],
                     func=AF.Exp, scale=-0.5)
        self.bar()
        if not zero_mean:
            self.mmN(psB, 0, t["ones128f"][:], rowb[:, NMU:NMU + W], W,
                     start=True, stop=True)
        self.mmN(psB, 768, t["ones128f"][:], rowb[:, VAR:VAR + W], W,
                 start=True, stop=True)
        self.bar()
        if not zero_mean:
            V.tensor_copy(t["nmuB"][:, 0:W], psB[:, 0:W])
        V.tensor_copy(t["invsB"][:, 0:W], psB[:, 768:768 + W])
        self.bar()

    def act_quant(self, sv, nk, W, wscale_ap):
        nc, V, S, t = self.nc, self.V, self.S, self.t
        psB = t["psB"]
        fold, ident = t["fold"], t["ident"]
        nchunk = W // 128
        S.activation(out=fold[:, 0:W], in_=sv[:, 0, 0:W], func=AF.Abs)
        for k in range(1, nk):
            S.activation(out=t["qtmp"][:, 0:W], in_=sv[:, k, 0:W], func=AF.Abs)
            self.bar()
            V.tensor_tensor(fold[:, 0:W], fold[:, 0:W], t["qtmp"][:, 0:W], ALU.max)
            self.bar()
        for q in range(nchunk):
            nc.tensor.transpose(psB[:, 768 + q * 128:768 + (q + 1) * 128],
                                fold[:, q * 128:(q + 1) * 128], ident[:])
        self.bar()
        for q in range(nchunk):
            V.tensor_reduce(t["colcl"][:, q:q + 1],
                            psB[:, 768 + q * 128:768 + (q + 1) * 128],
                            mybir.AxisListType.X, ALU.max)
        self.bar()
        V.tensor_scalar_max(t["colcl"][:, 0:nchunk], t["colcl"][:, 0:nchunk], 1e-5)
        self.bar()
        S.activation(out=t["cl2"][:, 0:nchunk], in_=t["colcl"][:, 0:nchunk], func=AF.Ln)
        S.activation(out=t["cl2"][:, 0:nchunk], in_=t["cl2"][:, 0:nchunk],
                     func=AF.Exp, scale=-1.0)
        self.bar()
        V.tensor_scalar_mul(t["cl2"][:, 0:nchunk], t["cl2"][:, 0:nchunk], 127.0)
        V.tensor_single_scalar(t["colcl"][:, 0:nchunk], t["colcl"][:, 0:nchunk],
                               wscale_ap, ALU.mult)
        V.tensor_scalar_mul(t["colcl"][:, 0:nchunk], t["colcl"][:, 0:nchunk],
                            1.0 / 127.0)
        self.bar()
        for q in range(nchunk):
            nc.tensor.matmul(psB[0:1, q * 128:(q + 1) * 128],
                             t["cl2"][:, q:q + 1], t["ident"][:],
                             start=True, stop=True)
            nc.tensor.matmul(psB[0:1, 768 + q * 128:768 + (q + 1) * 128],
                             t["colcl"][:, q:q + 1], t["ident"][:],
                             start=True, stop=True)
        self.bar()
        V.tensor_copy(t["qrow"][:, 0:W], psB[0:1, 0:W])
        V.tensor_copy(t["qrow"][:, 768:768 + W], psB[0:1, 768:768 + W])
        self.bar()
        self.mmN(psB, 0, t["ones128f"][:], t["qrow"][:, 0:W], W, start=True, stop=True)
        self.mmN(psB, 768, t["ones128f"][:], t["qrow"][:, 768:768 + W], W,
                 start=True, stop=True)
        self.bar()
        V.tensor_copy(t["sxb"][:, 0:W], psB[:, 0:W])
        V.tensor_copy(t["oscb"][:, 0:W], psB[:, 768:768 + W])
        self.bar()

    def rint_hop(self, W):
        V, t = self.V, self.t
        kint = t["kint"]
        V.tensor_copy(kint[:, 0:W], t["qtmp"][:, 0:W])
        V.tensor_copy(t["qtmp2"][:, 0:W], kint[:, 0:W])
        V.tensor_tensor(t["qtmp"][:, 0:W], t["qtmp"][:, 0:W],
                        t["qtmp2"][:, 0:W], ALU.subtract)
        V.tensor_scalar(t["qtmp"][:, 0:W], t["qtmp"][:, 0:W], 2.0, None, ALU.mult)
        V.tensor_copy(kint[:, 0:W], t["qtmp"][:, 0:W])
        V.tensor_copy(t["qtmp"][:, 0:W], kint[:, 0:W])
        V.tensor_tensor(t["qtmp2"][:, 0:W], t["qtmp2"][:, 0:W],
                        t["qtmp"][:, 0:W], ALU.add)

    def quantize(self, sv, kt, nk, W):
        V, t = self.V, self.t
        for k in range(nk):
            V.tensor_tensor(t["qtmp"][:, 0:W], sv[:, k, 0:W], t["sxb"][:, 0:W],
                            ALU.mult)
            V.tensor_scalar(t["qtmp"][:, 0:W], t["qtmp"][:, 0:W], 127.0, -127.0,
                            ALU.min, ALU.max)
            V.tensor_scalar(t["qtmp"][:, 0:W], t["qtmp"][:, 0:W], MAGIC, -MAGIC,
                            ALU.add, ALU.add)
            self.rint_hop(W)
            V.tensor_copy(kt[:, k * W:(k + 1) * W], t["qtmp2"][:, 0:W])
        self.bar()

    def quantize2(self, sv, ktv, nk, W):
        V, t = self.V, self.t
        for k in range(nk):
            V.tensor_tensor(t["qtmp"][:, 0:W], sv[:, k, 0:W], t["sxb"][:, 0:W],
                            ALU.mult)
            V.tensor_scalar(t["qtmp"][:, 0:W], t["qtmp"][:, 0:W], 127.0, -127.0,
                            ALU.min, ALU.max)
            V.tensor_scalar(t["qtmp"][:, 0:W], t["qtmp"][:, 0:W], MAGIC, -MAGIC,
                            ALU.add, ALU.add)
            self.rint_hop(W)
            V.tensor_copy(ktv[:, k, 0:W], t["qtmp2"][:, 0:W])
        self.bar()


def emit_wprep(p, wsrc, wqd, sc_key, nk, nm, tag):
    nc, t, V, S = p.nc, p.t, p.V, p.S
    ld, st = t["ld_w" + tag], t["st_w" + tag]
    M = nm * 128
    with p.stage("wprep" + tag):
        wtmp = p.sb("wtmp", [128, M])
        wtmp2 = p.sb("wtmp2", [128, M])
        wint = p.sb("wint", [128, M], I32)
        wbf = p.sb("wbf", [128, M], BF16)
        acc = p.sb("wacc", [128, 2])
        sc2 = p.sb("wsc2", [1, 2])
        swcol = p.sb("swcol", [128, 1])
        nc.vector.memset(acc[:, 0:1], 0.0)
        nc.vector.drain()
        p.bar()
        for k in range(nk):
            nc.gpsimd.dma_start(out=wtmp[:], in_=wsrc[k * 128:(k + 1) * 128, :]
                                ).then_inc(ld, 16)
            nc.vector.wait_ge(ld, 16 * (k + 1))
            p.bar()
            V.tensor_reduce(acc[:, 1:2], wtmp[:], mybir.AxisListType.X,
                            ALU.add, apply_absolute_value=True)
            V.tensor_tensor(acc[:, 0:1], acc[:, 0:1], acc[:, 1:2], ALU.add)
            p.bar()
        p.bar()
        nc.tensor.matmul(t["psB"][0:1, 0:1], acc[:, 0:1], t["onescf"][:],
                         start=True, stop=True)
        p.bar()
        V.tensor_scalar_mul(sc2[:, 0:1], t["psB"][0:1, 0:1], 1.0 / (nk * 128 * M))
        V.tensor_scalar_max(sc2[:, 0:1], sc2[:, 0:1], 1e-5)
        V.tensor_copy(t[sc_key][0:1, 0:1], sc2[:, 0:1])
        p.bar()
        S.activation(out=sc2[:, 1:2], in_=sc2[:, 0:1], func=AF.Ln)
        S.activation(out=sc2[:, 1:2], in_=sc2[:, 1:2], func=AF.Exp, scale=-1.0)
        p.bar()
        nc.tensor.matmul(t["psB"][:, 0:1], t["ones128f"][:], sc2[:, 1:2],
                         start=True, stop=True)
        nc.tensor.matmul(t["psB"][:, 512:513], t["ones128f"][:], sc2[:, 0:1],
                         start=True, stop=True)
        p.bar()
        V.tensor_copy(swcol[:], t["psB"][:, 0:1])
        V.tensor_copy(t[sc_key + "c"][:, 0:1], t["psB"][:, 512:513])
        p.bar()
        for k in range(nk):
            nc.gpsimd.dma_start(out=wtmp[:], in_=wsrc[k * 128:(k + 1) * 128, :]
                                ).then_inc(ld, 16)
            nc.vector.wait_ge(ld, 16 * (nk + k + 1))
            p.bar()
            V.tensor_single_scalar(wtmp[:], wtmp[:], swcol[:, 0:1], ALU.mult)
            V.tensor_scalar(wtmp[:], wtmp[:], 1.0, -1.0, ALU.min, ALU.max)
            V.tensor_scalar(wtmp[:], wtmp[:], MAGIC, -MAGIC, ALU.add, ALU.add)
            V.tensor_copy(wint[:], wtmp[:])
            V.tensor_copy(wtmp2[:], wint[:])
            V.tensor_tensor(wtmp[:], wtmp[:], wtmp2[:], ALU.subtract)
            V.tensor_scalar(wtmp[:], wtmp[:], 2.0, None, ALU.mult)
            V.tensor_copy(wint[:], wtmp[:])
            V.tensor_copy(wtmp[:], wint[:])
            V.tensor_tensor(wtmp2[:], wtmp2[:], wtmp[:], ALU.add)
            V.tensor_copy(wbf[:], wtmp2[:])
            p.bar()
            nc.sync.dma_start(
                out=wqd.rearrange("p (m k c) -> p m k c", k=nk, c=128)[:, :, k, :],
                in_=wbf.rearrange("p (m c) -> p m c", c=128)).then_inc(st, 16)
            nc.sync.wait_ge(st, 16 * (k + 1))
        p.bar()


def emit_s1(p):
    nc, t, V, S = p.nc, p.t, p.V, p.S
    ntile = p.ntok // TT
    ld, st = t["ld_s1"], t["st_s1"]
    psB = t["psB"]
    W = TT
    BCB = 768
    with p.stage("s1"):
        xt = p.sb("s1x", [128, 8 * W])
        ht = p.sb("s1h", [128, 8 * W])
        yt = p.sb("s1y", [128, 8 * W])
        stg = p.sb("s1stg", [128, 8 * W])
        nc.gpsimd.reg_mov(p.rG, 0)
        nc.vector.reg_mov(p.rV, 32)
        with p.tloop(ntile) as i:
            nc.gpsimd.wait_ge(st, p.rG)
            nc.gpsimd.reg_add(p.rG, p.rG, 16)
            nc.gpsimd.dma_start(out=xt.rearrange("p (k t) -> p k t", t=W),
                                in_=t["xb"].rearrange("(k p) t -> p k t", p=128)[
                                    :, :, ds(p.oG, TT)]).then_inc(ld, 16)
            nc.gpsimd.dma_start(out=ht.rearrange("p (k t) -> p k t", t=W),
                                in_=p.flatg(t["hd"], p.ntok, 8)).then_inc(ld, 16)
            nc.vector.wait_ge(ld, p.rV)
            nc.vector.reg_add(p.rV, p.rV, 32)
            p.bar()
            xv = xt.rearrange("p (k t) -> p k t", t=W)
            yv = yt.rearrange("p (k t) -> p k t", t=W)
            p.ln_stats(xv, 8, W)
            for k in range(8):
                V.tensor_tensor(t["qtmp"][:, 0:W], xv[:, k, :], t["nmuB"][:, 0:W], ALU.add)
                V.tensor_tensor(t["qtmp"][:, 0:W], t["qtmp"][:, 0:W],
                                t["invsB"][:, 0:W], ALU.mult)
                V.tensor_tensor(yv[:, k, :], t["qtmp"][:, 0:W],
                                ht[:, k * W:(k + 1) * W], ALU.add)
            p.bar()
            p.ln_stats(yv, 8, W)
            for k in range(8):
                V.tensor_tensor(t["qtmp"][:, 0:W], yv[:, k, :], t["nmuB"][:, 0:W], ALU.add)
                V.tensor_tensor(t["qtmp"][:, 0:W], t["qtmp"][:, 0:W],
                                t["invsB"][:, 0:W], ALU.mult)
                V.tensor_scalar(t["qtmp"][:, 0:W], t["qtmp"][:, 0:W],
                                t["gammab"][:, k:k + 1], t["betab"][:, k:k + 1],
                                ALU.mult, ALU.add)
                V.tensor_tensor(stg[:, k * W:(k + 1) * W], xv[:, k, :],
                                t["qtmp"][:, 0:W], ALU.add)
            p.bar()
            nc.sync.dma_start(out=p.chs(t["x1d"]),
                              in_=stg.rearrange("p (k t) -> p k t", t=W)
                              ).then_inc(st, 16)
        nc.sync.wait_ge(st, 16 * ntile)


def emit_ffn1(p, ind_key, tag):
    nc, t, V, S = p.nc, p.t, p.V, p.S
    ntile = p.ntok // TT
    ld, lw, st = t["ld_" + tag], t["lw_" + tag], t["st_" + tag]
    W = WF
    WC = TT
    psA, psB = t["psA"], t["psB"]
    BCB = 768
    with p.stage(tag):
        w1q = p.sb("w1qs", [128, 8 * DF], BF16)
        xt = p.sb("f1x", [128, 8 * W])
        s2t = xt
        k1 = p.sb("f1k", [128, 8 * W], BF16)
        hmid = p.sb("f1h", [128, 32 * W], BF16)
        hmv = hmid.rearrange("p (m t) -> p m t", t=W)
        oscrow = p.sb("f1osc", [1, WC])
        nc.gpsimd.dma_start(out=w1q[:], in_=t["w1qd"][:]).then_inc(lw, 16)
        nc.tensor.wait_ge(lw, 16)
        p.bar()
        nc.gpsimd.reg_mov(p.rG, 0)
        nc.vector.reg_mov(p.rV, 16)
        with p.tloop(ntile) as i:
            nc.gpsimd.wait_ge(st, p.rG)
            nc.gpsimd.reg_add(p.rG, p.rG, 32)
            nc.gpsimd.dma_start(out=xt.rearrange("p (k t) -> p k t", t=W),
                                in_=p.chg(t[ind_key], halo=HE_FFN)).then_inc(ld, 16)
            nc.vector.wait_ge(ld, p.rV)
            nc.vector.reg_add(p.rV, p.rV, 16)
            p.bar()
            xv = xt.rearrange("p (k t) -> p k t", t=W)
            sv = xv
            p.ln_stats(xv, 8, W)
            for k in range(8):
                V.tensor_tensor(t["qtmp"][:, 0:W], xv[:, k, :], t["nmuB"][:, 0:W], ALU.add)
                V.tensor_tensor(sv[:, k, :], t["qtmp"][:, 0:W],
                                t["invsB"][:, 0:W], ALU.mult)
            p.bar()
            p.act_quant(sv, 8, W, t["w1scc"][:, 0:1])
            p.quantize(sv, k1, 8, W)
            for m in range(32):
                pp = m % 2
                for k in range(8):
                    p.mmN(psA, pp * 1024,
                          w1q[:, (m * 8 + k) * 128:(m * 8 + k + 1) * 128],
                          k1[:, k * W:(k + 1) * W], W,
                          start=(k == 0), stop=(k == 7))
                if m % 2 == 1:
                    p.bar()
                    for mm in (m - 1, m):
                        V.tensor_tensor(hmid[:, mm * W:(mm + 1) * W],
                                        psA[:, (mm % 2) * 1024:(mm % 2) * 1024 + W],
                                        t["oscb"][:, 0:W], ALU.mult)
                    p.bar()
            CV = W - 2
            sqb = t["sqb"]
            for m0 in range(0, 32, 4):
                for m in range(m0, m0 + 4):
                    sl = sqb[:, (m % 4) * 768:(m % 4) * 768 + CV]
                    V.tensor_scalar(sl, hmid[:, m * W:m * W + CV],
                                    t["convw"][:, 3 * m:3 * m + 1],
                                    t["convb"][:, m:m + 1], ALU.mult, ALU.add)
                    V.scalar_tensor_tensor(sl, hmid[:, m * W + 1:m * W + 1 + CV],
                                           t["convw"][:, 3 * m + 1:3 * m + 2],
                                           sl, ALU.mult, ALU.add)
                    V.scalar_tensor_tensor(sl, hmid[:, m * W + 2:m * W + 2 + CV],
                                           t["convw"][:, 3 * m + 2:3 * m + 3],
                                           sl, ALU.mult, ALU.add)
                p.bar()
                for m in range(m0, m0 + 4):
                    S.activation(out=hmv[:, m, 0:WC],
                                 in_=sqb[:, (m % 4) * 768 + HE_FFN - 1:
                                         (m % 4) * 768 + HE_FFN - 1 + WC],
                                 func=AF.Silu)
                p.bar()
            p.act_quant(hmv[:, :, 0:WC], 32, WC, t["w2scc"][:, 0:1])
            p.quantize2(hmv[:, :, 0:WC], hmv[:, :, 0:WC], 32, WC)
            V.tensor_copy(oscrow[:, 0:WC], t["qrow"][:, 768:768 + WC])
            p.bar()
            nc.sync.dma_start(
                out=p.flats(t["k2d"], p.ntok, 32),
                in_=hmv[:, :, 0:WC]).then_inc(st, 16)
            nc.sync.dma_start(
                out=t["oscd"][:, ds(p.oS, WC)], in_=oscrow[:, 0:WC]
                ).then_inc(st, 16)
        nc.sync.wait_ge(st, 32 * ntile)


def emit_ffn2(p, ind_key, outd_key, tag):
    nc, t, V, S = p.nc, p.t, p.V, p.S
    ntile = p.ntok // TT
    ld, lw, st = t["ld_" + tag], t["lw_" + tag], t["st_" + tag]
    WC = TT
    psA, psB = t["psA"], t["psB"]
    with p.stage(tag):
        w2q = p.sb("w2qs", [128, 32 * D], BF16)
        k2 = p.sb("f2k", [128, 32 * WC], BF16)
        xc = p.sb("f2x", [128, 8 * WC])
        stg = p.sb("f2stg", [128, 8 * WC])
        oscrow = p.sb("f2osc", [1, WC])
        nc.gpsimd.dma_start(out=w2q[:], in_=t["w2qd"][:]).then_inc(lw, 16)
        nc.tensor.wait_ge(lw, 16)
        p.bar()
        nc.gpsimd.reg_mov(p.rG, 0)
        nc.vector.reg_mov(p.rV, 48)
        with p.tloop(ntile) as i:
            nc.gpsimd.wait_ge(st, p.rG)
            nc.gpsimd.reg_add(p.rG, p.rG, 16)
            nc.gpsimd.dma_start(out=k2.rearrange("p (k t) -> p k t", t=WC),
                                in_=p.flatg(t["k2d"], p.ntok, 32)).then_inc(ld, 16)
            nc.gpsimd.dma_start(out=xc.rearrange("p (k t) -> p k t", t=WC),
                                in_=p.chg(t[ind_key])).then_inc(ld, 16)
            nc.gpsimd.dma_start(out=oscrow[:, 0:WC],
                                in_=t["oscd"][:, ds(p.oG, WC)]).then_inc(ld, 16)
            nc.vector.wait_ge(ld, p.rV)
            nc.vector.reg_add(p.rV, p.rV, 48)
            p.bar()
            p.mmN(psB, 0, t["ones128f"][:], oscrow[:, 0:WC], WC, start=True, stop=True)
            p.bar()
            V.tensor_copy(t["oscb"][:, 0:WC], psB[:, 0:WC])
            p.bar()
            for m in range(8):
                pp = m % 2
                for k in range(32):
                    p.mmN(psA, pp * 1024,
                          w2q[:, (m * 32 + k) * 128:(m * 32 + k + 1) * 128],
                          k2[:, k * WC:(k + 1) * WC], WC,
                          start=(k == 0), stop=(k == 31))
                p.bar()
                V.tensor_tensor(t["qtmp"][:, 0:WC], psA[:, pp * 1024:pp * 1024 + WC],
                                t["oscb"][:, 0:WC], ALU.mult)
                V.tensor_tensor(stg[:, m * WC:(m + 1) * WC], t["qtmp"][:, 0:WC],
                                xc[:, m * WC:(m + 1) * WC], ALU.add)
                p.bar()
            if outd_key == "outd":
                nc.sync.dma_start(out=p.flats(t[outd_key], p.ntok, 8),
                                  in_=stg.rearrange("p (k t) -> p k t", t=WC)
                                  ).then_inc(st, 16)
            else:
                nc.sync.dma_start(out=p.chs(t[outd_key]),
                                  in_=stg.rearrange("p (k t) -> p k t", t=WC)
                                  ).then_inc(st, 16)
        nc.sync.wait_ge(st, 16 * ntile)


def emit_s3a(p):
    nc, t, V, S = p.nc, p.t, p.V, p.S
    ntile = p.ntok // TT
    ld, lw, st = t["ld_s3a"], t["lw_s3a"], t["st_s3a"]
    W = WA
    WC = TT
    psA, psB, psD = t["psA"], t["psB"], t["psD"]
    eDP, ePD, ePA, eAP = t["eDP"], t["ePD"], t["ePA"], t["eAP"]
    eAD, eDA, eDS = t["eAD"], t["eDA"], t["eDS"]
    eDPq, ePDq = t["eDPq"], t["ePDq"]
    with p.stage("s3a"):
        qkvw = p.sb("qkvws", [128, 8 * 3 * D], BF16)
        xt = p.sb("a_x", [128, 8 * W])
        xnt = p.sb("a_xn", [128, 8 * W], BF16)
        qkvT = p.sb("a_qkv", [128, 24 * W], BF16)
        kst = p.sb("a_kst", [128, W], BF16)
        vst = p.sb("a_vst", [128, W], BF16)
        qst = p.sb("a_qst", [128, 512], BF16)
        vT = p.sb("a_vT", [128, 768], BF16)
        expS = p.sb("a_exp", [128, 4 * 384], BF16)
        rsA = p.sb("a_rsA", [1, 512])
        rsB = p.sb("a_rsB", [1, 512])
        rn = p.sb("a_rn", [64, 512])
        attnT = p.sb("a_at", [64, 2 * WC], BF16)
        nc.gpsimd.dma_start(out=qkvw[:], in_=t["qkvwd"][:]).then_inc(lw, 16)
        nc.tensor.wait_ge(lw, 16)
        p.bar()
        nc.gpsimd.reg_mov(p.rG, 0)
        nc.vector.reg_mov(p.rV, 16)
        nc.vector.sem_inc(eDA, 1)
        nc.tensor.sem_inc(ePA, 1)
        with p.tloop(ntile) as i:
            nc.gpsimd.wait_ge(st, p.rG)
            nc.gpsimd.reg_add(p.rG, p.rG, 128)
            nc.sync.reg_mov(p.rY2, 0)
            nc.gpsimd.dma_start(out=xt.rearrange("p (k t) -> p k t", t=W),
                                in_=p.chg(t["x2d"], halo=HE_ATT)).then_inc(ld, 16)
            nc.vector.wait_ge(ld, p.rV)
            nc.vector.reg_add(p.rV, p.rV, 16)
            p.bar()
            xv = xt.rearrange("p (k t) -> p k t", t=W)
            p.ln_stats(xv, 8, W)
            for k in range(8):
                V.tensor_tensor(t["qtmp"][:, 0:W], xv[:, k, :], t["nmuB"][:, 0:W], ALU.add)
                V.tensor_tensor(xv[:, k, :], t["qtmp"][:, 0:W],
                                t["invsB"][:, 0:W], ALU.mult)
            p.bar()
            p.ln_stats(xv, 8, W, zero_mean=True)
            for k in range(8):
                V.tensor_tensor(t["qtmp"][:, 0:W], xv[:, k, :],
                                t["invsB"][:, 0:W], ALU.mult)
                V.tensor_scalar(xnt[:, k * W:(k + 1) * W], t["qtmp"][:, 0:W],
                                t["ln1wb"][:, k:k + 1], t["ln1bb"][:, k:k + 1],
                                ALU.mult, ALU.add)
            p.bar()
            for m in range(24):
                pp = m % 2
                if pp == 0 and m > 0:
                    p.ewait(nc.tensor, p.tDPq, eDPq)
                for k in range(8):
                    p.mmN(psA, pp * 1024,
                          qkvw[:, (m * 8 + k) * 128:(m * 8 + k + 1) * 128],
                          xnt[:, k * W:(k + 1) * W], W,
                          start=(k == 0), stop=(k == 7),
                          inc=(ePDq if (k == 7 and pp == 1) else None))
                if pp == 1:
                    p.ewait(nc.vector, p.vPDq, ePDq)
                    for mm2 in (m - 1, m):
                        sc = 0.125 if mm2 < 8 else 1.0
                        inst = V.tensor_scalar(
                            qkvT[:, mm2 * W:(mm2 + 1) * W],
                            psA[:, (mm2 % 2) * 1024:(mm2 % 2) * 1024 + W],
                            t["qkvbb"][:, mm2:mm2 + 1], sc, ALU.add, ALU.mult)
                    if m < 23:
                        inst.then_inc(eDPq, 1)
            nc.vector.reg_mov(p.rVo, 0)
            expv = expS.rearrange("p (q c) -> p q c", c=384)
            psAs = psA.rearrange("p (q c) -> p q c", c=512)
            with nc.Fori(0, NH // 2) as hp:
                oV = nc.vector.snap(p.rVo, min_val=0, max_val=7 * W)
                oY2 = nc.sync.snap(p.rY2, min_val=0, max_val=14)
                p._stage_regs += [oV.val, oY2.val]
                p.ewait(nc.vector, p.vPD, ePD, k=0)
                V.tensor_copy(kst[:], qkvT[:, ds(oV + 8 * W, W)])
                V.tensor_copy(vst[:], qkvT[:, ds(oV + 16 * W, W)])
                inst = V.tensor_copy(qst[:], qkvT[:, ds(oV + 128, 512)])
                inst.then_inc(eDP, 1)
                for b6 in (0, 1):
                    base = b6 * 64
                    p.ewait(nc.tensor, p.tDP, eDP, k=1)
                    for b in range(6):
                        nc.tensor.transpose(
                            psD[:, b * 64:(b + 1) * 64],
                            vst[base:base + 64, b * 128:(b + 1) * 128],
                            t["identb"][base:base + 64, base:base + 64])
                    nc.tensor.drain()
                    nc.tensor.sem_inc(ePD, 1)
                    p.ewait(nc.vector, p.vPD, ePD, k=1)
                    inst = V.tensor_copy(vT[:, b6 * 384:(b6 + 1) * 384],
                                         psD[:, 0:384])
                    if b6 == 0:
                        inst.then_inc(eDP, 1)
                for b6 in (0, 1):
                    base = b6 * 64
                    rs = rsA if b6 == 0 else rsB
                    p.ewait(nc.tensor, p.tAP, eAP, k=0)
                    for qb in range(4):
                        for b in range(3):
                            inst = nc.tensor.matmul(
                                psA[:, qb * 512 + b * 128:qb * 512 + (b + 1) * 128],
                                kst[base:base + 64, (qb + b) * 128:(qb + b + 1) * 128],
                                qst[base:base + 64, qb * 128:(qb + 1) * 128],
                                start=True, stop=True)
                    inst.then_inc(ePA, 1)
                    p.ewait(nc.scalar, p.sPA, ePA, k=2)
                    p.ewait(nc.scalar, p.sDA, eDA)
                    inst = S.activation(out=expv[:, 0:4, 0:384], in_=psAs[:, 0:4, 0:384],
                                        func=AF.Exp)
                    inst.then_inc(eAD, 1)
                    nc.scalar.sem_inc(eAP, 1)
                    p.ewait(nc.vector, p.vAD, eAD)
                    with nc.vector.If(i == 0):
                        nc.vector.memset(expS[:, 0:128], 0.0)
                        nc.vector.drain()
                    with nc.vector.If(i == ntile - 1):
                        nc.vector.memset(expS[:, 3 * 384 + 256:4 * 384], 0.0)
                        nc.vector.drain()
                    nc.vector.sem_inc(eDP, 1)
                    nc.vector.sem_inc(eDA, 1)
                    p.ewait(nc.tensor, p.tDP, eDP)
                    p.ewait(nc.tensor, p.tAP, eAP, k=0)
                    for qb in range(4):
                        for b in range(3):
                            inst = nc.tensor.matmul(
                                psA[0:1, qb * 512 + 384:qb * 512 + 512],
                                t["onescol"][:],
                                expS[:, qb * 384 + b * 128:qb * 384 + (b + 1) * 128],
                                start=(b == 0), stop=(b == 2))
                    inst.then_inc(ePA, 1)
                    for qb in range(4):
                        for b in range(3):
                            inst = nc.tensor.matmul(
                                psB[0:64, qb * 128:(qb + 1) * 128],
                                vT[:, (b6 * 6 + qb + b) * 64:(b6 * 6 + qb + b + 1) * 64],
                                expS[:, qb * 384 + b * 128:qb * 384 + (b + 1) * 128],
                                start=(b == 0), stop=(b == 2))
                    inst.then_inc(ePA, 1)
                    p.ewait(nc.scalar, p.sPA, ePA)
                    S.activation(out=rs.rearrange("o (q c) -> o q c", c=128),
                                 in_=psAs[0:1, 0:4, 384:512], func=AF.Ln)
                    inst = S.activation(out=rs[:, 0:512], in_=rs[:, 0:512],
                                        func=AF.Exp, scale=-1.0)
                    inst.then_inc(eAP, 1)
                    nc.scalar.sem_inc(eAP, 1)
                    p.ewait(nc.tensor, p.tAP, eAP, k=3)
                    inst = nc.tensor.matmul(psB[0:64, 512:1024], t["ones64f"][:],
                                            rs[:, 0:512], start=True, stop=True)
                    inst.then_inc(ePD, 1)
                    p.ewait(nc.vector, p.vPD, ePD)
                    V.tensor_copy(rn[:, 0:512], psB[0:64, 512:1024])
                    if b6 == 0:
                        nc.vector.wait_ge(st, p.vSD)
                        nc.vector.reg_add(p.vSD, p.vSD, 16)
                    inst = V.tensor_tensor(attnT[0:64, b6 * WC:(b6 + 1) * WC],
                                           psB[0:64, 0:512], rn[:, 0:512], ALU.mult)
                    if b6 == 1:
                        nc.vector.sem_inc(eDS, 1)
                p.ewait(nc.sync, p.rY, eDS)
                nc.sync.dma_start(
                    out=t["attd"].rearrange("p (k t) -> p k t", t=p.ntok)[
                        :, ds(oY2, 2), ds(p.oS, TT)],
                    in_=attnT.rearrange("p (c t) -> p c t", t=WC)
                    ).then_inc(st, 16)
                nc.vector.reg_add(p.rVo, p.rVo, W)
                nc.sync.reg_add(p.rY2, p.rY2, 2)
        nc.sync.wait_ge(st, 128 * ntile)


def emit_s3a2(p):
    nc, t, V, S = p.nc, p.t, p.V, p.S
    ntile = p.ntok // TT
    ld, lw, st = t["ld_s3a2"], t["lw_s3a2"], t["st_s3a2"]
    WC = TT
    psA = t["psA"]
    with p.stage("s3a2"):
        projw = p.sb("projws", [64, 16 * D], BF16)
        attnT = p.sb("p_at", [64, 16 * WC], BF16)
        x2c = p.sb("p_x2", [128, 8 * WC])
        s3c = p.sb("p_s3", [128, 8 * WC])
        stg = p.sb("p_stg", [128, 8 * WC])
        nc.gpsimd.dma_start(out=projw[:], in_=t["projwd"][:]).then_inc(lw, 16)
        nc.tensor.wait_ge(lw, 16)
        p.bar()
        nc.gpsimd.reg_mov(p.rG, 0)
        nc.vector.reg_mov(p.rV, 32)
        with p.tloop(ntile) as i:
            nc.gpsimd.wait_ge(st, p.rG)
            nc.gpsimd.reg_add(p.rG, p.rG, 16)
            nc.gpsimd.dma_start(out=attnT.rearrange("p (c t) -> p c t", t=WC),
                                in_=p.flatg(t["attd"], p.ntok, 16)).then_inc(ld, 16)
            nc.gpsimd.dma_start(out=x2c.rearrange("p (k t) -> p k t", t=WC),
                                in_=p.chg(t["x2d"])).then_inc(ld, 16)
            nc.vector.wait_ge(ld, p.rV)
            nc.vector.reg_add(p.rV, p.rV, 32)
            p.bar()
            x2v = x2c.rearrange("p (k t) -> p k t", t=WC)
            p.ln_stats(x2v, 8, WC)
            for k in range(8):
                V.tensor_tensor(t["qtmp"][:, 0:WC], x2v[:, k, :],
                                t["nmuB"][:, 0:WC], ALU.add)
                V.tensor_tensor(s3c[:, k * WC:(k + 1) * WC], t["qtmp"][:, 0:WC],
                                t["invsB"][:, 0:WC], ALU.mult)
            p.bar()
            for m in range(8):
                pp = m % 2
                for kh in range(16):
                    p.mmN(psA, pp * 1024,
                          projw[:, (m * 16 + kh) * 128:(m * 16 + kh + 1) * 128],
                          attnT[:, kh * WC:(kh + 1) * WC], WC,
                          start=(kh == 0), stop=(kh == 15))
                p.bar()
                V.tensor_scalar(t["qtmp"][:, 0:WC], psA[:, pp * 1024:pp * 1024 + WC],
                                t["projbb"][:, m:m + 1], None, ALU.add)
                V.tensor_tensor(stg[:, m * WC:(m + 1) * WC], t["qtmp"][:, 0:WC],
                                s3c[:, m * WC:(m + 1) * WC], ALU.add)
                p.bar()
            nc.sync.dma_start(out=p.chs(t["xb1d"]),
                              in_=stg.rearrange("p (k t) -> p k t", t=WC)
                              ).then_inc(st, 16)
        nc.sync.wait_ge(st, 16 * ntile)


def emit_s3b1(p):
    nc, t, V, S = p.nc, p.t, p.V, p.S
    ntile = p.ntok // TT
    ld, lw, st = t["ld_s3b1"], t["lw_s3b1"], t["st_s3b1"]
    W = TT
    psA, psB = t["psA"], t["psB"]
    BCB = 768
    with p.stage("s3b1"):
        fc1w = p.sb("fc1ws", [128, 8 * DF], BF16)
        xt = p.sb("b1x", [128, 8 * W])
        y2t = p.sb("b1y", [128, 8 * W], BF16)
        glb = p.sb("b1g", [128, 32 * W], BF16)
        u8 = p.sb("b1u8", [128, 8 * W])
        xa = p.sb("b1xa", [128, W])
        xc = p.sb("b1xc", [128, W])
        nh = p.sb("b1nh", [128, W])
        dh = p.sb("b1dh", [128, W])
        eb = p.sb("b1e", [128, W])
        nc.gpsimd.dma_start(out=fc1w[:], in_=t["fc1wd"][:]).then_inc(lw, 16)
        nc.tensor.wait_ge(lw, 16)
        p.bar()
        nc.gpsimd.reg_mov(p.rG, 0)
        nc.vector.reg_mov(p.rV, 16)
        with p.tloop(ntile) as i:
            nc.gpsimd.wait_ge(st, p.rG)
            nc.gpsimd.reg_add(p.rG, p.rG, 16)
            nc.gpsimd.dma_start(out=xt.rearrange("p (k t) -> p k t", t=W),
                                in_=p.chg(t["xb1d"])).then_inc(ld, 16)
            nc.vector.wait_ge(ld, p.rV)
            nc.vector.reg_add(p.rV, p.rV, 16)
            p.bar()
            xv = xt.rearrange("p (k t) -> p k t", t=W)
            p.ln_stats(xv, 8, W)
            for k in range(8):
                V.tensor_tensor(t["qtmp"][:, 0:W], xv[:, k, :], t["nmuB"][:, 0:W], ALU.add)
                V.tensor_tensor(t["qtmp"][:, 0:W], t["qtmp"][:, 0:W],
                                t["invsB"][:, 0:W], ALU.mult)
                V.tensor_scalar(y2t[:, k * W:(k + 1) * W], t["qtmp"][:, 0:W],
                                t["ln2wb"][:, k:k + 1], t["ln2bb"][:, k:k + 1],
                                ALU.mult, ALU.add)
            p.bar()
            glv = glb.rearrange("p (m t) -> p m t", t=W)
            u8v = u8.rearrange("p (m t) -> p m t", t=W)
            nc.vector.reg_mov(p.rVo, 0)
            nc.scalar.reg_mov(p.rS, 0)
            for g in range(4):
                for m8 in range(8):
                    m = g * 8 + m8
                    for k in range(8):
                        p.mmN(psA, (m8 % 4) * 512,
                              fc1w[:, (m * 8 + k) * 128:(m * 8 + k + 1) * 128],
                              y2t[:, k * W:(k + 1) * W], W,
                              start=(k == 0), stop=(k == 7))
                    if m8 % 4 == 3:
                        p.bar()
                        for q4 in range(4):
                            V.tensor_scalar(u8v[:, m8 - 3 + q4, 0:W],
                                            psA[:, q4 * 512:q4 * 512 + W],
                                            t["fc1bb"][:, m - 3 + q4:m - 2 + q4],
                                            None, ALU.add)
                        p.bar()
                gs = p.regscope(); gs.__enter__()
                with nc.Fori(0, 8) as mi:
                    oD = nc.vector.snap(p.rVo, min_val=g * 8, max_val=g * 8 + 7)
                    oA = nc.scalar.snap(p.rS, min_val=0, max_val=7)
                    p._stage_regs += [oD.val, oA.val]
                    uS = u8v[:, ds(oA, 1), 0:W]
                    uV = u8v[:, ds(oD - g * 8, 1), 0:W]
                    S.activation(out=xa[:], in_=uS, func=AF.Abs)
                    S.activation(out=nh[:], in_=uS, func=AF.Square)
                    S.activation(out=eb[:], in_=nh[:], func=AF.Exp, scale=-0.5)
                    p.bar()
                    V.tensor_scalar(xc[:], xa[:], 16.0, None, ALU.min)
                    V.tensor_scalar(nh[:], xc[:], GN[4][0], GN[3][0], ALU.mult, ALU.add)
                    V.tensor_scalar(nh[:], nh[:], GN[3][1], None, ALU.add)
                    for jj in (2, 1):
                        V.tensor_tensor(nh[:], nh[:], xc[:], ALU.mult)
                        V.tensor_scalar(nh[:], nh[:], GN[jj][0], GN[jj][1], ALU.add, ALU.add)
                    V.tensor_tensor(nh[:], nh[:], xc[:], ALU.mult)
                    V.tensor_scalar(nh[:], nh[:], 1.0, None, ALU.add)
                    V.tensor_scalar(dh[:], xc[:], GD[5][0], GD[4][0], ALU.mult, ALU.add)
                    for jj in (3, 2, 1):
                        V.tensor_tensor(dh[:], dh[:], xc[:], ALU.mult)
                        V.tensor_scalar(dh[:], dh[:], GD[jj][0], GD[jj][1], ALU.add, ALU.add)
                    V.tensor_tensor(dh[:], dh[:], xc[:], ALU.mult)
                    V.tensor_scalar(dh[:], dh[:], 1.0, None, ALU.add)
                    V.reciprocal(xc[:], dh[:])
                    V.tensor_tensor(nh[:], nh[:], xc[:], ALU.mult)
                    V.tensor_tensor(eb[:], eb[:], nh[:], ALU.mult)
                    V.tensor_scalar(eb[:], eb[:], -0.5, 0.5, ALU.mult, ALU.add)
                    V.tensor_tensor(eb[:], eb[:], xa[:], ALU.mult)
                    V.tensor_scalar(nh[:], uV, 0.5, None, ALU.mult)
                    V.tensor_tensor(glv[:, ds(oD, 1), 0:W], nh[:], eb[:], ALU.add)
                    p.bar()
                    nc.vector.reg_add(p.rVo, p.rVo, 1)
                    nc.scalar.reg_add(p.rS, p.rS, 1)
                gs.__exit__(None, None, None)
                nc.scalar.reg_mov(p.rS, 0)
            nc.sync.dma_start(
                out=p.flats(t["gld"], p.ntok, 32),
                in_=glb.rearrange("p (k t) -> p k t", t=W)).then_inc(st, 16)
        nc.sync.wait_ge(st, 16 * ntile)


def emit_s3b2(p):
    nc, t, V, S = p.nc, p.t, p.V, p.S
    ntile = p.ntok // TT
    ld, lw, st = t["ld_s3b2"], t["lw_s3b2"], t["st_s3b2"]
    W = TT
    psA = t["psA"]
    with p.stage("s3b2"):
        fc2w = p.sb("fc2ws", [128, 32 * D], BF16)
        glb = p.sb("b2g", [128, 32 * W], BF16)
        xb1 = p.sb("b2xb1", [128, 8 * W])
        x2 = p.sb("b2x2", [128, 8 * W])
        stg = p.sb("b2stg", [128, 8 * W])
        nc.gpsimd.dma_start(out=fc2w[:], in_=t["fc2wd"][:]).then_inc(lw, 16)
        nc.tensor.wait_ge(lw, 16)
        p.bar()
        nc.gpsimd.reg_mov(p.rG, 0)
        nc.vector.reg_mov(p.rV, 48)
        with p.tloop(ntile) as i:
            nc.gpsimd.wait_ge(st, p.rG)
            nc.gpsimd.reg_add(p.rG, p.rG, 16)
            nc.gpsimd.dma_start(out=glb.rearrange("p (k t) -> p k t", t=W),
                                in_=p.flatg(t["gld"], p.ntok, 32)).then_inc(ld, 16)
            nc.gpsimd.dma_start(out=xb1.rearrange("p (k t) -> p k t", t=W),
                                in_=p.chg(t["xb1d"])).then_inc(ld, 16)
            nc.gpsimd.dma_start(out=x2.rearrange("p (k t) -> p k t", t=W),
                                in_=p.chg(t["x2d"])).then_inc(ld, 16)
            nc.vector.wait_ge(ld, p.rV)
            nc.vector.reg_add(p.rV, p.rV, 48)
            p.bar()
            for m in range(8):
                pp = m % 2
                for k in range(32):
                    p.mmN(psA, pp * 1024,
                          fc2w[:, (m * 32 + k) * 128:(m * 32 + k + 1) * 128],
                          glb[:, k * W:(k + 1) * W], W,
                          start=(k == 0), stop=(k == 31))
                p.bar()
                V.tensor_scalar(t["qtmp"][:, 0:W], psA[:, pp * 1024:pp * 1024 + W],
                                t["fc2bb"][:, m:m + 1], None, ALU.add)
                V.tensor_tensor(t["qtmp"][:, 0:W], t["qtmp"][:, 0:W],
                                xb1[:, m * W:(m + 1) * W], ALU.add)
                V.tensor_tensor(stg[:, m * W:(m + 1) * W], t["qtmp"][:, 0:W],
                                x2[:, m * W:(m + 1) * W], ALU.add)
                p.bar()
            nc.sync.dma_start(out=p.chs(t["x3d"]),
                              in_=stg.rearrange("p (k t) -> p k t", t=W)
                              ).then_inc(st, 16)
        nc.sync.wait_ge(st, 16 * ntile)


def emit_phase2(nc, t, ntok, stages=8):
    p = P2(nc, t, ntok)
    t["p2obj"] = p
    p.edges_init()
    emit_wprep(p, t["w1T"], t["w1qd"], "w1sc", 8, 32, "1")
    emit_wprep(p, t["w2T"], t["w2qd"], "w2sc", 32, 8, "2")
    if stages >= 1:
        emit_s1(p)
    if stages >= 2:
        emit_ffn1(p, "x1d", "s2a")
    if stages >= 3:
        emit_ffn2(p, "x1d", "x2d", "s2b")
    if stages >= 4:
        emit_s3a(p)
    if stages >= 5:
        emit_s3a2(p)
    if stages >= 6:
        emit_s3b1(p)
    if stages >= 7:
        emit_s3b2(p)
    if stages >= 8:
        emit_ffn1(p, "x3d", "s4a")
    if stages >= 9:
        emit_ffn2(p, "x3d", "outd", "s4b")



def emit_phase_a(nc, t, nstep):
    V, S = _wrap(nc.vector), _wrap(nc.scalar)
    ncha = nstep // CHA
    xt, wx, gd, biasb = t["xt"], t["wx"], t["gd"], t["biasb"]
    onescol, ones128 = t["onescol"], t["ones128"]
    wxb, xbuf, sqb, s1b, t8 = t["wxb"], t["xbuf"], t["sqb"], t["s1b"], t["t8"]
    trow, nmub, invsb, gstage = t["trow"], t["nmub"], t["invsb"], t["gstage"]
    psS, psB, psP = t["psS"], t["psB"], t["psP"]
    (a_wl, a_xl, a_sq, a_mm2, a_mmS, a_ew1, a_iv, a_rows, a_mmB, a_s1, a_pg,
     a_gact, a_st, a_pgc) = (
        t["a_wl"], t["a_xl"], t["a_sq"], t["a_mm2"], t["a_mmS"], t["a_ew1"],
        t["a_iv"], t["a_rows"], t["a_mmB"], t["a_s1"], t["a_pg"], t["a_gact"],
        t["a_st"], t["a_pgc"])

    rQ = nc.scalar.alloc_register("rQ")
    rTsq = nc.tensor.alloc_register("rTsq")
    rTg = nc.tensor.alloc_register("rTg")
    rVp = nc.vector.alloc_register("rVp")
    nc.scalar.reg_mov(rQ, 1)
    nc.tensor.reg_mov(rTsq, 1)
    nc.tensor.reg_mov(rTg, 1)
    nc.vector.reg_mov(rVp, 1)
    nc.vector.sem_inc(a_gact, 2)

    for k in range(8):
        nc.gpsimd.dma_start(
            out=wxb.rearrange("p (m k c) -> p m k c", k=8, c=128)[:, :, k, :],
            in_=wx[k * 128:(k + 1) * 128, :].rearrange("p (m c) -> p m c", c=128),
        ).then_inc(a_wl, 16)

    with nc.Fori(0, ncha) as c:
        nc.gpsimd.wait_ge(a_s1, 8 * c)
        nc.gpsimd.wait_ge(a_sq, 8 * c)
        nc.gpsimd.wait_ge(a_pgc, c)
        for k in range(8):
            nc.gpsimd.dma_start(
                out=xbuf[:, k * TCA:(k + 1) * TCA],
                in_=xt[k * 128:(k + 1) * 128, :][:, ds(c * TCA, TCA)],
            ).then_inc(a_xl, 16)
        nc.tensor.wait_ge(a_wl, 128)
        nc.tensor.wait_ge(a_xl, (c + 1) * 128)
        nc.tensor.wait_ge(a_rows, c)
        for k in range(8):
            nc.tensor.matmul(psS[:, 0:TCA], onescol[:], xbuf[:, k * TCA:(k + 1) * TCA],
                             start=(k == 0), stop=(k == 7))
        nc.scalar.wait_ge(a_xl, (c + 1) * 128)
        for k in range(8):
            pq = k % 2
            if k < 2:
                nc.scalar.wait_ge(a_mmS, c)
            else:
                nc.scalar.wait_ge(a_mm2, rQ)
                nc.scalar.reg_add(rQ, rQ, 1)
            S.activation(out=sqb[:, pq * TCA:(pq + 1) * TCA],
                         in_=xbuf[:, k * TCA:(k + 1) * TCA],
                         func=AF.Square).then_inc(a_sq, 1)
            nc.tensor.wait_ge(a_sq, rTsq)
            nc.tensor.reg_add(rTsq, rTsq, 1)
            inst = nc.tensor.matmul(psS[:, TCA:2 * TCA], onescol[:],
                                    sqb[:, pq * TCA:(pq + 1) * TCA],
                                    start=(k == 0), stop=(k == 7))
            inst.then_inc(a_mmS if k == 7 else a_mm2, 1)
        nc.scalar.reg_add(rQ, rQ, 1)
        MU, VAR, NMU = 0, TCA, 2 * TCA
        nc.vector.wait_ge(a_mmS, c + 1)
        V.tensor_scalar_mul(trow[:, MU:MU + TCA], psS[:, 0:TCA], 1.0 / D)
        V.tensor_scalar_mul(trow[:, VAR:VAR + TCA], psS[:, TCA:2 * TCA], 1.0 / D)
        V.tensor_scalar_mul(trow[:, NMU:NMU + TCA], trow[:, MU:MU + TCA], -1.0)
        V.tensor_copy(nmub[:], trow[:, NMU:NMU + TCA])
        V.tensor_tensor(trow[:, MU:MU + TCA], trow[:, MU:MU + TCA],
                        trow[:, MU:MU + TCA], ALU.mult)
        V.tensor_tensor(trow[:, VAR:VAR + TCA], trow[:, VAR:VAR + TCA],
                        trow[:, MU:MU + TCA], ALU.subtract).then_inc(a_ew1, 1)
        nc.scalar.wait_ge(a_ew1, c + 1)
        S.activation(out=trow[:, VAR:VAR + TCA], in_=trow[:, VAR:VAR + TCA],
                     func=AF.Ln, bias=t["epsb"][:, 0:1])
        S.activation(out=trow[:, VAR:VAR + TCA], in_=trow[:, VAR:VAR + TCA],
                     func=AF.Exp, scale=-0.5).then_inc(a_iv, 1)
        nc.vector.wait_ge(a_iv, c + 1)
        V.tensor_copy(invsb[:], trow[:, VAR:VAR + TCA]).then_inc(a_rows, 1)
        nc.tensor.wait_ge(a_rows, c + 1)
        nc.tensor.wait_ge(a_s1, 8 * c)
        nc.tensor.matmul(psB[:, 0:TCA], ones128[:], nmub[:], start=True, stop=True)
        nc.tensor.matmul(psB[:, TCA:2 * TCA], ones128[:], invsb[:],
                         start=True, stop=True).then_inc(a_mmB, 1)
        nc.vector.wait_ge(a_mmB, c + 1)
        for k in range(8):
            V.tensor_tensor(t8[:], xbuf[:, k * TCA:(k + 1) * TCA],
                            psB[:, 0:TCA], ALU.add)
            V.tensor_tensor(s1b[:, k * TCA:(k + 1) * TCA], t8[:],
                            psB[:, TCA:2 * TCA], ALU.mult).then_inc(a_s1, 1)
        nc.tensor.wait_ge(a_s1, 8 * (c + 1))
        for m in range(NMG):
            pp = m % 2
            nc.tensor.wait_ge(a_gact, rTg)
            nc.tensor.reg_add(rTg, rTg, 1)
            for k in range(8):
                inst = nc.tensor.matmul(psP[:, pp * TCA:(pp + 1) * TCA],
                                        wxb[:, (m * 8 + k) * 128:(m * 8 + k + 1) * 128],
                                        s1b[:, k * TCA:(k + 1) * TCA],
                                        start=(k == 0), stop=(k == 7))
            inst.then_inc(a_pgc if m == NMG - 1 else a_pg, 1)
            if m == NMG - 1:
                nc.vector.wait_ge(a_pgc, c + 1)
            else:
                nc.vector.wait_ge(a_pg, rVp)
                nc.vector.reg_add(rVp, rVp, 1)
            if m == 0:
                nc.vector.wait_ge(a_st, 16 * c)
            V.tensor_scalar_add(gstage[:, m * TCA:(m + 1) * TCA],
                                psP[:, pp * TCA:(pp + 1) * TCA],
                                biasb[:, m:m + 1]).then_inc(a_gact, 1)
        nc.sync.wait_ge(a_gact, 2 + (c + 1) * NMG)
        nc.sync.dma_start(
            out=gd.rearrange("p (m t) -> p m t", m=NMG)[:, :, ds(c * TCA, TCA)],
            in_=gstage.rearrange("p (m t) -> p m t", m=NMG),
        ).then_inc(a_st, 16)
    nc.sync.wait_ge(a_st, ncha * 16)



def emit_phase_b(nc, t, nstep, record_hs=True):
    V, S = _wrap(nc.vector), _wrap(nc.scalar)
    nbody = nstep // (2 * INNER)
    wh, weg, gd, begb, hsout = t["wh"], t["weg"], t["gd"], t["begb"], t["hsout"]
    whb, gbuf = t["whb"], t["gbuf"]
    hT, gT, stc, stn = t["hT"], t["gT"], t["stc"], t["stn"]
    zg, zifo, sifo, gf, ef, tmp, tmp2, rn, hf, hsb = (
        t["zg"], t["zifo"], t["sifo"], t["gf"], t["ef"], t["tmp"], t["tmp2"],
        t["rn"], t["hf"], t["hsb"])
    psG, psI, psW = t["psG"], t["psI"], t["psW"]
    (b_wl, b_gl, b_mg, b_mi, b_mw, b_eg, b_ag, b_gt, b_ei, b_ai, b_ae,
     b_ec, b_hs, b_mo, b_eo, b_ao) = (
        t["b_wl"], t["b_gl"], t["b_mg"], t["b_mi"], t["b_mw"], t["b_eg"],
        t["b_ag"], t["b_gt"], t["b_ei"], t["b_ai"], t["b_ae"], t["b_ec"],
        t["b_hs"], t["b_mo"], t["b_eo"], t["b_ao"])

    rT = nc.tensor.alloc_register("rT")
    rV = nc.vector.alloc_register("rV")
    rS = nc.scalar.alloc_register("rS")
    nc.tensor.reg_mov(rT, 1)
    nc.vector.reg_mov(rV, 1)
    nc.scalar.reg_mov(rS, 1)

    for k in range(8):
        nc.gpsimd.dma_start(
            out=whb.rearrange("p (m k c) -> p m k c", k=8, c=128)[:, 0:NMG, k, :],
            in_=wh[k * 128:(k + 1) * 128, :].rearrange("p (m c) -> p m c", c=128),
        ).then_inc(b_wl, 16)
        nc.gpsimd.dma_start(
            out=whb.rearrange("p (m k c) -> p m k c", k=8, c=128)[:, NMG:NT, k, :],
            in_=weg[k * 128:(k + 1) * 128, :].rearrange("p (m c) -> p m c", c=128),
        ).then_inc(b_wl, 16)
    nc.vector.memset(hT[:], 0.0)
    nc.vector.drain()
    nc.vector.memset(stc[:], 0.0)
    nc.vector.drain()
    nc.vector.memset(stn[:], 0.0)
    nc.vector.drain()
    nc.vector.sem_inc(b_ec, 1)
    nc.tensor.wait_ge(b_wl, 256)

    with nc.Fori(0, nbody) as cb:
        nc.gpsimd.wait_ge(b_eo, 16 * cb)
        for H in range(2):
            nc.gpsimd.dma_start(
                out=gbuf[:, H * INNER * GSC:(H + 1) * INNER * GSC].rearrange(
                    "p (m t) -> p m t", m=NMG),
                in_=gd.rearrange("p (m t) -> p m t", m=NMG)[
                    :, :, ds(cb * 2 * INNER * B + H * INNER * B, INNER * B)],
            ).then_inc(b_gl, 16)
        for j in range(2 * INNER):
            H = j // INNER
            pa = j % 2
            gt_ap = gbuf.rearrange("p (h m t) -> p h m t", h=2, m=NMG)[
                :, H, :, (j % INNER) * 8:(j % INNER + 1) * 8]
            nc.tensor.wait_ge(b_ec, rT)
            for m in range(8):
                for k in range(8):
                    inst = nc.tensor.matmul(
                        psG[:, pa * 64 + m * 8:pa * 64 + (m + 1) * 8],
                        whb[:, (m * 8 + k) * 128:(m * 8 + k + 1) * 128],
                        hT[:, k * 8:(k + 1) * 8],
                        start=(k == 0), stop=(k == 7))
            inst.then_inc(b_mg, 1)
            for m in range(8, 24):
                for k in range(8):
                    inst = nc.tensor.matmul(
                        psI[:, pa * 192 + (m - 8) * 8:pa * 192 + (m - 7) * 8],
                        whb[:, (m * 8 + k) * 128:(m * 8 + k + 1) * 128],
                        hT[:, k * 8:(k + 1) * 8],
                        start=(k == 0), stop=(k == 7))
            inst.then_inc(b_mi, 1)
            nc.vector.wait_ge(b_mg, rV)
            if j == 0:
                nc.vector.wait_ge(b_gl, (cb + 1) * 32)
            V.tensor_tensor(
                zg.rearrange("p (m t) -> p m t", m=8),
                psG[:, pa * 64:pa * 64 + 64].rearrange("p (m t) -> p m t", m=8),
                gt_ap[:, 0:8, :], ALU.add).then_inc(b_eg, 1)
            nc.scalar.wait_ge(b_eg, rS)
            S.activation(out=gf[:], in_=zg[:], func=AF.Tanh).then_inc(b_ag, 1)
            nc.vector.wait_ge(b_ag, rV)
            V.tensor_copy(gT[:], gf[:]).then_inc(b_gt, 1)
            nc.tensor.wait_ge(b_gt, rT)
            nc.tensor.reg_add(rT, rT, 1)
            for m in range(NMG, NT):
                mo = m - NMG
                for k in range(8):
                    inst = nc.tensor.matmul(
                        psW[:, pa * 64 + mo * 8:pa * 64 + (mo + 1) * 8],
                        whb[:, (m * 8 + k) * 128:(m * 8 + k + 1) * 128],
                        gT[:, k * 8:(k + 1) * 8],
                        start=(k == 0), stop=(k == 7))
            inst.then_inc(b_mw, 1)
            for m in range(24, 32):
                for k in range(8):
                    inst = nc.tensor.matmul(
                        psI[:, pa * 192 + (m - 8) * 8:pa * 192 + (m - 7) * 8],
                        whb[:, (m * 8 + k) * 128:(m * 8 + k + 1) * 128],
                        hT[:, k * 8:(k + 1) * 8],
                        start=(k == 0), stop=(k == 7))
            inst.then_inc(b_mo, 1)
            nc.vector.wait_ge(b_mi, rV)
            V.tensor_tensor(
                zifo.rearrange("p (m t) -> p m t", m=24)[:, 0:16, :],
                psI[:, pa * 192:pa * 192 + 128].rearrange("p (m t) -> p m t", m=16),
                gt_ap[:, 8:24, :], ALU.add).then_inc(b_ei, 1)
            nc.scalar.wait_ge(b_ei, rS)
            S.activation(out=sifo[:, 0:128], in_=zifo[:, 0:128],
                         func=AF.Sigmoid).then_inc(b_ai, 1)
            nc.scalar.wait_ge(b_mw, rS)
            for mo in range(NMW):
                inst = S.activation(
                    out=ef[:, mo * 8:(mo + 1) * 8],
                    in_=psW[:, pa * 64 + mo * 8:pa * 64 + (mo + 1) * 8],
                    func=AF.Exp, bias=begb[:, mo:mo + 1])
            inst.then_inc(b_ae, 1)
            SI, SF, SO = 0, 64, 128
            nc.vector.wait_ge(b_ai, rV)
            nc.vector.wait_ge(b_ae, rV)
            if record_hs and j == 0:
                nc.vector.wait_ge(b_hs, 32 * cb)
            V.tensor_tensor(stc[:], sifo[:, SF:SF + 64], stc[:], ALU.mult)
            V.tensor_tensor(tmp[:], sifo[:, SI:SI + 64], gf[:], ALU.mult)
            V.tensor_tensor(stc[:], stc[:], tmp[:], ALU.add)
            V.tensor_tensor(stn[:], sifo[:, SF:SF + 64], stn[:], ALU.mult)
            V.tensor_tensor(tmp2[:], sifo[:, SI:SI + 64], ef[:], ALU.mult)
            V.tensor_tensor(stn[:], stn[:], tmp2[:], ALU.add)
            V.reciprocal(rn[:], stn[:])
            V.tensor_tensor(tmp[:], stc[:], rn[:], ALU.mult)
            nc.vector.wait_ge(b_mo, rV)
            V.tensor_tensor(
                zifo.rearrange("p (m t) -> p m t", m=24)[:, 16:24, :],
                psI[:, pa * 192 + 128:pa * 192 + 192].rearrange(
                    "p (m t) -> p m t", m=8),
                gt_ap[:, 24:32, :], ALU.add).then_inc(b_eo, 1)
            nc.scalar.wait_ge(b_eo, rS)
            nc.scalar.reg_add(rS, rS, 1)
            S.activation(out=sifo[:, SO:SO + 64], in_=zifo[:, 128:192],
                         func=AF.Sigmoid).then_inc(b_ao, 1)
            nc.vector.wait_ge(b_ao, rV)
            nc.vector.reg_add(rV, rV, 1)
            V.tensor_tensor(hf[:], sifo[:, SO:SO + 64], tmp[:], ALU.mult)
            if record_hs:
                V.tensor_copy(
                    hsb.rearrange("p (m s) -> p m s", s=2 * INNER)[:, :, j],
                    hf.rearrange("p (m b) -> p m b", b=8)[:, :, 0])
            V.tensor_copy(hT[:], hf[:]).then_inc(b_ec, 1)
            if record_hs and (j == INNER - 1 or j == 2 * INNER - 1):
                nc.sync.wait_ge(b_ec, 16 * cb + (H + 1) * INNER + 1)
                nc.sync.dma_start(
                    out=hsout.rearrange("p (m t) -> p m t", m=8)[
                        :, :, ds(16 * cb + H * INNER, INNER)],
                    in_=hsb.rearrange("p (m s) -> p m s", s=2 * INNER)[
                        :, :, H * INNER:(H + 1) * INNER],
                ).then_inc(b_hs, 16)
    if record_hs:
        nc.sync.wait_ge(b_hs, nbody * 32)


def host_tile_lhsT(w):
    K, M = w.shape
    nk, nm = K // 128, M // 128
    out = np.empty((128, nm * nk * 128), np.float32)
    for m in range(nm):
        for k in range(nk):
            out[:, (m * nk + k) * 128:(m * nk + k + 1) * 128] = \
                w[k * 128:(k + 1) * 128, m * 128:(m + 1) * 128]
    return out


def host_tile_lhsT64(w):
    K, M = w.shape
    nk, nm = K // 64, M // 128
    out = np.empty((64, nm * nk * 128), np.float32)
    for m in range(nm):
        for kh in range(nk):
            out[:, (m * nk + kh) * 128:(m * nk + kh + 1) * 128] = \
                w[kh * 64:(kh + 1) * 64, m * 128:(m + 1) * 128]
    return out


def perp(v):
    C = v.shape[0]
    return np.ascontiguousarray(v.reshape(C // 128, 128).T)



def build_full(nstep, ntok, stages=9):
    nc = bacc_mod.Bacc("TRN2", target_bir_lowering=False)
    t = {}
    TP = ntok + 2 * PT
    def din(name, shape):
        t[name] = nc.dram_tensor(name, shape, F32, kind="ExternalInput").ap()
    def dint(name, shape, dtype=F32):
        t[name] = nc.dram_tensor(name, shape, dtype, kind="Internal").ap()
    din("xb", [D, ntok])
    dint("hd", [128, 8 * ntok])
    din("sxt", [D, nstep * B]); din("wx", [D, 4 * SL]); din("wh", [D, 4 * SL])
    din("weg", [D, SL]); din("biasT", [128, NMG]); din("begT", [128, NMW])
    dint("gd", [128, NMG * (nstep + 2 * INNER) * B])
    din("identin", [128, 128])
    din("w1T", [D, DF]); din("w2T", [DF, D])
    din("qkvwin", [128, 24 * 8 * 128]); din("projwin", [64, 8 * 16 * 128])
    din("fc1win", [128, 32 * 8 * 128]); din("fc2win", [128, 8 * 32 * 128])
    din("vecsin", [128, 8 + 8 + 8 + 8 + 8 + 8 + 24 + 8 + 32 + 8 + 96 + 32])
    t["outd"] = nc.dram_tensor("outd", [128, 8 * ntok], F32, kind="ExternalOutput").ap()
    for nmx in ("x1d", "x2d", "x3d", "xb1d"):
        dint(nmx, [128, 8 * TP])
    dint("k2d", [128, 32 * ntok], BF16)
    dint("attd", [64, 16 * ntok], BF16)
    dint("gld", [128, 32 * ntok], BF16)
    dint("oscd", [1, ntok])
    dint("w1qd", [128, 8 * DF], BF16)
    dint("w2qd", [128, 32 * D], BF16)

    ts = {"xt": t["sxt"], "wx": t["wx"], "wh": t["wh"], "weg": t["weg"],
          "biasT": t["biasT"], "begT": t["begT"], "hsout": t["hd"],
          "gd": t["gd"]}
    scan_sems = []
    def ssem(name):
        h = nc.alloc_semaphore(name="sc_" + name)
        ts[name] = h
        scan_sems.append(h)
    esta = contextlib.ExitStack()
    estb = contextlib.ExitStack()
    estc = contextlib.ExitStack()
    def ssb(est_, name, shape, dtype=F32):
        ts[name] = est_.enter_context(nc.sbuf_tensor("sc_" + name, shape, dtype))
    def spsum(est_, name, shape):
        ts[name] = est_.enter_context(nc.psum_tensor("sc_" + name, shape, F32))
    for s in ["s_init", "a_wl", "a_xl", "a_sq", "a_mm2", "a_mmS", "a_ew1",
              "a_iv", "a_rows", "a_mmB", "a_s1", "a_pg", "a_gact", "a_st",
              "a_pgc", "b_wl", "b_gl", "b_mg", "b_mi", "b_mw", "b_eg", "b_ag",
              "b_gt", "b_ei", "b_ai", "b_ae", "b_ec", "b_hs", "b_mo", "b_eo",
              "b_ao"]:
        ssem(s)
    ssb(estc, "ones128", [1, 128], BF16); ssb(estc, "onescol", [128, 1], BF16)
    ssb(estc, "epsb", [1, 1])
    ssb(estc, "biasb", [128, NMG]); ssb(estc, "begb", [128, NMW])
    ssb(esta, "wxb", [128, 8 * 4 * SL], BF16)
    ssb(esta, "xbuf", [128, 8 * TCA], BF16)
    ssb(esta, "sqb", [128, 2 * TCA], BF16)
    ssb(esta, "s1b", [128, 8 * TCA], BF16)
    ssb(esta, "t8", [128, TCA])
    ssb(esta, "trow", [1, 3 * TCA])
    ssb(esta, "nmub", [1, TCA], BF16); ssb(esta, "invsb", [1, TCA], BF16)
    ssb(esta, "gstage", [128, NMG * TCA])
    spsum(esta, "psS", [1, 2 * TCA]); spsum(esta, "psB", [128, 2 * TCA])
    spsum(esta, "psP", [128, 2 * TCA])
    Vs = _wrap(nc.vector)
    nc.vector.memset(ts["epsb"][:], LN_EPS); nc.vector.drain()
    nc.vector.memset(ts["ones128"][:], 1.0); nc.vector.drain()
    nc.vector.memset(ts["onescol"][:], 1.0); nc.vector.drain()
    nc.sync.dma_start(out=ts["biasb"][:], in_=ts["biasT"][:]).then_inc(ts["s_init"], 16)
    nc.sync.dma_start(out=ts["begb"][:], in_=ts["begT"][:]).then_inc(ts["s_init"], 16)
    nc.vector.wait_ge(ts["s_init"], 32)
    nc.all_engine_barrier()
    emit_phase_a(nc, ts, nstep)
    nc.all_engine_barrier()
    esta.close()
    ssb(estb, "whb", [128, 8 * NT * 128], BF16)
    ssb(estb, "gbuf", [128, 2 * INNER * GSC])
    ssb(estb, "hT", [128, 64], BF16); ssb(estb, "gT", [128, 64], BF16)
    ssb(estb, "stc", [128, 64]); ssb(estb, "stn", [128, 64])
    ssb(estb, "zg", [128, 64]); ssb(estb, "zifo", [128, 192])
    ssb(estb, "sifo", [128, 192])
    ssb(estb, "gf", [128, 64]); ssb(estb, "ef", [128, 64])
    ssb(estb, "tmp", [128, 64]); ssb(estb, "tmp2", [128, 64])
    ssb(estb, "rn", [128, 64]); ssb(estb, "hf", [128, 64])
    ssb(estb, "hsb", [128, 2 * INNER * 8])
    spsum(estb, "psG", [128, 2 * 64]); spsum(estb, "psI", [128, 2 * 192])
    spsum(estb, "psW", [128, 2 * 64])
    emit_phase_b(nc, ts, nstep, record_hs=True)
    nc.all_engine_barrier()
    estb.close()
    estc.close()
    nc.clear_and_free_semaphores(scan_sems)
    nc.all_engine_barrier()
    nc.all_engine_barrier()
    est = contextlib.ExitStack()
    def sb(name, shape, dtype=F32):
        t[name] = est.enter_context(nc.sbuf_tensor(name, shape, dtype))
    def psum(name, shape, dtype=F32):
        t[name] = est.enter_context(nc.psum_tensor(name, shape, dtype))
    def sem(name):
        t[name] = est.enter_context(nc.semaphore(name))
    sb("ones128", [1, 128], BF16); sb("ones64", [1, 64], BF16)
    sb("onescol", [128, 1], BF16); sb("onescf", [128, 1]); sb("ones128f", [1, 128]); sb("ones64f", [1, 64]); sb("epsb", [1, 1])
    sb("ident", [128, 128]); sb("identb", [128, 128], BF16)
    sb("sqb", [128, 4 * 768])
    sb("rowb", [1, 3 * 768])
    sb("qtmp", [128, 768]); sb("qtmp2", [128, 768])
    sb("kint", [128, 768], I32)
    sb("fold", [128, 768])
    sb("colcl", [128, 8]); sb("cl2", [128, 8]); sb("clb", [128, 16], BF16)
    sb("qrow", [1, 2 * 768])
    sb("oscb", [128, 768]); sb("sxb", [128, 768])
    sb("nmuB", [128, 768]); sb("invsB", [128, 768])
    sb("zrow", [128, 8 * PT])
    sb("gammab", [128, 8]); sb("betab", [128, 8])
    sb("ln1wb", [128, 8]); sb("ln1bb", [128, 8])
    sb("ln2wb", [128, 8]); sb("ln2bb", [128, 8])
    sb("qkvbb", [128, 24]); sb("projbb", [128, 8])
    sb("fc1bb", [128, 32]); sb("fc2bb", [128, 8])
    sb("convw", [128, 96]); sb("convb", [128, 32])
    sb("w1sc", [1, 1]); sb("w2sc", [1, 1])
    sb("w1scc", [128, 1]); sb("w2scc", [128, 1])
    sb("vecs", [128, 256])
    psum("psA", [128, 2048]); psum("psB", [128, 1536])
    psum("psD", [128, 384], BF16)
    for s in ["s_init", "ld_w1", "st_w1", "ld_w2", "st_w2",
              "ld_s1", "st_s1",
              "ld_s2a", "lw_s2a", "st_s2a", "ld_s2b", "lw_s2b", "st_s2b",
              "ld_s3a", "lw_s3a", "st_s3a", "ld_s3a2", "lw_s3a2", "st_s3a2",
              "ld_s3b1", "lw_s3b1", "st_s3b1", "ld_s3b2", "lw_s3b2", "st_s3b2",
              "ld_s4a", "lw_s4a", "st_s4a", "ld_s4b", "lw_s4b", "st_s4b"]:
        sem(s)
    t["qkvwd"] = t["qkvwin"]; t["projwd"] = t["projwin"]
    t["fc1wd"] = t["fc1win"]; t["fc2wd"] = t["fc2win"]

    si = t["s_init"]
    nc.sync.dma_start(out=t["ident"][:], in_=t["identin"][:]).then_inc(si, 16)
    nc.sync.dma_start(out=t["vecs"][:, 0:248], in_=t["vecsin"][:, 0:248]).then_inc(si, 16)
    nc.vector.wait_ge(si, 32)
    V = _wrap(nc.vector)
    V.tensor_copy(t["identb"][:], t["ident"][:])
    nc.vector.memset(t["ones128"][:], 1.0); nc.vector.drain()
    nc.vector.memset(t["ones64"][:], 1.0); nc.vector.drain()
    nc.vector.memset(t["onescol"][:], 1.0); nc.vector.drain()
    nc.vector.memset(t["onescf"][:], 1.0); nc.vector.drain()
    nc.vector.memset(t["ones128f"][:], 1.0); nc.vector.drain()
    nc.vector.memset(t["ones64f"][:], 1.0); nc.vector.drain()
    nc.vector.memset(t["epsb"][:], LN_EPS); nc.vector.drain()
    nc.vector.memset(t["zrow"][:], 0.0); nc.vector.drain()
    off = 0
    for name, w in [("gammab", 8), ("betab", 8), ("ln1wb", 8), ("ln1bb", 8),
                    ("ln2wb", 8), ("ln2bb", 8), ("qkvbb", 24), ("projbb", 8),
                    ("fc1bb", 32), ("fc2bb", 8), ("convw", 96), ("convb", 32)]:
        V.tensor_copy(t[name][:], t["vecs"][:, off:off + w])
        off += w
    nc.vector.sem_inc(si, 1)
    nc.sync.wait_ge(si, 33)
    for nmx in ("x1d", "x2d", "x3d"):
        dv = t[nmx].rearrange("p (k t) -> p k t", t=ntok + 2 * PT)
        nc.sync.dma_start(out=dv[:, :, 0:PT],
                          in_=t["zrow"].rearrange("p (k t) -> p k t", t=PT)
                          ).then_inc(si, 16)
        nc.sync.dma_start(out=dv[:, :, ntok + PT:ntok + 2 * PT],
                          in_=t["zrow"].rearrange("p (k t) -> p k t", t=PT)
                          ).then_inc(si, 16)
    nc.sync.wait_ge(si, 33 + 96)
    nc.all_engine_barrier()
    emit_phase2(nc, t, ntok, stages)
    nc.all_engine_barrier()
    est.close()
    return nc




NSTEP_FULL = 4096
_CACHE = {}


def _get_nc():
    if "nc" not in _CACHE:
        nc = build_full(NSTEP_FULL, NSTEP_FULL, 9)
        nc.compile()
        _CACHE["nc"] = nc
    return _CACHE["nc"]


def kernel(**inp):
    f32 = np.float32
    g = lambda k: np.ascontiguousarray(np.asarray(inp[k], dtype=f32))
    x = g("L0_attn_a_x")
    Wi, Wf = g("L0_attn_a_Wi"), g("L0_attn_a_Wf")
    Wg, Wo = g("L0_attn_a_Wg"), g("L0_attn_a_Wo")
    Weg = g("L0_attn_a_Weg")
    bi, bf = g("L0_attn_a_bi"), g("L0_attn_a_bf")
    bg, bo = g("L0_attn_a_bg"), g("L0_attn_a_bo")
    beg = g("L0_attn_a_beg")
    wx = np.ascontiguousarray(np.concatenate([Wg[:D], Wi[:D], Wf[:D], Wo[:D]], 1))
    wh = np.ascontiguousarray(np.concatenate([Wg[D:], Wi[D:], Wf[D:], Wo[D:]], 1))
    biasT = np.ascontiguousarray(np.concatenate([bg, bi, bf, bo]).reshape(NMG, 128).T)
    begT = perp(beg)
    Wd = {
        "gamma": g("L0_attn_a_gamma"), "beta": g("L0_attn_a_beta"),
        "ln1_w": g("L0_attn_b_ln1_w"), "ln1_b": g("L0_attn_b_ln1_b"),
        "ln2_w": g("L0_attn_b_ln2_w"), "ln2_b": g("L0_attn_b_ln2_b"),
        "qkv_w": g("L0_attn_b_qkv_w"), "qkv_b": g("L0_attn_b_qkv_b"),
        "proj_w": g("L0_attn_b_proj_w"), "proj_b": g("L0_attn_b_proj_b"),
        "fc1_w": g("L0_attn_b_fc1_w"), "fc1_b": g("L0_attn_b_fc1_b"),
        "fc2_w": g("L0_attn_b_fc2_w"), "fc2_b": g("L0_attn_b_fc2_b"),
        "w1": g("L0_ffn_w1"), "w2": g("L0_ffn_w2"),
        "conv_w": g("L0_ffn_conv_w"), "conv_b": g("L0_ffn_conv_b"),
    }
    vecs = np.concatenate([
        perp(Wd["gamma"]), perp(Wd["beta"]), perp(Wd["ln1_w"]), perp(Wd["ln1_b"]),
        perp(Wd["ln2_w"]), perp(Wd["ln2_b"]), perp(Wd["qkv_b"]), perp(Wd["proj_b"]),
        perp(Wd["fc1_b"]), perp(Wd["fc2_b"]),
        np.ascontiguousarray(Wd["conv_w"].reshape(32, 128, 3).transpose(1, 0, 2)
                             .reshape(128, 96)),
        perp(Wd["conv_b"])], axis=1).astype(np.float32)
    shared = {
        "wx": wx, "wh": wh, "weg": np.ascontiguousarray(Weg),
        "biasT": biasT, "begT": begT,
        "identin": np.eye(128, dtype=f32),
        "w1T": np.ascontiguousarray(Wd["w1"].T),
        "w2T": np.ascontiguousarray(Wd["w2"].T),
        "qkvwin": host_tile_lhsT(Wd["qkv_w"].T),
        "projwin": host_tile_lhsT64(Wd["proj_w"].T),
        "fc1win": host_tile_lhsT(Wd["fc1_w"].T),
        "fc2win": host_tile_lhsT(Wd["fc2_w"].T),
        "vecsin": vecs,
    }
    S = x.shape[1]
    in_maps = []
    for c in range(8):
        perm = [(c + j) % 8 for j in range(8)]
        m = dict(shared)
        m["sxt"] = np.ascontiguousarray(x[perm].transpose(2, 1, 0).reshape(D, S * B))
        m["xb"] = np.ascontiguousarray(x[c].T)
        in_maps.append(m)
    res = run_bass_kernel_spmd(_get_nc(), in_maps, core_ids=list(range(8)))
    out = np.empty((8, S, D), f32)
    for c in range(8):
        out[c] = (res.results[c]["outd"].reshape(128, 8, S)
                  .transpose(2, 1, 0).reshape(S, D))
    return out


# version 05
# speedup vs baseline: 1.0863x; 1.0017x over previous
import os
os.environ.setdefault("NEURON_SCRATCHPAD_PAGE_SIZE", "640")
import contextlib
import numpy as np
import ml_dtypes
import concourse.bacc as bacc_mod
import concourse.mybir as mybir
from concourse.bass import ds
from concourse.bass_utils import run_bass_kernel_spmd

F32 = mybir.dt.float32
I32 = mybir.dt.int32
BF16 = mybir.dt.bfloat16
AF = mybir.ActivationFunctionType
ALU = mybir.AluOpType

BF16 = mybir.dt.bfloat16
I32 = mybir.dt.int32
AF = mybir.ActivationFunctionType
ALU = mybir.AluOpType

TT = 512
PT = 128
D = 1024
DF = 4096
NH = 16
LN_EPS = 1e-5
HE_FFN = 64
HE_ATT = 128
WF = TT + 2 * HE_FFN
WA = TT + 2 * HE_ATT
MAGIC = 12582912.0
GN = [(1.0, 0.0),
      (0.7847558856010437, 2.4108107510301124e-08),
      (0.3021286129951477, 1.8317987127147717e-09),
      (0.060451213270425797, 3.7214936687846034e-10),
      (0.005342473741620779, 0.0)]
GD = [(1.0, 0.0),
      (1.582640528678894, -4.984901735838321e-08),
      (1.0648928880691528, -2.0261421429523807e-09),
      (0.38475489616394043, 9.13366449140085e-09),
      (0.07580475509166718, 0.0),
      (0.00669455248862505, 0.0)]

INNER = 8
B = 8
SL = 1024
NMG = 4 * SL // 128
NMW = SL // 128
NT = NMG + NMW
CHA = 64
TCA = CHA * B
GSC = NMG * 8

def _wrap(eng):
    class W:
        def __getattr__(self, name):
            fn = getattr(eng, name)
            def call(*a, **k):
                inst = fn(*a, **k)
                eng.drain()
                return inst
            return call
    return W()


class P2:
    def __init__(self, nc, t, ntok):
        self.nc = nc
        self.t = t
        self.ntok = ntok
        self.TP = ntok + 2 * PT
        self.V = _wrap(nc.vector)
        self.S = _wrap(nc.scalar)
        self.scope = None
        self.rG = nc.gpsimd.alloc_register("p2rG")
        self.rV = nc.vector.alloc_register("p2rV")
        self.rGO = nc.gpsimd.alloc_register("p2rGO")
        self.rSO = nc.sync.alloc_register("p2rSO")
        self.rT = nc.tensor.alloc_register("p2rT")
        self.rVo = nc.vector.alloc_register("p2rVo")
        self.rS = nc.scalar.alloc_register("p2rS")
        self.tDP = nc.tensor.alloc_register("p2tDP")
        self.tDPq = nc.tensor.alloc_register("p2tDPq")
        self.vPDq = nc.vector.alloc_register("p2vPDq")
        self.tAP = nc.tensor.alloc_register("p2tAP")
        self.vPD = nc.vector.alloc_register("p2vPD")
        self.vAD = nc.vector.alloc_register("p2vAD")
        self.sPA = nc.scalar.alloc_register("p2sPA")
        self.sDA = nc.scalar.alloc_register("p2sDA")
        self.rY = nc.sync.alloc_register("p2rY")
        self.rY2 = nc.sync.alloc_register("p2rY2")
        self.vSD = nc.vector.alloc_register("p2vSD")
        nc.gpsimd.reg_mov(self.rGO, 0)
        nc.sync.reg_mov(self.rSO, 0)
        self.oG = None
        self.oS = None
        self._stage_regs = []


    def bar(self):
        self.nc.all_engine_barrier()

    def edges_init(self):
        nc, t = self.nc, self.t
        for nm in ("eDP", "ePD", "ePA", "eAP", "eAD", "eDA", "eDS", "eDPq",
                   "ePDq"):
            t[nm] = nc.alloc_semaphore(name="p2" + nm)
        for eng, reg in ((nc.tensor, self.tDP), (nc.tensor, self.tAP),
                         (nc.vector, self.vPD), (nc.vector, self.vAD),
                         (nc.scalar, self.sPA), (nc.scalar, self.sDA),
                         (nc.sync, self.rY), (nc.tensor, self.tDPq),
                         (nc.vector, self.vPDq), (nc.vector, self.vSD)):
            eng.reg_mov(reg, 0)

    def ewait(self, eng, reg, sem, k=1):
        if k:
            eng.reg_add(reg, reg, k)
        eng.wait_ge(sem, reg)

    def sb(self, name, shape, dtype=F32):
        uname = f"{self.stage_name}_{name}"
        self.t[name] = self.scope.enter_context(self.nc.sbuf_tensor(uname, shape, dtype))
        return self.t[name]

    def flush_regs(self):
        for eng in (self.nc.tensor, self.nc.vector, self.nc.scalar,
                    self.nc.gpsimd, self.nc.sync):
            try:
                eng.get_value_cache().barrier()
            except Exception:
                pass

    @contextlib.contextmanager
    def tloop(self, ntile):
        nc = self.nc
        nc.gpsimd.reg_mov(self.rGO, 0)
        nc.sync.reg_mov(self.rSO, 0)
        mx = max(0, self.ntok - TT)
        with nc.Fori(0, ntile) as i:
            self.oG = nc.gpsimd.snap(self.rGO, min_val=0, max_val=mx)
            self.oS = nc.sync.snap(self.rSO, min_val=0, max_val=mx)
            for ov in (self.oG, self.oS):
                try:
                    self._stage_regs.append(ov.val)
                except Exception:
                    pass
            yield i
            nc.gpsimd.reg_add(self.rGO, self.rGO, TT)
            nc.sync.reg_add(self.rSO, self.rSO, TT)

    @contextlib.contextmanager
    def regscope(self):
        nc = self.nc
        prev_stage_regs = self._stage_regs
        self._stage_regs = []
        allocated = []
        orig = nc.alloc_registers
        orig64 = nc.alloc_registers64
        def wrap(*a, **k):
            hs = orig(*a, **k)
            try:
                for h in hs:
                    allocated.append(h)
            except Exception:
                pass
            return hs
        def wrap64(*a, **k):
            hs = orig64(*a, **k)
            try:
                for h in hs:
                    allocated.append(h.lo)
                    allocated.append(h.hi)
            except Exception:
                pass
            return hs
        nc.alloc_registers = wrap
        nc.alloc_registers64 = wrap64
        engs = (nc.tensor, nc.vector, nc.scalar, nc.gpsimd, nc.sync)
        eorig = []
        for e in engs:
            o1, o64 = e.alloc_register, e.alloc_register64
            eorig.append((e, o1, o64))
            def mk1(o1):
                def f(*a, **k):
                    h = o1(*a, **k)
                    allocated.append(h)
                    return h
                return f
            def mk64(o64):
                def f(*a, **k):
                    h = o64(*a, **k)
                    allocated.append(h.lo)
                    allocated.append(h.hi)
                    return h
                return f
            e.alloc_register = mk1(o1)
            e.alloc_register64 = mk64(o64)
        try:
            yield
        finally:
            nc.alloc_registers = orig
            nc.alloc_registers64 = orig64
            for e, o1, o64 in eorig:
                e.alloc_register = o1
                e.alloc_register64 = o64
            for h in allocated + self._stage_regs:
                try:
                    nc.free_register(h)
                except Exception:
                    pass
            self._stage_regs = prev_stage_regs
            self.flush_regs()

    @contextlib.contextmanager
    def stage(self, name):
        old = self.scope
        olds = getattr(self, "stage_name", "")
        self.stage_name = name
        self.scope = contextlib.ExitStack()
        with self.regscope():
            yield
            self.bar()
        self.scope.close()
        self.scope = old
        self.stage_name = olds

    def mmN(self, psum, off, lhsT, rhs, W, start, stop, inc=None):
        a = 0
        while a < W:
            b = min(W, ((off + a) // 512 + 1) * 512 - off)
            inst = self.nc.tensor.matmul(psum[:, off + a:off + b], lhsT,
                                         rhs[:, a:b], start=start, stop=stop)
            if inc is not None and b == W:
                inst.then_inc(inc, 1)
            a = b

    def chunk_ap(self, dram, ti, halo=0, nk=8):
        return dram.rearrange("p (k t) -> p k t", t=self.TP)[
            :, 0:nk, ds(ti * TT + PT - halo, TT + 2 * halo)]

    def chg(self, dram, halo=0, nk=8):
        return dram.rearrange("p (k t) -> p k t", t=self.TP)[
            :, 0:nk, PT - halo:][:, :, ds(self.oG, TT + 2 * halo)]

    def chs(self, dram, nk=8):
        return dram.rearrange("p (k t) -> p k t", t=self.TP)[
            :, 0:nk, PT:][:, :, ds(self.oS, TT)]

    def flatg(self, dram, t_extent, nk):
        return dram.rearrange("p (k t) -> p k t", t=t_extent)[
            :, 0:nk, :][:, :, ds(self.oG, TT)]

    def flats(self, dram, t_extent, nk):
        return dram.rearrange("p (k t) -> p k t", t=t_extent)[
            :, 0:nk, :][:, :, ds(self.oS, TT)]

    def ln_stats(self, xv, nk, W, zero_mean=False):
        nc, V, S, t = self.nc, self.V, self.S, self.t
        psB = t["psB"]
        psS = t["psA"][0:1, :]
        sqb, rowb = t["sqb"], t["rowb"]
        onescf, ones128 = t["onescf"], t["ones128"]
        for r in range(0, nk, 4):
            re = min(r + 4, nk)
            for k in range(r, re):
                S.activation(out=sqb[:, (k - r) * W:(k - r + 1) * W],
                             in_=xv[:, k, 0:W], func=AF.Square)
            self.bar()
            for k in range(r, re):
                if not zero_mean:
                    self.mmN(psS, 0, onescf[:], xv[:, k, 0:W], W,
                             start=(k == 0), stop=(k == nk - 1))
                self.mmN(psS, 1024, onescf[:], sqb[:, (k - r) * W:(k - r + 1) * W],
                         W, start=(k == 0), stop=(k == nk - 1))
            self.bar()
        MU, VAR, NMU = 0, W, 2 * W
        if not zero_mean:
            V.tensor_scalar_mul(rowb[:, MU:MU + W], psS[:, 0:W], 1.0 / D)
            V.tensor_scalar_mul(rowb[:, NMU:NMU + W], rowb[:, MU:MU + W], -1.0)
        V.tensor_scalar_mul(rowb[:, VAR:VAR + W], psS[:, 1024:1024 + W], 1.0 / D)
        if not zero_mean:
            V.tensor_tensor(rowb[:, MU:MU + W], rowb[:, MU:MU + W],
                            rowb[:, MU:MU + W], ALU.mult)
            V.tensor_tensor(rowb[:, VAR:VAR + W], rowb[:, VAR:VAR + W],
                            rowb[:, MU:MU + W], ALU.subtract)
        self.bar()
        S.activation(out=rowb[:, VAR:VAR + W], in_=rowb[:, VAR:VAR + W],
                     func=AF.Ln, bias=t["epsb"][:, 0:1])
        S.activation(out=rowb[:, VAR:VAR + W], in_=rowb[:, VAR:VAR + W],
                     func=AF.Exp, scale=-0.5)
        self.bar()
        if not zero_mean:
            self.mmN(psB, 0, t["ones128f"][:], rowb[:, NMU:NMU + W], W,
                     start=True, stop=True)
        self.mmN(psB, 768, t["ones128f"][:], rowb[:, VAR:VAR + W], W,
                 start=True, stop=True)
        self.bar()
        if not zero_mean:
            V.tensor_copy(t["nmuB"][:, 0:W], psB[:, 0:W])
        V.tensor_copy(t["invsB"][:, 0:W], psB[:, 768:768 + W])
        self.bar()

    def act_quant(self, sv, nk, W, wscale_ap):
        nc, V, S, t = self.nc, self.V, self.S, self.t
        psB = t["psB"]
        fold, ident = t["fold"], t["ident"]
        nchunk = W // 128
        self.bar()
        V.tensor_copy(fold[:, 0:W], sv[:, 0, 0:W])
        V.tensor_copy(t["qtmp"][:, 0:W], sv[:, 0, 0:W])
        for k in range(1, nk):
            V.tensor_tensor(fold[:, 0:W], fold[:, 0:W], sv[:, k, 0:W], ALU.max)
            V.tensor_tensor(t["qtmp"][:, 0:W], t["qtmp"][:, 0:W], sv[:, k, 0:W],
                            ALU.min)
        V.tensor_scalar(t["qtmp"][:, 0:W], t["qtmp"][:, 0:W], -1.0, None, ALU.mult)
        V.tensor_tensor(fold[:, 0:W], fold[:, 0:W], t["qtmp"][:, 0:W], ALU.max)
        self.bar()
        for q in range(nchunk):
            nc.tensor.transpose(psB[:, 768 + q * 128:768 + (q + 1) * 128],
                                fold[:, q * 128:(q + 1) * 128], ident[:])
        self.bar()
        for q in range(nchunk):
            V.tensor_reduce(t["colcl"][:, q:q + 1],
                            psB[:, 768 + q * 128:768 + (q + 1) * 128],
                            mybir.AxisListType.X, ALU.max)
        self.bar()
        V.tensor_scalar_max(t["colcl"][:, 0:nchunk], t["colcl"][:, 0:nchunk], 1e-5)
        self.bar()
        S.activation(out=t["cl2"][:, 0:nchunk], in_=t["colcl"][:, 0:nchunk], func=AF.Ln)
        S.activation(out=t["cl2"][:, 0:nchunk], in_=t["cl2"][:, 0:nchunk],
                     func=AF.Exp, scale=-1.0)
        self.bar()
        V.tensor_scalar_mul(t["cl2"][:, 0:nchunk], t["cl2"][:, 0:nchunk], 127.0)
        V.tensor_single_scalar(t["colcl"][:, 0:nchunk], t["colcl"][:, 0:nchunk],
                               wscale_ap, ALU.mult)
        V.tensor_scalar_mul(t["colcl"][:, 0:nchunk], t["colcl"][:, 0:nchunk],
                            1.0 / 127.0)
        self.bar()
        for q in range(nchunk):
            nc.tensor.matmul(psB[0:1, q * 128:(q + 1) * 128],
                             t["cl2"][:, q:q + 1], t["ident"][:],
                             start=True, stop=True)
            nc.tensor.matmul(psB[0:1, 768 + q * 128:768 + (q + 1) * 128],
                             t["colcl"][:, q:q + 1], t["ident"][:],
                             start=True, stop=True)
        self.bar()
        V.tensor_copy(t["qrow"][:, 0:W], psB[0:1, 0:W])
        V.tensor_copy(t["qrow"][:, 768:768 + W], psB[0:1, 768:768 + W])
        self.bar()
        self.mmN(psB, 0, t["ones128f"][:], t["qrow"][:, 0:W], W, start=True, stop=True)
        self.mmN(psB, 768, t["ones128f"][:], t["qrow"][:, 768:768 + W], W,
                 start=True, stop=True)
        self.bar()
        V.tensor_copy(t["sxb"][:, 0:W], psB[:, 0:W])
        V.tensor_copy(t["oscb"][:, 0:W], psB[:, 768:768 + W])
        self.bar()

    def rint_hop(self, W):
        V, t = self.V, self.t
        kint = t["kint"]
        V.tensor_copy(kint[:, 0:W], t["qtmp"][:, 0:W])
        V.tensor_copy(t["qtmp2"][:, 0:W], kint[:, 0:W])
        V.tensor_tensor(t["qtmp"][:, 0:W], t["qtmp"][:, 0:W],
                        t["qtmp2"][:, 0:W], ALU.subtract)
        V.tensor_scalar(t["qtmp"][:, 0:W], t["qtmp"][:, 0:W], 2.0, None, ALU.mult)
        V.tensor_copy(kint[:, 0:W], t["qtmp"][:, 0:W])
        V.tensor_copy(t["qtmp"][:, 0:W], kint[:, 0:W])
        V.tensor_tensor(t["qtmp2"][:, 0:W], t["qtmp2"][:, 0:W],
                        t["qtmp"][:, 0:W], ALU.add)

    def quantize(self, sv, kt, nk, W):
        V, t = self.V, self.t
        for k in range(nk):
            V.tensor_tensor(t["qtmp"][:, 0:W], sv[:, k, 0:W], t["sxb"][:, 0:W],
                            ALU.mult)
            V.tensor_scalar(t["qtmp"][:, 0:W], t["qtmp"][:, 0:W], 127.0, -127.0,
                            ALU.min, ALU.max)
            V.tensor_scalar(t["qtmp"][:, 0:W], t["qtmp"][:, 0:W], MAGIC, -MAGIC,
                            ALU.add, ALU.add)
            self.rint_hop(W)
            V.tensor_copy(kt[:, k * W:(k + 1) * W], t["qtmp2"][:, 0:W])
        self.bar()

    def quantize2(self, sv, ktv, nk, W):
        V, t = self.V, self.t
        for k in range(nk):
            V.tensor_tensor(t["qtmp"][:, 0:W], sv[:, k, 0:W], t["sxb"][:, 0:W],
                            ALU.mult)
            V.tensor_scalar(t["qtmp"][:, 0:W], t["qtmp"][:, 0:W], 127.0, -127.0,
                            ALU.min, ALU.max)
            V.tensor_scalar(t["qtmp"][:, 0:W], t["qtmp"][:, 0:W], MAGIC, -MAGIC,
                            ALU.add, ALU.add)
            self.rint_hop(W)
            V.tensor_copy(ktv[:, k, 0:W], t["qtmp2"][:, 0:W])
        self.bar()


def emit_wprep(p, wsrc, wqd, sc_key, nk, nm, tag):
    nc, t, V, S = p.nc, p.t, p.V, p.S
    ld, st = t["ld_w" + tag], t["st_w" + tag]
    M = nm * 128
    with p.stage("wprep" + tag):
        wtmp = p.sb("wtmp", [128, M])
        wtmp2 = p.sb("wtmp2", [128, M])
        wint = p.sb("wint", [128, M], I32)
        wbf = p.sb("wbf", [128, M], BF16)
        acc = p.sb("wacc", [128, 2])
        sc2 = p.sb("wsc2", [1, 2])
        swcol = p.sb("swcol", [128, 1])
        nc.vector.memset(acc[:, 0:1], 0.0)
        nc.vector.drain()
        p.bar()
        for k in range(nk):
            nc.gpsimd.dma_start(out=wtmp[:], in_=wsrc[k * 128:(k + 1) * 128, :]
                                ).then_inc(ld, 16)
            nc.vector.wait_ge(ld, 16 * (k + 1))
            p.bar()
            V.tensor_reduce(acc[:, 1:2], wtmp[:], mybir.AxisListType.X,
                            ALU.add, apply_absolute_value=True)
            V.tensor_tensor(acc[:, 0:1], acc[:, 0:1], acc[:, 1:2], ALU.add)
            p.bar()
        p.bar()
        nc.tensor.matmul(t["psB"][0:1, 0:1], acc[:, 0:1], t["onescf"][:],
                         start=True, stop=True)
        p.bar()
        V.tensor_scalar_mul(sc2[:, 0:1], t["psB"][0:1, 0:1], 1.0 / (nk * 128 * M))
        V.tensor_scalar_max(sc2[:, 0:1], sc2[:, 0:1], 1e-5)
        V.tensor_copy(t[sc_key][0:1, 0:1], sc2[:, 0:1])
        p.bar()
        S.activation(out=sc2[:, 1:2], in_=sc2[:, 0:1], func=AF.Ln)
        S.activation(out=sc2[:, 1:2], in_=sc2[:, 1:2], func=AF.Exp, scale=-1.0)
        p.bar()
        nc.tensor.matmul(t["psB"][:, 0:1], t["ones128f"][:], sc2[:, 1:2],
                         start=True, stop=True)
        nc.tensor.matmul(t["psB"][:, 512:513], t["ones128f"][:], sc2[:, 0:1],
                         start=True, stop=True)
        p.bar()
        V.tensor_copy(swcol[:], t["psB"][:, 0:1])
        V.tensor_copy(t[sc_key + "c"][:, 0:1], t["psB"][:, 512:513])
        p.bar()
        for k in range(nk):
            nc.gpsimd.dma_start(out=wtmp[:], in_=wsrc[k * 128:(k + 1) * 128, :]
                                ).then_inc(ld, 16)
            nc.vector.wait_ge(ld, 16 * (nk + k + 1))
            p.bar()
            V.tensor_single_scalar(wtmp[:], wtmp[:], swcol[:, 0:1], ALU.mult)
            V.tensor_scalar(wtmp[:], wtmp[:], 1.0, -1.0, ALU.min, ALU.max)
            V.tensor_scalar(wtmp[:], wtmp[:], MAGIC, -MAGIC, ALU.add, ALU.add)
            V.tensor_copy(wint[:], wtmp[:])
            V.tensor_copy(wtmp2[:], wint[:])
            V.tensor_tensor(wtmp[:], wtmp[:], wtmp2[:], ALU.subtract)
            V.tensor_scalar(wtmp[:], wtmp[:], 2.0, None, ALU.mult)
            V.tensor_copy(wint[:], wtmp[:])
            V.tensor_copy(wtmp[:], wint[:])
            V.tensor_tensor(wtmp2[:], wtmp2[:], wtmp[:], ALU.add)
            V.tensor_copy(wbf[:], wtmp2[:])
            p.bar()
            nc.sync.dma_start(
                out=wqd.rearrange("p (m k c) -> p m k c", k=nk, c=128)[:, :, k, :],
                in_=wbf.rearrange("p (m c) -> p m c", c=128)).then_inc(st, 16)
            nc.sync.wait_ge(st, 16 * (k + 1))
        p.bar()


def emit_s1(p):
    nc, t, V, S = p.nc, p.t, p.V, p.S
    ntile = p.ntok // TT
    ld, st = t["ld_s1"], t["st_s1"]
    psB = t["psB"]
    W = TT
    BCB = 768
    with p.stage("s1"):
        xt = p.sb("s1x", [128, 8 * W])
        ht = p.sb("s1h", [128, 8 * W])
        yt = p.sb("s1y", [128, 8 * W])
        stg = p.sb("s1stg", [128, 8 * W])
        nc.gpsimd.reg_mov(p.rG, 0)
        nc.vector.reg_mov(p.rV, 32)
        with p.tloop(ntile) as i:
            nc.gpsimd.wait_ge(st, p.rG)
            nc.gpsimd.reg_add(p.rG, p.rG, 16)
            nc.gpsimd.dma_start(out=xt.rearrange("p (k t) -> p k t", t=W),
                                in_=t["xb"].rearrange("(k p) t -> p k t", p=128)[
                                    :, :, ds(p.oG, TT)]).then_inc(ld, 16)
            nc.gpsimd.dma_start(out=ht.rearrange("p (k t) -> p k t", t=W),
                                in_=p.flatg(t["hd"], p.ntok, 8)).then_inc(ld, 16)
            nc.vector.wait_ge(ld, p.rV)
            nc.vector.reg_add(p.rV, p.rV, 32)
            p.bar()
            xv = xt.rearrange("p (k t) -> p k t", t=W)
            yv = yt.rearrange("p (k t) -> p k t", t=W)
            p.ln_stats(xv, 8, W)
            for k in range(8):
                V.tensor_tensor(t["qtmp"][:, 0:W], xv[:, k, :], t["nmuB"][:, 0:W], ALU.add)
                V.tensor_tensor(t["qtmp"][:, 0:W], t["qtmp"][:, 0:W],
                                t["invsB"][:, 0:W], ALU.mult)
                V.tensor_tensor(yv[:, k, :], t["qtmp"][:, 0:W],
                                ht[:, k * W:(k + 1) * W], ALU.add)
            p.bar()
            p.ln_stats(yv, 8, W)
            for k in range(8):
                V.tensor_tensor(t["qtmp"][:, 0:W], yv[:, k, :], t["nmuB"][:, 0:W], ALU.add)
                V.tensor_tensor(t["qtmp"][:, 0:W], t["qtmp"][:, 0:W],
                                t["invsB"][:, 0:W], ALU.mult)
                V.tensor_scalar(t["qtmp"][:, 0:W], t["qtmp"][:, 0:W],
                                t["gammab"][:, k:k + 1], t["betab"][:, k:k + 1],
                                ALU.mult, ALU.add)
                V.tensor_tensor(stg[:, k * W:(k + 1) * W], xv[:, k, :],
                                t["qtmp"][:, 0:W], ALU.add)
            p.bar()
            nc.sync.dma_start(out=p.chs(t["x1d"]),
                              in_=stg.rearrange("p (k t) -> p k t", t=W)
                              ).then_inc(st, 16)
        nc.sync.wait_ge(st, 16 * ntile)


def emit_ffn1(p, ind_key, tag):
    nc, t, V, S = p.nc, p.t, p.V, p.S
    ntile = p.ntok // TT
    ld, lw, st = t["ld_" + tag], t["lw_" + tag], t["st_" + tag]
    W = WF
    WC = TT
    psA, psB = t["psA"], t["psB"]
    BCB = 768
    with p.stage(tag):
        w1q = p.sb("w1qs", [128, 8 * DF], BF16)
        xt = p.sb("f1x", [128, 8 * W])
        s2t = xt
        k1 = p.sb("f1k", [128, 8 * W], BF16)
        hmid = p.sb("f1h", [128, 32 * W], BF16)
        hmv = hmid.rearrange("p (m t) -> p m t", t=W)
        oscrow = p.sb("f1osc", [1, WC])
        nc.gpsimd.dma_start(out=w1q[:], in_=t["w1qd"][:]).then_inc(lw, 16)
        nc.tensor.wait_ge(lw, 16)
        p.bar()
        nc.gpsimd.reg_mov(p.rG, 0)
        nc.vector.reg_mov(p.rV, 16)
        with p.tloop(ntile) as i:
            nc.gpsimd.wait_ge(st, p.rG)
            nc.gpsimd.reg_add(p.rG, p.rG, 32)
            nc.gpsimd.dma_start(out=xt.rearrange("p (k t) -> p k t", t=W),
                                in_=p.chg(t[ind_key], halo=HE_FFN)).then_inc(ld, 16)
            nc.vector.wait_ge(ld, p.rV)
            nc.vector.reg_add(p.rV, p.rV, 16)
            p.bar()
            xv = xt.rearrange("p (k t) -> p k t", t=W)
            sv = xv
            p.ln_stats(xv, 8, W)
            for k in range(8):
                V.tensor_tensor(t["qtmp"][:, 0:W], xv[:, k, :], t["nmuB"][:, 0:W], ALU.add)
                V.tensor_tensor(sv[:, k, :], t["qtmp"][:, 0:W],
                                t["invsB"][:, 0:W], ALU.mult)
            p.bar()
            p.act_quant(sv, 8, W, t["w1scc"][:, 0:1])
            p.quantize(sv, k1, 8, W)
            for m in range(32):
                pp = m % 2
                if pp == 0 and m > 0:
                    p.ewait(nc.tensor, p.tDPq, t["eDPq"])
                for k in range(8):
                    p.mmN(psA, pp * 1024,
                          w1q[:, (m * 8 + k) * 128:(m * 8 + k + 1) * 128],
                          k1[:, k * W:(k + 1) * W], W,
                          start=(k == 0), stop=(k == 7),
                          inc=(t["ePDq"] if (k == 7 and pp == 1) else None))
                if pp == 1:
                    p.ewait(nc.vector, p.vPDq, t["ePDq"])
                    for mm in (m - 1, m):
                        inst = V.tensor_tensor(
                            hmid[:, mm * W:(mm + 1) * W],
                            psA[:, (mm % 2) * 1024:(mm % 2) * 1024 + W],
                            t["oscb"][:, 0:W], ALU.mult)
                    if m < 31:
                        inst.then_inc(t["eDPq"], 1)
            p.bar()
            CV = W - 2
            sqb = t["sqb"]
            for m0 in range(0, 32, 4):
                for m in range(m0, m0 + 4):
                    sl = sqb[:, (m % 4) * 768:(m % 4) * 768 + CV]
                    V.tensor_scalar(sl, hmid[:, m * W:m * W + CV],
                                    t["convw"][:, 3 * m:3 * m + 1],
                                    t["convb"][:, m:m + 1], ALU.mult, ALU.add)
                    V.scalar_tensor_tensor(sl, hmid[:, m * W + 1:m * W + 1 + CV],
                                           t["convw"][:, 3 * m + 1:3 * m + 2],
                                           sl, ALU.mult, ALU.add)
                    V.scalar_tensor_tensor(sl, hmid[:, m * W + 2:m * W + 2 + CV],
                                           t["convw"][:, 3 * m + 2:3 * m + 3],
                                           sl, ALU.mult, ALU.add)
                p.bar()
                for m in range(m0, m0 + 4):
                    S.activation(out=hmv[:, m, 0:WC],
                                 in_=sqb[:, (m % 4) * 768 + HE_FFN - 1:
                                         (m % 4) * 768 + HE_FFN - 1 + WC],
                                 func=AF.Silu)
                p.bar()
            p.act_quant(hmv[:, :, 0:WC], 32, WC, t["w2scc"][:, 0:1])
            p.quantize2(hmv[:, :, 0:WC], hmv[:, :, 0:WC], 32, WC)
            V.tensor_copy(oscrow[:, 0:WC], t["qrow"][:, 768:768 + WC])
            p.bar()
            nc.sync.dma_start(
                out=p.flats(t["k2d"], p.ntok, 32),
                in_=hmv[:, :, 0:WC]).then_inc(st, 16)
            nc.sync.dma_start(
                out=t["oscd"][:, ds(p.oS, WC)], in_=oscrow[:, 0:WC]
                ).then_inc(st, 16)
        nc.sync.wait_ge(st, 32 * ntile)


def emit_ffn2(p, ind_key, outd_key, tag):
    nc, t, V, S = p.nc, p.t, p.V, p.S
    ntile = p.ntok // TT
    ld, lw, st = t["ld_" + tag], t["lw_" + tag], t["st_" + tag]
    WC = TT
    psA, psB = t["psA"], t["psB"]
    with p.stage(tag):
        w2q = p.sb("w2qs", [128, 32 * D], BF16)
        k2 = p.sb("f2k", [128, 32 * WC], BF16)
        xc = p.sb("f2x", [128, 8 * WC])
        stg = p.sb("f2stg", [128, 8 * WC])
        oscrow = p.sb("f2osc", [1, WC])
        nc.gpsimd.dma_start(out=w2q[:], in_=t["w2qd"][:]).then_inc(lw, 16)
        nc.tensor.wait_ge(lw, 16)
        p.bar()
        nc.gpsimd.reg_mov(p.rG, 0)
        nc.vector.reg_mov(p.rV, 48)
        with p.tloop(ntile) as i:
            nc.gpsimd.wait_ge(st, p.rG)
            nc.gpsimd.reg_add(p.rG, p.rG, 16)
            nc.gpsimd.dma_start(out=k2.rearrange("p (k t) -> p k t", t=WC),
                                in_=p.flatg(t["k2d"], p.ntok, 32)).then_inc(ld, 16)
            nc.gpsimd.dma_start(out=xc.rearrange("p (k t) -> p k t", t=WC),
                                in_=p.chg(t[ind_key])).then_inc(ld, 16)
            nc.gpsimd.dma_start(out=oscrow[:, 0:WC],
                                in_=t["oscd"][:, ds(p.oG, WC)]).then_inc(ld, 16)
            nc.vector.wait_ge(ld, p.rV)
            nc.vector.reg_add(p.rV, p.rV, 48)
            p.bar()
            p.mmN(psB, 0, t["ones128f"][:], oscrow[:, 0:WC], WC, start=True, stop=True)
            p.bar()
            V.tensor_copy(t["oscb"][:, 0:WC], psB[:, 0:WC])
            p.bar()
            for m in range(8):
                pp = m % 2
                for k in range(32):
                    p.mmN(psA, pp * 1024,
                          w2q[:, (m * 32 + k) * 128:(m * 32 + k + 1) * 128],
                          k2[:, k * WC:(k + 1) * WC], WC,
                          start=(k == 0), stop=(k == 31))
                p.bar()
                V.tensor_tensor(t["qtmp"][:, 0:WC], psA[:, pp * 1024:pp * 1024 + WC],
                                t["oscb"][:, 0:WC], ALU.mult)
                V.tensor_tensor(stg[:, m * WC:(m + 1) * WC], t["qtmp"][:, 0:WC],
                                xc[:, m * WC:(m + 1) * WC], ALU.add)
                p.bar()
            if outd_key == "outd":
                nc.sync.dma_start(out=p.flats(t[outd_key], p.ntok, 8),
                                  in_=stg.rearrange("p (k t) -> p k t", t=WC)
                                  ).then_inc(st, 16)
            else:
                nc.sync.dma_start(out=p.chs(t[outd_key]),
                                  in_=stg.rearrange("p (k t) -> p k t", t=WC)
                                  ).then_inc(st, 16)
        nc.sync.wait_ge(st, 16 * ntile)


def emit_s3a(p):
    nc, t, V, S = p.nc, p.t, p.V, p.S
    ntile = p.ntok // TT
    ld, lw, st = t["ld_s3a"], t["lw_s3a"], t["st_s3a"]
    W = WA
    WC = TT
    psA, psB, psD = t["psA"], t["psB"], t["psD"]
    eDP, ePD, ePA, eAP = t["eDP"], t["ePD"], t["ePA"], t["eAP"]
    eAD, eDA, eDS = t["eAD"], t["eDA"], t["eDS"]
    eDPq, ePDq = t["eDPq"], t["ePDq"]
    with p.stage("s3a"):
        qkvw = p.sb("qkvws", [128, 8 * 3 * D], BF16)
        xt = p.sb("a_x", [128, 8 * W])
        xnt = p.sb("a_xn", [128, 8 * W], BF16)
        qkvT = p.sb("a_qkv", [128, 24 * W], BF16)
        kst = p.sb("a_kst", [128, W], BF16)
        vst = p.sb("a_vst", [128, W], BF16)
        qst = p.sb("a_qst", [128, 512], BF16)
        vT = p.sb("a_vT", [128, 768], BF16)
        expS = p.sb("a_exp", [128, 4 * 384], BF16)
        rsA = p.sb("a_rsA", [1, 512])
        rsB = p.sb("a_rsB", [1, 512])
        rn = p.sb("a_rn", [64, 512])
        attnT = p.sb("a_at", [64, 2 * WC], BF16)
        nc.gpsimd.dma_start(out=qkvw[:], in_=t["qkvwd"][:]).then_inc(lw, 16)
        nc.tensor.wait_ge(lw, 16)
        p.bar()
        nc.gpsimd.reg_mov(p.rG, 0)
        nc.vector.reg_mov(p.rV, 16)
        nc.vector.sem_inc(eDA, 1)
        nc.tensor.sem_inc(ePA, 1)
        with p.tloop(ntile) as i:
            nc.gpsimd.wait_ge(st, p.rG)
            nc.gpsimd.reg_add(p.rG, p.rG, 128)
            nc.sync.reg_mov(p.rY2, 0)
            nc.gpsimd.dma_start(out=xt.rearrange("p (k t) -> p k t", t=W),
                                in_=p.chg(t["x2d"], halo=HE_ATT)).then_inc(ld, 16)
            nc.vector.wait_ge(ld, p.rV)
            nc.vector.reg_add(p.rV, p.rV, 16)
            p.bar()
            xv = xt.rearrange("p (k t) -> p k t", t=W)
            p.ln_stats(xv, 8, W)
            for k in range(8):
                V.tensor_tensor(t["qtmp"][:, 0:W], xv[:, k, :], t["nmuB"][:, 0:W], ALU.add)
                V.tensor_tensor(xv[:, k, :], t["qtmp"][:, 0:W],
                                t["invsB"][:, 0:W], ALU.mult)
            p.bar()
            p.ln_stats(xv, 8, W, zero_mean=True)
            for k in range(8):
                V.tensor_tensor(t["qtmp"][:, 0:W], xv[:, k, :],
                                t["invsB"][:, 0:W], ALU.mult)
                V.tensor_scalar(xnt[:, k * W:(k + 1) * W], t["qtmp"][:, 0:W],
                                t["ln1wb"][:, k:k + 1], t["ln1bb"][:, k:k + 1],
                                ALU.mult, ALU.add)
            p.bar()
            for m in range(24):
                pp = m % 2
                if pp == 0 and m > 0:
                    p.ewait(nc.tensor, p.tDPq, eDPq)
                for k in range(8):
                    p.mmN(psA, pp * 1024,
                          qkvw[:, (m * 8 + k) * 128:(m * 8 + k + 1) * 128],
                          xnt[:, k * W:(k + 1) * W], W,
                          start=(k == 0), stop=(k == 7),
                          inc=(ePDq if (k == 7 and pp == 1) else None))
                if pp == 1:
                    p.ewait(nc.vector, p.vPDq, ePDq)
                    for mm2 in (m - 1, m):
                        sc = 0.125 if mm2 < 8 else 1.0
                        inst = V.tensor_scalar(
                            qkvT[:, mm2 * W:(mm2 + 1) * W],
                            psA[:, (mm2 % 2) * 1024:(mm2 % 2) * 1024 + W],
                            t["qkvbb"][:, mm2:mm2 + 1], sc, ALU.add, ALU.mult)
                    if m < 23:
                        inst.then_inc(eDPq, 1)
            nc.vector.reg_mov(p.rVo, 0)
            expv = expS.rearrange("p (q c) -> p q c", c=384)
            psAs = psA.rearrange("p (q c) -> p q c", c=512)
            with nc.Fori(0, NH // 2) as hp:
                oV = nc.vector.snap(p.rVo, min_val=0, max_val=7 * W)
                oY2 = nc.sync.snap(p.rY2, min_val=0, max_val=14)
                p._stage_regs += [oV.val, oY2.val]
                p.ewait(nc.vector, p.vPD, ePD, k=0)
                V.tensor_copy(kst[:], qkvT[:, ds(oV + 8 * W, W)])
                V.tensor_copy(vst[:], qkvT[:, ds(oV + 16 * W, W)])
                inst = V.tensor_copy(qst[:], qkvT[:, ds(oV + 128, 512)])
                inst.then_inc(eDP, 1)
                for b6 in (0, 1):
                    base = b6 * 64
                    p.ewait(nc.tensor, p.tDP, eDP, k=1)
                    for b in range(6):
                        nc.tensor.transpose(
                            psD[:, b * 64:(b + 1) * 64],
                            vst[base:base + 64, b * 128:(b + 1) * 128],
                            t["identb"][base:base + 64, base:base + 64])
                    nc.tensor.drain()
                    nc.tensor.sem_inc(ePD, 1)
                    p.ewait(nc.vector, p.vPD, ePD, k=1)
                    inst = V.tensor_copy(vT[:, b6 * 384:(b6 + 1) * 384],
                                         psD[:, 0:384])
                    if b6 == 0:
                        inst.then_inc(eDP, 1)
                for b6 in (0, 1):
                    base = b6 * 64
                    rs = rsA if b6 == 0 else rsB
                    p.ewait(nc.tensor, p.tAP, eAP, k=0)
                    for qb in range(4):
                        for b in range(3):
                            inst = nc.tensor.matmul(
                                psA[:, qb * 512 + b * 128:qb * 512 + (b + 1) * 128],
                                kst[base:base + 64, (qb + b) * 128:(qb + b + 1) * 128],
                                qst[base:base + 64, qb * 128:(qb + 1) * 128],
                                start=True, stop=True)
                    inst.then_inc(ePA, 1)
                    p.ewait(nc.scalar, p.sPA, ePA, k=2)
                    p.ewait(nc.scalar, p.sDA, eDA)
                    inst = S.activation(out=expv[:, 0:4, 0:384], in_=psAs[:, 0:4, 0:384],
                                        func=AF.Exp)
                    inst.then_inc(eAD, 1)
                    nc.scalar.sem_inc(eAP, 1)
                    p.ewait(nc.vector, p.vAD, eAD)
                    with nc.vector.If(i == 0):
                        nc.vector.memset(expS[:, 0:128], 0.0)
                        nc.vector.drain()
                    with nc.vector.If(i == ntile - 1):
                        nc.vector.memset(expS[:, 3 * 384 + 256:4 * 384], 0.0)
                        nc.vector.drain()
                    nc.vector.sem_inc(eDP, 1)
                    nc.vector.sem_inc(eDA, 1)
                    p.ewait(nc.tensor, p.tDP, eDP)
                    p.ewait(nc.tensor, p.tAP, eAP, k=0)
                    for qb in range(4):
                        for b in range(3):
                            inst = nc.tensor.matmul(
                                psA[0:1, qb * 512 + 384:qb * 512 + 512],
                                t["onescol"][:],
                                expS[:, qb * 384 + b * 128:qb * 384 + (b + 1) * 128],
                                start=(b == 0), stop=(b == 2))
                    inst.then_inc(ePA, 1)
                    for qb in range(4):
                        for b in range(3):
                            inst = nc.tensor.matmul(
                                psB[0:64, qb * 128:(qb + 1) * 128],
                                vT[:, (b6 * 6 + qb + b) * 64:(b6 * 6 + qb + b + 1) * 64],
                                expS[:, qb * 384 + b * 128:qb * 384 + (b + 1) * 128],
                                start=(b == 0), stop=(b == 2))
                    inst.then_inc(ePA, 1)
                    p.ewait(nc.scalar, p.sPA, ePA)
                    S.activation(out=rs.rearrange("o (q c) -> o q c", c=128),
                                 in_=psAs[0:1, 0:4, 384:512], func=AF.Ln)
                    inst = S.activation(out=rs[:, 0:512], in_=rs[:, 0:512],
                                        func=AF.Exp, scale=-1.0)
                    inst.then_inc(eAP, 1)
                    nc.scalar.sem_inc(eAP, 1)
                    p.ewait(nc.tensor, p.tAP, eAP, k=3)
                    inst = nc.tensor.matmul(psB[0:64, 512:1024], t["ones64f"][:],
                                            rs[:, 0:512], start=True, stop=True)
                    inst.then_inc(ePD, 1)
                    p.ewait(nc.vector, p.vPD, ePD)
                    V.tensor_copy(rn[:, 0:512], psB[0:64, 512:1024])
                    if b6 == 0:
                        nc.vector.wait_ge(st, p.vSD)
                        nc.vector.reg_add(p.vSD, p.vSD, 16)
                    inst = V.tensor_tensor(attnT[0:64, b6 * WC:(b6 + 1) * WC],
                                           psB[0:64, 0:512], rn[:, 0:512], ALU.mult)
                    if b6 == 1:
                        nc.vector.sem_inc(eDS, 1)
                p.ewait(nc.sync, p.rY, eDS)
                nc.sync.dma_start(
                    out=t["attd"].rearrange("p (k t) -> p k t", t=p.ntok)[
                        :, ds(oY2, 2), ds(p.oS, TT)],
                    in_=attnT.rearrange("p (c t) -> p c t", t=WC)
                    ).then_inc(st, 16)
                nc.vector.reg_add(p.rVo, p.rVo, W)
                nc.sync.reg_add(p.rY2, p.rY2, 2)
        nc.sync.wait_ge(st, 128 * ntile)


def emit_s3a2(p):
    nc, t, V, S = p.nc, p.t, p.V, p.S
    ntile = p.ntok // TT
    ld, lw, st = t["ld_s3a2"], t["lw_s3a2"], t["st_s3a2"]
    WC = TT
    psA = t["psA"]
    with p.stage("s3a2"):
        projw = p.sb("projws", [64, 16 * D], BF16)
        attnT = p.sb("p_at", [64, 16 * WC], BF16)
        x2c = p.sb("p_x2", [128, 8 * WC])
        s3c = p.sb("p_s3", [128, 8 * WC])
        stg = p.sb("p_stg", [128, 8 * WC])
        nc.gpsimd.dma_start(out=projw[:], in_=t["projwd"][:]).then_inc(lw, 16)
        nc.tensor.wait_ge(lw, 16)
        p.bar()
        nc.gpsimd.reg_mov(p.rG, 0)
        nc.vector.reg_mov(p.rV, 32)
        with p.tloop(ntile) as i:
            nc.gpsimd.wait_ge(st, p.rG)
            nc.gpsimd.reg_add(p.rG, p.rG, 16)
            nc.gpsimd.dma_start(out=attnT.rearrange("p (c t) -> p c t", t=WC),
                                in_=p.flatg(t["attd"], p.ntok, 16)).then_inc(ld, 16)
            nc.gpsimd.dma_start(out=x2c.rearrange("p (k t) -> p k t", t=WC),
                                in_=p.chg(t["x2d"])).then_inc(ld, 16)
            nc.vector.wait_ge(ld, p.rV)
            nc.vector.reg_add(p.rV, p.rV, 32)
            p.bar()
            x2v = x2c.rearrange("p (k t) -> p k t", t=WC)
            p.ln_stats(x2v, 8, WC)
            for k in range(8):
                V.tensor_tensor(t["qtmp"][:, 0:WC], x2v[:, k, :],
                                t["nmuB"][:, 0:WC], ALU.add)
                V.tensor_tensor(s3c[:, k * WC:(k + 1) * WC], t["qtmp"][:, 0:WC],
                                t["invsB"][:, 0:WC], ALU.mult)
            p.bar()
            for m in range(8):
                pp = m % 2
                for kh in range(16):
                    p.mmN(psA, pp * 1024,
                          projw[:, (m * 16 + kh) * 128:(m * 16 + kh + 1) * 128],
                          attnT[:, kh * WC:(kh + 1) * WC], WC,
                          start=(kh == 0), stop=(kh == 15))
                p.bar()
                V.tensor_scalar(t["qtmp"][:, 0:WC], psA[:, pp * 1024:pp * 1024 + WC],
                                t["projbb"][:, m:m + 1], None, ALU.add)
                V.tensor_tensor(stg[:, m * WC:(m + 1) * WC], t["qtmp"][:, 0:WC],
                                s3c[:, m * WC:(m + 1) * WC], ALU.add)
                p.bar()
            nc.sync.dma_start(out=p.chs(t["xb1d"]),
                              in_=stg.rearrange("p (k t) -> p k t", t=WC)
                              ).then_inc(st, 16)
        nc.sync.wait_ge(st, 16 * ntile)


def emit_s3b1(p):
    nc, t, V, S = p.nc, p.t, p.V, p.S
    ntile = p.ntok // TT
    ld, lw, st = t["ld_s3b1"], t["lw_s3b1"], t["st_s3b1"]
    W = TT
    psA, psB = t["psA"], t["psB"]
    BCB = 768
    with p.stage("s3b1"):
        fc1w = p.sb("fc1ws", [128, 8 * DF], BF16)
        xt = p.sb("b1x", [128, 8 * W])
        y2t = p.sb("b1y", [128, 8 * W], BF16)
        glb = p.sb("b1g", [128, 32 * W], BF16)
        u8 = p.sb("b1u8", [128, 8 * W])
        xa = p.sb("b1xa", [128, W])
        xc = p.sb("b1xc", [128, W])
        nh = p.sb("b1nh", [128, W])
        dh = p.sb("b1dh", [128, W])
        eb = p.sb("b1e", [128, W])
        nc.gpsimd.dma_start(out=fc1w[:], in_=t["fc1wd"][:]).then_inc(lw, 16)
        nc.tensor.wait_ge(lw, 16)
        p.bar()
        nc.gpsimd.reg_mov(p.rG, 0)
        nc.vector.reg_mov(p.rV, 16)
        with p.tloop(ntile) as i:
            nc.gpsimd.wait_ge(st, p.rG)
            nc.gpsimd.reg_add(p.rG, p.rG, 16)
            nc.gpsimd.dma_start(out=xt.rearrange("p (k t) -> p k t", t=W),
                                in_=p.chg(t["xb1d"])).then_inc(ld, 16)
            nc.vector.wait_ge(ld, p.rV)
            nc.vector.reg_add(p.rV, p.rV, 16)
            p.bar()
            xv = xt.rearrange("p (k t) -> p k t", t=W)
            p.ln_stats(xv, 8, W)
            for k in range(8):
                V.tensor_tensor(t["qtmp"][:, 0:W], xv[:, k, :], t["nmuB"][:, 0:W], ALU.add)
                V.tensor_tensor(t["qtmp"][:, 0:W], t["qtmp"][:, 0:W],
                                t["invsB"][:, 0:W], ALU.mult)
                V.tensor_scalar(y2t[:, k * W:(k + 1) * W], t["qtmp"][:, 0:W],
                                t["ln2wb"][:, k:k + 1], t["ln2bb"][:, k:k + 1],
                                ALU.mult, ALU.add)
            p.bar()
            glv = glb.rearrange("p (m t) -> p m t", t=W)
            u8v = u8.rearrange("p (m t) -> p m t", t=W)
            nc.vector.reg_mov(p.rVo, 0)
            nc.scalar.reg_mov(p.rS, 0)
            for g in range(4):
                for m8 in range(8):
                    m = g * 8 + m8
                    for k in range(8):
                        p.mmN(psA, (m8 % 4) * 512,
                              fc1w[:, (m * 8 + k) * 128:(m * 8 + k + 1) * 128],
                              y2t[:, k * W:(k + 1) * W], W,
                              start=(k == 0), stop=(k == 7))
                    if m8 % 4 == 3:
                        p.bar()
                        for q4 in range(4):
                            V.tensor_scalar(u8v[:, m8 - 3 + q4, 0:W],
                                            psA[:, q4 * 512:q4 * 512 + W],
                                            t["fc1bb"][:, m - 3 + q4:m - 2 + q4],
                                            None, ALU.add)
                        p.bar()
                gs = p.regscope(); gs.__enter__()
                with nc.Fori(0, 8) as mi:
                    oD = nc.vector.snap(p.rVo, min_val=g * 8, max_val=g * 8 + 7)
                    oA = nc.scalar.snap(p.rS, min_val=0, max_val=7)
                    p._stage_regs += [oD.val, oA.val]
                    uS = u8v[:, ds(oA, 1), 0:W]
                    uV = u8v[:, ds(oD - g * 8, 1), 0:W]
                    S.activation(out=xa[:], in_=uS, func=AF.Abs)
                    S.activation(out=nh[:], in_=uS, func=AF.Square)
                    S.activation(out=eb[:], in_=nh[:], func=AF.Exp, scale=-0.5)
                    p.bar()
                    V.tensor_scalar(xc[:], xa[:], 16.0, None, ALU.min)
                    V.tensor_scalar(nh[:], xc[:], GN[4][0], GN[3][0], ALU.mult, ALU.add)
                    V.tensor_scalar(nh[:], nh[:], GN[3][1], None, ALU.add)
                    for jj in (2, 1):
                        V.tensor_tensor(nh[:], nh[:], xc[:], ALU.mult)
                        V.tensor_scalar(nh[:], nh[:], GN[jj][0], GN[jj][1], ALU.add, ALU.add)
                    V.tensor_tensor(nh[:], nh[:], xc[:], ALU.mult)
                    V.tensor_scalar(nh[:], nh[:], 1.0, None, ALU.add)
                    V.tensor_scalar(dh[:], xc[:], GD[5][0], GD[4][0], ALU.mult, ALU.add)
                    for jj in (3, 2, 1):
                        V.tensor_tensor(dh[:], dh[:], xc[:], ALU.mult)
                        V.tensor_scalar(dh[:], dh[:], GD[jj][0], GD[jj][1], ALU.add, ALU.add)
                    V.tensor_tensor(dh[:], dh[:], xc[:], ALU.mult)
                    V.tensor_scalar(dh[:], dh[:], 1.0, None, ALU.add)
                    V.reciprocal(xc[:], dh[:])
                    V.tensor_tensor(nh[:], nh[:], xc[:], ALU.mult)
                    V.tensor_tensor(eb[:], eb[:], nh[:], ALU.mult)
                    V.tensor_scalar(eb[:], eb[:], -0.5, 0.5, ALU.mult, ALU.add)
                    V.tensor_tensor(eb[:], eb[:], xa[:], ALU.mult)
                    V.tensor_scalar(nh[:], uV, 0.5, None, ALU.mult)
                    V.tensor_tensor(glv[:, ds(oD, 1), 0:W], nh[:], eb[:], ALU.add)
                    p.bar()
                    nc.vector.reg_add(p.rVo, p.rVo, 1)
                    nc.scalar.reg_add(p.rS, p.rS, 1)
                gs.__exit__(None, None, None)
                nc.scalar.reg_mov(p.rS, 0)
            nc.sync.dma_start(
                out=p.flats(t["gld"], p.ntok, 32),
                in_=glb.rearrange("p (k t) -> p k t", t=W)).then_inc(st, 16)
        nc.sync.wait_ge(st, 16 * ntile)


def emit_s3b2(p):
    nc, t, V, S = p.nc, p.t, p.V, p.S
    ntile = p.ntok // TT
    ld, lw, st = t["ld_s3b2"], t["lw_s3b2"], t["st_s3b2"]
    W = TT
    psA = t["psA"]
    with p.stage("s3b2"):
        fc2w = p.sb("fc2ws", [128, 32 * D], BF16)
        glb = p.sb("b2g", [128, 32 * W], BF16)
        xb1 = p.sb("b2xb1", [128, 8 * W])
        x2 = p.sb("b2x2", [128, 8 * W])
        stg = p.sb("b2stg", [128, 8 * W])
        nc.gpsimd.dma_start(out=fc2w[:], in_=t["fc2wd"][:]).then_inc(lw, 16)
        nc.tensor.wait_ge(lw, 16)
        p.bar()
        nc.gpsimd.reg_mov(p.rG, 0)
        nc.vector.reg_mov(p.rV, 48)
        with p.tloop(ntile) as i:
            nc.gpsimd.wait_ge(st, p.rG)
            nc.gpsimd.reg_add(p.rG, p.rG, 16)
            nc.gpsimd.dma_start(out=glb.rearrange("p (k t) -> p k t", t=W),
                                in_=p.flatg(t["gld"], p.ntok, 32)).then_inc(ld, 16)
            nc.gpsimd.dma_start(out=xb1.rearrange("p (k t) -> p k t", t=W),
                                in_=p.chg(t["xb1d"])).then_inc(ld, 16)
            nc.gpsimd.dma_start(out=x2.rearrange("p (k t) -> p k t", t=W),
                                in_=p.chg(t["x2d"])).then_inc(ld, 16)
            nc.vector.wait_ge(ld, p.rV)
            nc.vector.reg_add(p.rV, p.rV, 48)
            p.bar()
            for m in range(8):
                pp = m % 2
                for k in range(32):
                    p.mmN(psA, pp * 1024,
                          fc2w[:, (m * 32 + k) * 128:(m * 32 + k + 1) * 128],
                          glb[:, k * W:(k + 1) * W], W,
                          start=(k == 0), stop=(k == 31))
                p.bar()
                V.tensor_scalar(t["qtmp"][:, 0:W], psA[:, pp * 1024:pp * 1024 + W],
                                t["fc2bb"][:, m:m + 1], None, ALU.add)
                V.tensor_tensor(t["qtmp"][:, 0:W], t["qtmp"][:, 0:W],
                                xb1[:, m * W:(m + 1) * W], ALU.add)
                V.tensor_tensor(stg[:, m * W:(m + 1) * W], t["qtmp"][:, 0:W],
                                x2[:, m * W:(m + 1) * W], ALU.add)
                p.bar()
            nc.sync.dma_start(out=p.chs(t["x3d"]),
                              in_=stg.rearrange("p (k t) -> p k t", t=W)
                              ).then_inc(st, 16)
        nc.sync.wait_ge(st, 16 * ntile)


def emit_phase2(nc, t, ntok, stages=8):
    p = P2(nc, t, ntok)
    t["p2obj"] = p
    p.edges_init()
    emit_wprep(p, t["w1T"], t["w1qd"], "w1sc", 8, 32, "1")
    emit_wprep(p, t["w2T"], t["w2qd"], "w2sc", 32, 8, "2")
    if stages >= 1:
        emit_s1(p)
    if stages >= 2:
        emit_ffn1(p, "x1d", "s2a")
    if stages >= 3:
        emit_ffn2(p, "x1d", "x2d", "s2b")
    if stages >= 4:
        emit_s3a(p)
    if stages >= 5:
        emit_s3a2(p)
    if stages >= 6:
        emit_s3b1(p)
    if stages >= 7:
        emit_s3b2(p)
    if stages >= 8:
        emit_ffn1(p, "x3d", "s4a")
    if stages >= 9:
        emit_ffn2(p, "x3d", "outd", "s4b")



def emit_phase_a(nc, t, nstep):
    V, S = _wrap(nc.vector), _wrap(nc.scalar)
    ncha = nstep // CHA
    xt, wx, gd, biasb = t["xt"], t["wx"], t["gd"], t["biasb"]
    onescol, ones128 = t["onescol"], t["ones128"]
    wxb, xbuf, sqb, s1b, t8 = t["wxb"], t["xbuf"], t["sqb"], t["s1b"], t["t8"]
    trow, nmub, invsb, gstage = t["trow"], t["nmub"], t["invsb"], t["gstage"]
    psS, psB, psP = t["psS"], t["psB"], t["psP"]
    (a_wl, a_xl, a_sq, a_mm2, a_mmS, a_ew1, a_iv, a_rows, a_mmB, a_s1, a_pg,
     a_gact, a_st, a_pgc) = (
        t["a_wl"], t["a_xl"], t["a_sq"], t["a_mm2"], t["a_mmS"], t["a_ew1"],
        t["a_iv"], t["a_rows"], t["a_mmB"], t["a_s1"], t["a_pg"], t["a_gact"],
        t["a_st"], t["a_pgc"])

    rQ = nc.scalar.alloc_register("rQ")
    rTsq = nc.tensor.alloc_register("rTsq")
    rTg = nc.tensor.alloc_register("rTg")
    rVp = nc.vector.alloc_register("rVp")
    nc.scalar.reg_mov(rQ, 1)
    nc.tensor.reg_mov(rTsq, 1)
    nc.tensor.reg_mov(rTg, 1)
    nc.vector.reg_mov(rVp, 1)
    nc.vector.sem_inc(a_gact, 2)

    for k in range(8):
        nc.gpsimd.dma_start(
            out=wxb.rearrange("p (m k c) -> p m k c", k=8, c=128)[:, :, k, :],
            in_=wx[k * 128:(k + 1) * 128, :].rearrange("p (m c) -> p m c", c=128),
        ).then_inc(a_wl, 16)

    with nc.Fori(0, ncha) as c:
        nc.gpsimd.wait_ge(a_s1, 8 * c)
        nc.gpsimd.wait_ge(a_sq, 8 * c)
        nc.gpsimd.wait_ge(a_pgc, c)
        for k in range(8):
            nc.gpsimd.dma_start(
                out=xbuf[:, k * TCA:(k + 1) * TCA],
                in_=xt[k * 128:(k + 1) * 128, :][:, ds(c * TCA, TCA)],
            ).then_inc(a_xl, 16)
        nc.tensor.wait_ge(a_wl, 128)
        nc.tensor.wait_ge(a_xl, (c + 1) * 128)
        nc.tensor.wait_ge(a_rows, c)
        for k in range(8):
            nc.tensor.matmul(psS[:, 0:TCA], onescol[:], xbuf[:, k * TCA:(k + 1) * TCA],
                             start=(k == 0), stop=(k == 7))
        nc.scalar.wait_ge(a_xl, (c + 1) * 128)
        for k in range(8):
            pq = k % 2
            if k < 2:
                nc.scalar.wait_ge(a_mmS, c)
            else:
                nc.scalar.wait_ge(a_mm2, rQ)
                nc.scalar.reg_add(rQ, rQ, 1)
            S.activation(out=sqb[:, pq * TCA:(pq + 1) * TCA],
                         in_=xbuf[:, k * TCA:(k + 1) * TCA],
                         func=AF.Square).then_inc(a_sq, 1)
            nc.tensor.wait_ge(a_sq, rTsq)
            nc.tensor.reg_add(rTsq, rTsq, 1)
            inst = nc.tensor.matmul(psS[:, TCA:2 * TCA], onescol[:],
                                    sqb[:, pq * TCA:(pq + 1) * TCA],
                                    start=(k == 0), stop=(k == 7))
            inst.then_inc(a_mmS if k == 7 else a_mm2, 1)
        nc.scalar.reg_add(rQ, rQ, 1)
        MU, VAR, NMU = 0, TCA, 2 * TCA
        nc.vector.wait_ge(a_mmS, c + 1)
        V.tensor_scalar_mul(trow[:, MU:MU + TCA], psS[:, 0:TCA], 1.0 / D)
        V.tensor_scalar_mul(trow[:, VAR:VAR + TCA], psS[:, TCA:2 * TCA], 1.0 / D)
        V.tensor_scalar_mul(trow[:, NMU:NMU + TCA], trow[:, MU:MU + TCA], -1.0)
        V.tensor_copy(nmub[:], trow[:, NMU:NMU + TCA])
        V.tensor_tensor(trow[:, MU:MU + TCA], trow[:, MU:MU + TCA],
                        trow[:, MU:MU + TCA], ALU.mult)
        V.tensor_tensor(trow[:, VAR:VAR + TCA], trow[:, VAR:VAR + TCA],
                        trow[:, MU:MU + TCA], ALU.subtract).then_inc(a_ew1, 1)
        nc.scalar.wait_ge(a_ew1, c + 1)
        S.activation(out=trow[:, VAR:VAR + TCA], in_=trow[:, VAR:VAR + TCA],
                     func=AF.Ln, bias=t["epsb"][:, 0:1])
        S.activation(out=trow[:, VAR:VAR + TCA], in_=trow[:, VAR:VAR + TCA],
                     func=AF.Exp, scale=-0.5).then_inc(a_iv, 1)
        nc.vector.wait_ge(a_iv, c + 1)
        V.tensor_copy(invsb[:], trow[:, VAR:VAR + TCA]).then_inc(a_rows, 1)
        nc.tensor.wait_ge(a_rows, c + 1)
        nc.tensor.wait_ge(a_s1, 8 * c)
        nc.tensor.matmul(psB[:, 0:TCA], ones128[:], nmub[:], start=True, stop=True)
        nc.tensor.matmul(psB[:, TCA:2 * TCA], ones128[:], invsb[:],
                         start=True, stop=True).then_inc(a_mmB, 1)
        nc.vector.wait_ge(a_mmB, c + 1)
        for k in range(8):
            V.tensor_tensor(t8[:], xbuf[:, k * TCA:(k + 1) * TCA],
                            psB[:, 0:TCA], ALU.add)
            V.tensor_tensor(s1b[:, k * TCA:(k + 1) * TCA], t8[:],
                            psB[:, TCA:2 * TCA], ALU.mult).then_inc(a_s1, 1)
        nc.tensor.wait_ge(a_s1, 8 * (c + 1))
        for m in range(NMG):
            pp = m % 2
            nc.tensor.wait_ge(a_gact, rTg)
            nc.tensor.reg_add(rTg, rTg, 1)
            for k in range(8):
                inst = nc.tensor.matmul(psP[:, pp * TCA:(pp + 1) * TCA],
                                        wxb[:, (m * 8 + k) * 128:(m * 8 + k + 1) * 128],
                                        s1b[:, k * TCA:(k + 1) * TCA],
                                        start=(k == 0), stop=(k == 7))
            inst.then_inc(a_pgc if m == NMG - 1 else a_pg, 1)
            if m == NMG - 1:
                nc.vector.wait_ge(a_pgc, c + 1)
            else:
                nc.vector.wait_ge(a_pg, rVp)
                nc.vector.reg_add(rVp, rVp, 1)
            if m == 0:
                nc.vector.wait_ge(a_st, 16 * c)
            V.tensor_scalar_add(gstage[:, m * TCA:(m + 1) * TCA],
                                psP[:, pp * TCA:(pp + 1) * TCA],
                                biasb[:, m:m + 1]).then_inc(a_gact, 1)
        nc.sync.wait_ge(a_gact, 2 + (c + 1) * NMG)
        nc.sync.dma_start(
            out=gd.rearrange("p (m t) -> p m t", m=NMG)[:, :, ds(c * TCA, TCA)],
            in_=gstage.rearrange("p (m t) -> p m t", m=NMG),
        ).then_inc(a_st, 16)
    nc.sync.wait_ge(a_st, ncha * 16)



def emit_phase_b(nc, t, nstep, record_hs=True):
    V, S = _wrap(nc.vector), _wrap(nc.scalar)
    nbody = nstep // (2 * INNER)
    wh, weg, gd, begb, hsout = t["wh"], t["weg"], t["gd"], t["begb"], t["hsout"]
    whb, gbuf = t["whb"], t["gbuf"]
    hT, gT, stc, stn = t["hT"], t["gT"], t["stc"], t["stn"]
    zg, zifo, sifo, gf, ef, tmp, tmp2, rn, hf, hsb = (
        t["zg"], t["zifo"], t["sifo"], t["gf"], t["ef"], t["tmp"], t["tmp2"],
        t["rn"], t["hf"], t["hsb"])
    psG, psI, psW = t["psG"], t["psI"], t["psW"]
    (b_wl, b_gl, b_mg, b_mi, b_mw, b_eg, b_ag, b_gt, b_ei, b_ai, b_ae,
     b_ec, b_hs, b_mo, b_eo, b_ao) = (
        t["b_wl"], t["b_gl"], t["b_mg"], t["b_mi"], t["b_mw"], t["b_eg"],
        t["b_ag"], t["b_gt"], t["b_ei"], t["b_ai"], t["b_ae"], t["b_ec"],
        t["b_hs"], t["b_mo"], t["b_eo"], t["b_ao"])

    rT = nc.tensor.alloc_register("rT")
    rV = nc.vector.alloc_register("rV")
    rS = nc.scalar.alloc_register("rS")
    nc.tensor.reg_mov(rT, 1)
    nc.vector.reg_mov(rV, 1)
    nc.scalar.reg_mov(rS, 1)

    for k in range(8):
        nc.gpsimd.dma_start(
            out=whb.rearrange("p (m k c) -> p m k c", k=8, c=128)[:, 0:NMG, k, :],
            in_=wh[k * 128:(k + 1) * 128, :].rearrange("p (m c) -> p m c", c=128),
        ).then_inc(b_wl, 16)
        nc.gpsimd.dma_start(
            out=whb.rearrange("p (m k c) -> p m k c", k=8, c=128)[:, NMG:NT, k, :],
            in_=weg[k * 128:(k + 1) * 128, :].rearrange("p (m c) -> p m c", c=128),
        ).then_inc(b_wl, 16)
    nc.vector.memset(hT[:], 0.0)
    nc.vector.drain()
    nc.vector.memset(stc[:], 0.0)
    nc.vector.drain()
    nc.vector.memset(stn[:], 0.0)
    nc.vector.drain()
    nc.vector.sem_inc(b_ec, 1)
    nc.tensor.wait_ge(b_wl, 256)

    with nc.Fori(0, nbody) as cb:
        nc.gpsimd.wait_ge(b_eo, 16 * cb)
        for H in range(2):
            nc.gpsimd.dma_start(
                out=gbuf[:, H * INNER * GSC:(H + 1) * INNER * GSC].rearrange(
                    "p (m t) -> p m t", m=NMG),
                in_=gd.rearrange("p (m t) -> p m t", m=NMG)[
                    :, :, ds(cb * 2 * INNER * B + H * INNER * B, INNER * B)],
            ).then_inc(b_gl, 16)
        for j in range(2 * INNER):
            H = j // INNER
            pa = j % 2
            gt_ap = gbuf.rearrange("p (h m t) -> p h m t", h=2, m=NMG)[
                :, H, :, (j % INNER) * 8:(j % INNER + 1) * 8]
            nc.tensor.wait_ge(b_ec, rT)
            for m in range(8):
                for k in range(8):
                    inst = nc.tensor.matmul(
                        psG[:, pa * 64 + m * 8:pa * 64 + (m + 1) * 8],
                        whb[:, (m * 8 + k) * 128:(m * 8 + k + 1) * 128],
                        hT[:, k * 8:(k + 1) * 8],
                        start=(k == 0), stop=(k == 7))
            inst.then_inc(b_mg, 1)
            for m in range(8, 24):
                for k in range(8):
                    inst = nc.tensor.matmul(
                        psI[:, pa * 192 + (m - 8) * 8:pa * 192 + (m - 7) * 8],
                        whb[:, (m * 8 + k) * 128:(m * 8 + k + 1) * 128],
                        hT[:, k * 8:(k + 1) * 8],
                        start=(k == 0), stop=(k == 7))
            inst.then_inc(b_mi, 1)
            nc.vector.wait_ge(b_mg, rV)
            if j == 0:
                nc.vector.wait_ge(b_gl, (cb + 1) * 32)
            V.tensor_tensor(
                zg.rearrange("p (m t) -> p m t", m=8),
                psG[:, pa * 64:pa * 64 + 64].rearrange("p (m t) -> p m t", m=8),
                gt_ap[:, 0:8, :], ALU.add).then_inc(b_eg, 1)
            nc.scalar.wait_ge(b_eg, rS)
            S.activation(out=gf[:], in_=zg[:], func=AF.Tanh).then_inc(b_ag, 1)
            nc.vector.wait_ge(b_ag, rV)
            V.tensor_copy(gT[:], gf[:]).then_inc(b_gt, 1)
            nc.tensor.wait_ge(b_gt, rT)
            nc.tensor.reg_add(rT, rT, 1)
            for m in range(NMG, NT):
                mo = m - NMG
                for k in range(8):
                    inst = nc.tensor.matmul(
                        psW[:, pa * 64 + mo * 8:pa * 64 + (mo + 1) * 8],
                        whb[:, (m * 8 + k) * 128:(m * 8 + k + 1) * 128],
                        gT[:, k * 8:(k + 1) * 8],
                        start=(k == 0), stop=(k == 7))
            inst.then_inc(b_mw, 1)
            for m in range(24, 32):
                for k in range(8):
                    inst = nc.tensor.matmul(
                        psI[:, pa * 192 + (m - 8) * 8:pa * 192 + (m - 7) * 8],
                        whb[:, (m * 8 + k) * 128:(m * 8 + k + 1) * 128],
                        hT[:, k * 8:(k + 1) * 8],
                        start=(k == 0), stop=(k == 7))
            inst.then_inc(b_mo, 1)
            nc.vector.wait_ge(b_mi, rV)
            V.tensor_tensor(
                zifo.rearrange("p (m t) -> p m t", m=24)[:, 0:16, :],
                psI[:, pa * 192:pa * 192 + 128].rearrange("p (m t) -> p m t", m=16),
                gt_ap[:, 8:24, :], ALU.add).then_inc(b_ei, 1)
            nc.scalar.wait_ge(b_ei, rS)
            S.activation(out=sifo[:, 0:128], in_=zifo[:, 0:128],
                         func=AF.Sigmoid).then_inc(b_ai, 1)
            nc.scalar.wait_ge(b_mw, rS)
            for mo in range(NMW):
                inst = S.activation(
                    out=ef[:, mo * 8:(mo + 1) * 8],
                    in_=psW[:, pa * 64 + mo * 8:pa * 64 + (mo + 1) * 8],
                    func=AF.Exp, bias=begb[:, mo:mo + 1])
            inst.then_inc(b_ae, 1)
            SI, SF, SO = 0, 64, 128
            nc.vector.wait_ge(b_ai, rV)
            nc.vector.wait_ge(b_ae, rV)
            if record_hs and j == 0:
                nc.vector.wait_ge(b_hs, 32 * cb)
            V.tensor_tensor(stc[:], sifo[:, SF:SF + 64], stc[:], ALU.mult)
            V.tensor_tensor(tmp[:], sifo[:, SI:SI + 64], gf[:], ALU.mult)
            V.tensor_tensor(stc[:], stc[:], tmp[:], ALU.add)
            V.tensor_tensor(stn[:], sifo[:, SF:SF + 64], stn[:], ALU.mult)
            V.tensor_tensor(tmp2[:], sifo[:, SI:SI + 64], ef[:], ALU.mult)
            V.tensor_tensor(stn[:], stn[:], tmp2[:], ALU.add)
            V.reciprocal(rn[:], stn[:])
            V.tensor_tensor(tmp[:], stc[:], rn[:], ALU.mult)
            nc.vector.wait_ge(b_mo, rV)
            V.tensor_tensor(
                zifo.rearrange("p (m t) -> p m t", m=24)[:, 16:24, :],
                psI[:, pa * 192 + 128:pa * 192 + 192].rearrange(
                    "p (m t) -> p m t", m=8),
                gt_ap[:, 24:32, :], ALU.add).then_inc(b_eo, 1)
            nc.scalar.wait_ge(b_eo, rS)
            nc.scalar.reg_add(rS, rS, 1)
            S.activation(out=sifo[:, SO:SO + 64], in_=zifo[:, 128:192],
                         func=AF.Sigmoid).then_inc(b_ao, 1)
            nc.vector.wait_ge(b_ao, rV)
            nc.vector.reg_add(rV, rV, 1)
            V.tensor_tensor(hf[:], sifo[:, SO:SO + 64], tmp[:], ALU.mult)
            if record_hs:
                V.tensor_copy(
                    hsb.rearrange("p (m s) -> p m s", s=2 * INNER)[:, :, j],
                    hf.rearrange("p (m b) -> p m b", b=8)[:, :, 0])
            V.tensor_copy(hT[:], hf[:]).then_inc(b_ec, 1)
            if record_hs and (j == INNER - 1 or j == 2 * INNER - 1):
                nc.sync.wait_ge(b_ec, 16 * cb + (H + 1) * INNER + 1)
                nc.sync.dma_start(
                    out=hsout.rearrange("p (m t) -> p m t", m=8)[
                        :, :, ds(16 * cb + H * INNER, INNER)],
                    in_=hsb.rearrange("p (m s) -> p m s", s=2 * INNER)[
                        :, :, H * INNER:(H + 1) * INNER],
                ).then_inc(b_hs, 16)
    if record_hs:
        nc.sync.wait_ge(b_hs, nbody * 32)


def host_tile_lhsT(w):
    K, M = w.shape
    nk, nm = K // 128, M // 128
    out = np.empty((128, nm * nk * 128), np.float32)
    for m in range(nm):
        for k in range(nk):
            out[:, (m * nk + k) * 128:(m * nk + k + 1) * 128] = \
                w[k * 128:(k + 1) * 128, m * 128:(m + 1) * 128]
    return out


def host_tile_lhsT64(w):
    K, M = w.shape
    nk, nm = K // 64, M // 128
    out = np.empty((64, nm * nk * 128), np.float32)
    for m in range(nm):
        for kh in range(nk):
            out[:, (m * nk + kh) * 128:(m * nk + kh + 1) * 128] = \
                w[kh * 64:(kh + 1) * 64, m * 128:(m + 1) * 128]
    return out


def perp(v):
    C = v.shape[0]
    return np.ascontiguousarray(v.reshape(C // 128, 128).T)



def build_full(nstep, ntok, stages=9):
    nc = bacc_mod.Bacc("TRN2", target_bir_lowering=False)
    t = {}
    TP = ntok + 2 * PT
    def din(name, shape):
        t[name] = nc.dram_tensor(name, shape, F32, kind="ExternalInput").ap()
    def dint(name, shape, dtype=F32):
        t[name] = nc.dram_tensor(name, shape, dtype, kind="Internal").ap()
    din("xb", [D, ntok])
    dint("hd", [128, 8 * ntok])
    din("sxt", [D, nstep * B]); din("wx", [D, 4 * SL]); din("wh", [D, 4 * SL])
    din("weg", [D, SL]); din("biasT", [128, NMG]); din("begT", [128, NMW])
    dint("gd", [128, NMG * (nstep + 2 * INNER) * B])
    din("identin", [128, 128])
    din("w1T", [D, DF]); din("w2T", [DF, D])
    din("qkvwin", [128, 24 * 8 * 128]); din("projwin", [64, 8 * 16 * 128])
    din("fc1win", [128, 32 * 8 * 128]); din("fc2win", [128, 8 * 32 * 128])
    din("vecsin", [128, 8 + 8 + 8 + 8 + 8 + 8 + 24 + 8 + 32 + 8 + 96 + 32])
    t["outd"] = nc.dram_tensor("outd", [128, 8 * ntok], F32, kind="ExternalOutput").ap()
    for nmx in ("x1d", "x2d", "x3d", "xb1d"):
        dint(nmx, [128, 8 * TP])
    dint("k2d", [128, 32 * ntok], BF16)
    dint("attd", [64, 16 * ntok], BF16)
    dint("gld", [128, 32 * ntok], BF16)
    dint("oscd", [1, ntok])
    dint("w1qd", [128, 8 * DF], BF16)
    dint("w2qd", [128, 32 * D], BF16)

    ts = {"xt": t["sxt"], "wx": t["wx"], "wh": t["wh"], "weg": t["weg"],
          "biasT": t["biasT"], "begT": t["begT"], "hsout": t["hd"],
          "gd": t["gd"]}
    scan_sems = []
    def ssem(name):
        h = nc.alloc_semaphore(name="sc_" + name)
        ts[name] = h
        scan_sems.append(h)
    esta = contextlib.ExitStack()
    estb = contextlib.ExitStack()
    estc = contextlib.ExitStack()
    def ssb(est_, name, shape, dtype=F32):
        ts[name] = est_.enter_context(nc.sbuf_tensor("sc_" + name, shape, dtype))
    def spsum(est_, name, shape):
        ts[name] = est_.enter_context(nc.psum_tensor("sc_" + name, shape, F32))
    for s in ["s_init", "a_wl", "a_xl", "a_sq", "a_mm2", "a_mmS", "a_ew1",
              "a_iv", "a_rows", "a_mmB", "a_s1", "a_pg", "a_gact", "a_st",
              "a_pgc", "b_wl", "b_gl", "b_mg", "b_mi", "b_mw", "b_eg", "b_ag",
              "b_gt", "b_ei", "b_ai", "b_ae", "b_ec", "b_hs", "b_mo", "b_eo",
              "b_ao"]:
        ssem(s)
    ssb(estc, "ones128", [1, 128], BF16); ssb(estc, "onescol", [128, 1], BF16)
    ssb(estc, "epsb", [1, 1])
    ssb(estc, "biasb", [128, NMG]); ssb(estc, "begb", [128, NMW])
    ssb(esta, "wxb", [128, 8 * 4 * SL], BF16)
    ssb(esta, "xbuf", [128, 8 * TCA], BF16)
    ssb(esta, "sqb", [128, 2 * TCA], BF16)
    ssb(esta, "s1b", [128, 8 * TCA], BF16)
    ssb(esta, "t8", [128, TCA])
    ssb(esta, "trow", [1, 3 * TCA])
    ssb(esta, "nmub", [1, TCA], BF16); ssb(esta, "invsb", [1, TCA], BF16)
    ssb(esta, "gstage", [128, NMG * TCA])
    spsum(esta, "psS", [1, 2 * TCA]); spsum(esta, "psB", [128, 2 * TCA])
    spsum(esta, "psP", [128, 2 * TCA])
    Vs = _wrap(nc.vector)
    nc.vector.memset(ts["epsb"][:], LN_EPS); nc.vector.drain()
    nc.vector.memset(ts["ones128"][:], 1.0); nc.vector.drain()
    nc.vector.memset(ts["onescol"][:], 1.0); nc.vector.drain()
    nc.sync.dma_start(out=ts["biasb"][:], in_=ts["biasT"][:]).then_inc(ts["s_init"], 16)
    nc.sync.dma_start(out=ts["begb"][:], in_=ts["begT"][:]).then_inc(ts["s_init"], 16)
    nc.vector.wait_ge(ts["s_init"], 32)
    nc.all_engine_barrier()
    emit_phase_a(nc, ts, nstep)
    nc.all_engine_barrier()
    esta.close()
    ssb(estb, "whb", [128, 8 * NT * 128], BF16)
    ssb(estb, "gbuf", [128, 2 * INNER * GSC])
    ssb(estb, "hT", [128, 64], BF16); ssb(estb, "gT", [128, 64], BF16)
    ssb(estb, "stc", [128, 64]); ssb(estb, "stn", [128, 64])
    ssb(estb, "zg", [128, 64]); ssb(estb, "zifo", [128, 192])
    ssb(estb, "sifo", [128, 192])
    ssb(estb, "gf", [128, 64]); ssb(estb, "ef", [128, 64])
    ssb(estb, "tmp", [128, 64]); ssb(estb, "tmp2", [128, 64])
    ssb(estb, "rn", [128, 64]); ssb(estb, "hf", [128, 64])
    ssb(estb, "hsb", [128, 2 * INNER * 8])
    spsum(estb, "psG", [128, 2 * 64]); spsum(estb, "psI", [128, 2 * 192])
    spsum(estb, "psW", [128, 2 * 64])
    emit_phase_b(nc, ts, nstep, record_hs=True)
    nc.all_engine_barrier()
    estb.close()
    estc.close()
    nc.clear_and_free_semaphores(scan_sems)
    nc.all_engine_barrier()
    nc.all_engine_barrier()
    est = contextlib.ExitStack()
    def sb(name, shape, dtype=F32):
        t[name] = est.enter_context(nc.sbuf_tensor(name, shape, dtype))
    def psum(name, shape, dtype=F32):
        t[name] = est.enter_context(nc.psum_tensor(name, shape, dtype))
    def sem(name):
        t[name] = est.enter_context(nc.semaphore(name))
    sb("ones128", [1, 128], BF16); sb("ones64", [1, 64], BF16)
    sb("onescol", [128, 1], BF16); sb("onescf", [128, 1]); sb("ones128f", [1, 128]); sb("ones64f", [1, 64]); sb("epsb", [1, 1])
    sb("ident", [128, 128]); sb("identb", [128, 128], BF16)
    sb("sqb", [128, 4 * 768])
    sb("rowb", [1, 3 * 768])
    sb("qtmp", [128, 768]); sb("qtmp2", [128, 768])
    sb("kint", [128, 768], I32)
    sb("fold", [128, 768])
    sb("colcl", [128, 8]); sb("cl2", [128, 8]); sb("clb", [128, 16], BF16)
    sb("qrow", [1, 2 * 768])
    sb("oscb", [128, 768]); sb("sxb", [128, 768])
    sb("nmuB", [128, 768]); sb("invsB", [128, 768])
    sb("zrow", [128, 8 * PT])
    sb("gammab", [128, 8]); sb("betab", [128, 8])
    sb("ln1wb", [128, 8]); sb("ln1bb", [128, 8])
    sb("ln2wb", [128, 8]); sb("ln2bb", [128, 8])
    sb("qkvbb", [128, 24]); sb("projbb", [128, 8])
    sb("fc1bb", [128, 32]); sb("fc2bb", [128, 8])
    sb("convw", [128, 96]); sb("convb", [128, 32])
    sb("w1sc", [1, 1]); sb("w2sc", [1, 1])
    sb("w1scc", [128, 1]); sb("w2scc", [128, 1])
    sb("vecs", [128, 256])
    psum("psA", [128, 2048]); psum("psB", [128, 1536])
    psum("psD", [128, 384], BF16)
    for s in ["s_init", "ld_w1", "st_w1", "ld_w2", "st_w2",
              "ld_s1", "st_s1",
              "ld_s2a", "lw_s2a", "st_s2a", "ld_s2b", "lw_s2b", "st_s2b",
              "ld_s3a", "lw_s3a", "st_s3a", "ld_s3a2", "lw_s3a2", "st_s3a2",
              "ld_s3b1", "lw_s3b1", "st_s3b1", "ld_s3b2", "lw_s3b2", "st_s3b2",
              "ld_s4a", "lw_s4a", "st_s4a", "ld_s4b", "lw_s4b", "st_s4b"]:
        sem(s)
    t["qkvwd"] = t["qkvwin"]; t["projwd"] = t["projwin"]
    t["fc1wd"] = t["fc1win"]; t["fc2wd"] = t["fc2win"]

    si = t["s_init"]
    nc.sync.dma_start(out=t["ident"][:], in_=t["identin"][:]).then_inc(si, 16)
    nc.sync.dma_start(out=t["vecs"][:, 0:248], in_=t["vecsin"][:, 0:248]).then_inc(si, 16)
    nc.vector.wait_ge(si, 32)
    V = _wrap(nc.vector)
    V.tensor_copy(t["identb"][:], t["ident"][:])
    nc.vector.memset(t["ones128"][:], 1.0); nc.vector.drain()
    nc.vector.memset(t["ones64"][:], 1.0); nc.vector.drain()
    nc.vector.memset(t["onescol"][:], 1.0); nc.vector.drain()
    nc.vector.memset(t["onescf"][:], 1.0); nc.vector.drain()
    nc.vector.memset(t["ones128f"][:], 1.0); nc.vector.drain()
    nc.vector.memset(t["ones64f"][:], 1.0); nc.vector.drain()
    nc.vector.memset(t["epsb"][:], LN_EPS); nc.vector.drain()
    nc.vector.memset(t["zrow"][:], 0.0); nc.vector.drain()
    off = 0
    for name, w in [("gammab", 8), ("betab", 8), ("ln1wb", 8), ("ln1bb", 8),
                    ("ln2wb", 8), ("ln2bb", 8), ("qkvbb", 24), ("projbb", 8),
                    ("fc1bb", 32), ("fc2bb", 8), ("convw", 96), ("convb", 32)]:
        V.tensor_copy(t[name][:], t["vecs"][:, off:off + w])
        off += w
    nc.vector.sem_inc(si, 1)
    nc.sync.wait_ge(si, 33)
    for nmx in ("x1d", "x2d", "x3d"):
        dv = t[nmx].rearrange("p (k t) -> p k t", t=ntok + 2 * PT)
        nc.sync.dma_start(out=dv[:, :, 0:PT],
                          in_=t["zrow"].rearrange("p (k t) -> p k t", t=PT)
                          ).then_inc(si, 16)
        nc.sync.dma_start(out=dv[:, :, ntok + PT:ntok + 2 * PT],
                          in_=t["zrow"].rearrange("p (k t) -> p k t", t=PT)
                          ).then_inc(si, 16)
    nc.sync.wait_ge(si, 33 + 96)
    nc.all_engine_barrier()
    emit_phase2(nc, t, ntok, stages)
    nc.all_engine_barrier()
    est.close()
    return nc




NSTEP_FULL = 4096
_CACHE = {}


def _get_nc():
    if "nc" not in _CACHE:
        nc = build_full(NSTEP_FULL, NSTEP_FULL, 9)
        nc.compile()
        _CACHE["nc"] = nc
    return _CACHE["nc"]


def kernel(**inp):
    f32 = np.float32
    g = lambda k: np.ascontiguousarray(np.asarray(inp[k], dtype=f32))
    x = g("L0_attn_a_x")
    Wi, Wf = g("L0_attn_a_Wi"), g("L0_attn_a_Wf")
    Wg, Wo = g("L0_attn_a_Wg"), g("L0_attn_a_Wo")
    Weg = g("L0_attn_a_Weg")
    bi, bf = g("L0_attn_a_bi"), g("L0_attn_a_bf")
    bg, bo = g("L0_attn_a_bg"), g("L0_attn_a_bo")
    beg = g("L0_attn_a_beg")
    wx = np.ascontiguousarray(np.concatenate([Wg[:D], Wi[:D], Wf[:D], Wo[:D]], 1))
    wh = np.ascontiguousarray(np.concatenate([Wg[D:], Wi[D:], Wf[D:], Wo[D:]], 1))
    biasT = np.ascontiguousarray(np.concatenate([bg, bi, bf, bo]).reshape(NMG, 128).T)
    begT = perp(beg)
    Wd = {
        "gamma": g("L0_attn_a_gamma"), "beta": g("L0_attn_a_beta"),
        "ln1_w": g("L0_attn_b_ln1_w"), "ln1_b": g("L0_attn_b_ln1_b"),
        "ln2_w": g("L0_attn_b_ln2_w"), "ln2_b": g("L0_attn_b_ln2_b"),
        "qkv_w": g("L0_attn_b_qkv_w"), "qkv_b": g("L0_attn_b_qkv_b"),
        "proj_w": g("L0_attn_b_proj_w"), "proj_b": g("L0_attn_b_proj_b"),
        "fc1_w": g("L0_attn_b_fc1_w"), "fc1_b": g("L0_attn_b_fc1_b"),
        "fc2_w": g("L0_attn_b_fc2_w"), "fc2_b": g("L0_attn_b_fc2_b"),
        "w1": g("L0_ffn_w1"), "w2": g("L0_ffn_w2"),
        "conv_w": g("L0_ffn_conv_w"), "conv_b": g("L0_ffn_conv_b"),
    }
    vecs = np.concatenate([
        perp(Wd["gamma"]), perp(Wd["beta"]), perp(Wd["ln1_w"]), perp(Wd["ln1_b"]),
        perp(Wd["ln2_w"]), perp(Wd["ln2_b"]), perp(Wd["qkv_b"]), perp(Wd["proj_b"]),
        perp(Wd["fc1_b"]), perp(Wd["fc2_b"]),
        np.ascontiguousarray(Wd["conv_w"].reshape(32, 128, 3).transpose(1, 0, 2)
                             .reshape(128, 96)),
        perp(Wd["conv_b"])], axis=1).astype(np.float32)
    shared = {
        "wx": wx, "wh": wh, "weg": np.ascontiguousarray(Weg),
        "biasT": biasT, "begT": begT,
        "identin": np.eye(128, dtype=f32),
        "w1T": np.ascontiguousarray(Wd["w1"].T),
        "w2T": np.ascontiguousarray(Wd["w2"].T),
        "qkvwin": host_tile_lhsT(Wd["qkv_w"].T),
        "projwin": host_tile_lhsT64(Wd["proj_w"].T),
        "fc1win": host_tile_lhsT(Wd["fc1_w"].T),
        "fc2win": host_tile_lhsT(Wd["fc2_w"].T),
        "vecsin": vecs,
    }
    S = x.shape[1]
    in_maps = []
    for c in range(8):
        perm = [(c + j) % 8 for j in range(8)]
        m = dict(shared)
        m["sxt"] = np.ascontiguousarray(x[perm].transpose(2, 1, 0).reshape(D, S * B))
        m["xb"] = np.ascontiguousarray(x[c].T)
        in_maps.append(m)
    res = run_bass_kernel_spmd(_get_nc(), in_maps, core_ids=list(range(8)))
    out = np.empty((8, S, D), f32)
    for c in range(8):
        out[c] = (res.results[c]["outd"].reshape(128, 8, S)
                  .transpose(2, 1, 0).reshape(S, D))
    return out


# version 06 (final)
# speedup vs baseline: 1.0926x; 1.0058x over previous
import os
os.environ.setdefault("NEURON_SCRATCHPAD_PAGE_SIZE", "640")
import contextlib
import numpy as np
import ml_dtypes
import concourse.bacc as bacc_mod
import concourse.mybir as mybir
from concourse.bass import ds
from concourse.bass_utils import run_bass_kernel_spmd

F32 = mybir.dt.float32
I32 = mybir.dt.int32
BF16 = mybir.dt.bfloat16
AF = mybir.ActivationFunctionType
ALU = mybir.AluOpType

BF16 = mybir.dt.bfloat16
I32 = mybir.dt.int32
AF = mybir.ActivationFunctionType
ALU = mybir.AluOpType

TT = 512
PT = 128
D = 1024
DF = 4096
NH = 16
LN_EPS = 1e-5
HE_FFN = 64
HE_ATT = 128
WF = TT + 2 * HE_FFN
WA = TT + 2 * HE_ATT
MAGIC = 12582912.0
GN = [(1.0, 0.0),
      (0.7847558856010437, 2.4108107510301124e-08),
      (0.3021286129951477, 1.8317987127147717e-09),
      (0.060451213270425797, 3.7214936687846034e-10),
      (0.005342473741620779, 0.0)]
GD = [(1.0, 0.0),
      (1.582640528678894, -4.984901735838321e-08),
      (1.0648928880691528, -2.0261421429523807e-09),
      (0.38475489616394043, 9.13366449140085e-09),
      (0.07580475509166718, 0.0),
      (0.00669455248862505, 0.0)]

INNER = 8
B = 8
SL = 1024
NMG = 4 * SL // 128
NMW = SL // 128
NT = NMG + NMW
CHA = 64
TCA = CHA * B
GSC = NMG * 8

def _wrap(eng):
    class W:
        def __getattr__(self, name):
            fn = getattr(eng, name)
            def call(*a, **k):
                inst = fn(*a, **k)
                eng.drain()
                return inst
            return call
    return W()


class P2:
    def __init__(self, nc, t, ntok):
        self.nc = nc
        self.t = t
        self.ntok = ntok
        self.TP = ntok + 2 * PT
        self.V = _wrap(nc.vector)
        self.S = _wrap(nc.scalar)
        self.scope = None
        self.rG = nc.gpsimd.alloc_register("p2rG")
        self.rV = nc.vector.alloc_register("p2rV")
        self.rGO = nc.gpsimd.alloc_register("p2rGO")
        self.rSO = nc.sync.alloc_register("p2rSO")
        self.rT = nc.tensor.alloc_register("p2rT")
        self.rVo = nc.vector.alloc_register("p2rVo")
        self.rS = nc.scalar.alloc_register("p2rS")
        self.tDP = nc.tensor.alloc_register("p2tDP")
        self.tDPq = nc.tensor.alloc_register("p2tDPq")
        self.vPDq = nc.vector.alloc_register("p2vPDq")
        self.tAP = nc.tensor.alloc_register("p2tAP")
        self.vPD = nc.vector.alloc_register("p2vPD")
        self.vAD = nc.vector.alloc_register("p2vAD")
        self.sPA = nc.scalar.alloc_register("p2sPA")
        self.sDA = nc.scalar.alloc_register("p2sDA")
        self.rY = nc.sync.alloc_register("p2rY")
        self.rY2 = nc.sync.alloc_register("p2rY2")
        self.vSD = nc.vector.alloc_register("p2vSD")
        nc.gpsimd.reg_mov(self.rGO, 0)
        nc.sync.reg_mov(self.rSO, 0)
        self.oG = None
        self.oS = None
        self._stage_regs = []


    def bar(self):
        self.nc.all_engine_barrier()

    def edges_init(self):
        nc, t = self.nc, self.t
        for nm in ("eDP", "ePD", "ePA", "eAP", "eAD", "eDA", "eDS", "eDPq",
                   "ePDq"):
            t[nm] = nc.alloc_semaphore(name="p2" + nm)
        for eng, reg in ((nc.tensor, self.tDP), (nc.tensor, self.tAP),
                         (nc.vector, self.vPD), (nc.vector, self.vAD),
                         (nc.scalar, self.sPA), (nc.scalar, self.sDA),
                         (nc.sync, self.rY), (nc.tensor, self.tDPq),
                         (nc.vector, self.vPDq), (nc.vector, self.vSD)):
            eng.reg_mov(reg, 0)

    def ewait(self, eng, reg, sem, k=1):
        if k:
            eng.reg_add(reg, reg, k)
        eng.wait_ge(sem, reg)

    def sb(self, name, shape, dtype=F32):
        uname = f"{self.stage_name}_{name}"
        self.t[name] = self.scope.enter_context(self.nc.sbuf_tensor(uname, shape, dtype))
        return self.t[name]

    def flush_regs(self):
        for eng in (self.nc.tensor, self.nc.vector, self.nc.scalar,
                    self.nc.gpsimd, self.nc.sync):
            try:
                eng.get_value_cache().barrier()
            except Exception:
                pass

    @contextlib.contextmanager
    def tloop(self, ntile):
        nc = self.nc
        nc.gpsimd.reg_mov(self.rGO, 0)
        nc.sync.reg_mov(self.rSO, 0)
        mx = max(0, self.ntok - TT)
        with nc.Fori(0, ntile) as i:
            self.oG = nc.gpsimd.snap(self.rGO, min_val=0, max_val=mx)
            self.oS = nc.sync.snap(self.rSO, min_val=0, max_val=mx)
            for ov in (self.oG, self.oS):
                try:
                    self._stage_regs.append(ov.val)
                except Exception:
                    pass
            yield i
            nc.gpsimd.reg_add(self.rGO, self.rGO, TT)
            nc.sync.reg_add(self.rSO, self.rSO, TT)

    @contextlib.contextmanager
    def regscope(self):
        nc = self.nc
        prev_stage_regs = self._stage_regs
        self._stage_regs = []
        allocated = []
        orig = nc.alloc_registers
        orig64 = nc.alloc_registers64
        def wrap(*a, **k):
            hs = orig(*a, **k)
            try:
                for h in hs:
                    allocated.append(h)
            except Exception:
                pass
            return hs
        def wrap64(*a, **k):
            hs = orig64(*a, **k)
            try:
                for h in hs:
                    allocated.append(h.lo)
                    allocated.append(h.hi)
            except Exception:
                pass
            return hs
        nc.alloc_registers = wrap
        nc.alloc_registers64 = wrap64
        engs = (nc.tensor, nc.vector, nc.scalar, nc.gpsimd, nc.sync)
        eorig = []
        for e in engs:
            o1, o64 = e.alloc_register, e.alloc_register64
            eorig.append((e, o1, o64))
            def mk1(o1):
                def f(*a, **k):
                    h = o1(*a, **k)
                    allocated.append(h)
                    return h
                return f
            def mk64(o64):
                def f(*a, **k):
                    h = o64(*a, **k)
                    allocated.append(h.lo)
                    allocated.append(h.hi)
                    return h
                return f
            e.alloc_register = mk1(o1)
            e.alloc_register64 = mk64(o64)
        try:
            yield
        finally:
            nc.alloc_registers = orig
            nc.alloc_registers64 = orig64
            for e, o1, o64 in eorig:
                e.alloc_register = o1
                e.alloc_register64 = o64
            for h in allocated + self._stage_regs:
                try:
                    nc.free_register(h)
                except Exception:
                    pass
            self._stage_regs = prev_stage_regs
            self.flush_regs()

    @contextlib.contextmanager
    def stage(self, name):
        old = self.scope
        olds = getattr(self, "stage_name", "")
        self.stage_name = name
        self.scope = contextlib.ExitStack()
        with self.regscope():
            yield
            self.bar()
        self.scope.close()
        self.scope = old
        self.stage_name = olds

    def mmN(self, psum, off, lhsT, rhs, W, start, stop, inc=None):
        a = 0
        while a < W:
            b = min(W, ((off + a) // 512 + 1) * 512 - off)
            inst = self.nc.tensor.matmul(psum[:, off + a:off + b], lhsT,
                                         rhs[:, a:b], start=start, stop=stop)
            if inc is not None and b == W:
                inst.then_inc(inc, 1)
            a = b

    def chunk_ap(self, dram, ti, halo=0, nk=8):
        return dram.rearrange("p (k t) -> p k t", t=self.TP)[
            :, 0:nk, ds(ti * TT + PT - halo, TT + 2 * halo)]

    def chg(self, dram, halo=0, nk=8):
        return dram.rearrange("p (k t) -> p k t", t=self.TP)[
            :, 0:nk, PT - halo:][:, :, ds(self.oG, TT + 2 * halo)]

    def chs(self, dram, nk=8):
        return dram.rearrange("p (k t) -> p k t", t=self.TP)[
            :, 0:nk, PT:][:, :, ds(self.oS, TT)]

    def flatg(self, dram, t_extent, nk):
        return dram.rearrange("p (k t) -> p k t", t=t_extent)[
            :, 0:nk, :][:, :, ds(self.oG, TT)]

    def flats(self, dram, t_extent, nk):
        return dram.rearrange("p (k t) -> p k t", t=t_extent)[
            :, 0:nk, :][:, :, ds(self.oS, TT)]

    def ln_stats(self, xv, nk, W, zero_mean=False):
        nc, V, S, t = self.nc, self.V, self.S, self.t
        psB = t["psB"]
        psS = t["psA"][0:1, :]
        sqb, rowb = t["sqb"], t["rowb"]
        onescf, ones128 = t["onescf"], t["ones128"]
        for r in range(0, nk, 4):
            re = min(r + 4, nk)
            for k in range(r, re):
                S.activation(out=sqb[:, (k - r) * W:(k - r + 1) * W],
                             in_=xv[:, k, 0:W], func=AF.Square)
            self.bar()
            for k in range(r, re):
                if not zero_mean:
                    self.mmN(psS, 0, onescf[:], xv[:, k, 0:W], W,
                             start=(k == 0), stop=(k == nk - 1))
                self.mmN(psS, 1024, onescf[:], sqb[:, (k - r) * W:(k - r + 1) * W],
                         W, start=(k == 0), stop=(k == nk - 1))
            self.bar()
        MU, VAR, NMU = 0, W, 2 * W
        if not zero_mean:
            V.tensor_scalar_mul(rowb[:, MU:MU + W], psS[:, 0:W], 1.0 / D)
            V.tensor_scalar_mul(rowb[:, NMU:NMU + W], rowb[:, MU:MU + W], -1.0)
        V.tensor_scalar_mul(rowb[:, VAR:VAR + W], psS[:, 1024:1024 + W], 1.0 / D)
        if not zero_mean:
            V.tensor_tensor(rowb[:, MU:MU + W], rowb[:, MU:MU + W],
                            rowb[:, MU:MU + W], ALU.mult)
            V.tensor_tensor(rowb[:, VAR:VAR + W], rowb[:, VAR:VAR + W],
                            rowb[:, MU:MU + W], ALU.subtract)
        self.bar()
        S.activation(out=rowb[:, VAR:VAR + W], in_=rowb[:, VAR:VAR + W],
                     func=AF.Ln, bias=t["epsb"][:, 0:1])
        S.activation(out=rowb[:, VAR:VAR + W], in_=rowb[:, VAR:VAR + W],
                     func=AF.Exp, scale=-0.5)
        self.bar()
        if not zero_mean:
            self.mmN(psB, 0, t["ones128f"][:], rowb[:, NMU:NMU + W], W,
                     start=True, stop=True)
        self.mmN(psB, 768, t["ones128f"][:], rowb[:, VAR:VAR + W], W,
                 start=True, stop=True)
        self.bar()
        if not zero_mean:
            V.tensor_copy(t["nmuB"][:, 0:W], psB[:, 0:W])
        V.tensor_copy(t["invsB"][:, 0:W], psB[:, 768:768 + W])
        self.bar()

    def act_quant(self, sv, nk, W, wscale_ap):
        nc, V, S, t = self.nc, self.V, self.S, self.t
        psB = t["psB"]
        fold, ident = t["fold"], t["ident"]
        nchunk = W // 128
        self.bar()
        V.tensor_copy(fold[:, 0:W], sv[:, 0, 0:W])
        V.tensor_copy(t["qtmp"][:, 0:W], sv[:, 0, 0:W])
        for k in range(1, nk):
            V.tensor_tensor(fold[:, 0:W], fold[:, 0:W], sv[:, k, 0:W], ALU.max)
            V.tensor_tensor(t["qtmp"][:, 0:W], t["qtmp"][:, 0:W], sv[:, k, 0:W],
                            ALU.min)
        V.tensor_scalar(t["qtmp"][:, 0:W], t["qtmp"][:, 0:W], -1.0, None, ALU.mult)
        V.tensor_tensor(fold[:, 0:W], fold[:, 0:W], t["qtmp"][:, 0:W], ALU.max)
        self.bar()
        for q in range(nchunk):
            nc.tensor.transpose(psB[:, 768 + q * 128:768 + (q + 1) * 128],
                                fold[:, q * 128:(q + 1) * 128], ident[:])
        self.bar()
        for q in range(nchunk):
            V.tensor_reduce(t["colcl"][:, q:q + 1],
                            psB[:, 768 + q * 128:768 + (q + 1) * 128],
                            mybir.AxisListType.X, ALU.max)
        self.bar()
        V.tensor_scalar_max(t["colcl"][:, 0:nchunk], t["colcl"][:, 0:nchunk], 1e-5)
        self.bar()
        S.activation(out=t["cl2"][:, 0:nchunk], in_=t["colcl"][:, 0:nchunk], func=AF.Ln)
        S.activation(out=t["cl2"][:, 0:nchunk], in_=t["cl2"][:, 0:nchunk],
                     func=AF.Exp, scale=-1.0)
        self.bar()
        V.tensor_scalar_mul(t["cl2"][:, 0:nchunk], t["cl2"][:, 0:nchunk], 127.0)
        V.tensor_single_scalar(t["colcl"][:, 0:nchunk], t["colcl"][:, 0:nchunk],
                               wscale_ap, ALU.mult)
        V.tensor_scalar_mul(t["colcl"][:, 0:nchunk], t["colcl"][:, 0:nchunk],
                            1.0 / 127.0)
        self.bar()
        for q in range(nchunk):
            nc.tensor.matmul(psB[0:1, q * 128:(q + 1) * 128],
                             t["cl2"][:, q:q + 1], t["ident"][:],
                             start=True, stop=True)
            nc.tensor.matmul(psB[0:1, 768 + q * 128:768 + (q + 1) * 128],
                             t["colcl"][:, q:q + 1], t["ident"][:],
                             start=True, stop=True)
        self.bar()
        V.tensor_copy(t["qrow"][:, 0:W], psB[0:1, 0:W])
        V.tensor_copy(t["qrow"][:, 768:768 + W], psB[0:1, 768:768 + W])
        self.bar()
        self.mmN(psB, 0, t["ones128f"][:], t["qrow"][:, 0:W], W, start=True, stop=True)
        self.mmN(psB, 768, t["ones128f"][:], t["qrow"][:, 768:768 + W], W,
                 start=True, stop=True)
        self.bar()
        V.tensor_copy(t["sxb"][:, 0:W], psB[:, 0:W])
        V.tensor_copy(t["oscb"][:, 0:W], psB[:, 768:768 + W])
        self.bar()

    def rint_hop(self, W):
        V, t = self.V, self.t
        kint = t["kint"]
        V.tensor_copy(kint[:, 0:W], t["qtmp"][:, 0:W])
        V.tensor_copy(t["qtmp2"][:, 0:W], kint[:, 0:W])
        V.tensor_tensor(t["qtmp"][:, 0:W], t["qtmp"][:, 0:W],
                        t["qtmp2"][:, 0:W], ALU.subtract)
        V.tensor_scalar(t["qtmp"][:, 0:W], t["qtmp"][:, 0:W], 2.0, None, ALU.mult)
        V.tensor_copy(kint[:, 0:W], t["qtmp"][:, 0:W])
        V.tensor_copy(t["qtmp"][:, 0:W], kint[:, 0:W])
        V.tensor_tensor(t["qtmp2"][:, 0:W], t["qtmp2"][:, 0:W],
                        t["qtmp"][:, 0:W], ALU.add)

    def quantize(self, sv, kt, nk, W):
        V, t = self.V, self.t
        for k in range(nk):
            V.tensor_tensor(t["qtmp"][:, 0:W], sv[:, k, 0:W], t["sxb"][:, 0:W],
                            ALU.mult)
            V.tensor_scalar(t["qtmp"][:, 0:W], t["qtmp"][:, 0:W], 127.0, -127.0,
                            ALU.min, ALU.max)
            V.tensor_scalar(t["qtmp"][:, 0:W], t["qtmp"][:, 0:W], MAGIC, -MAGIC,
                            ALU.add, ALU.add)
            self.rint_hop(W)
            V.tensor_copy(kt[:, k * W:(k + 1) * W], t["qtmp2"][:, 0:W])
        self.bar()

    def quantize2(self, sv, ktv, nk, W):
        V, t = self.V, self.t
        for k in range(nk):
            V.tensor_tensor(t["qtmp"][:, 0:W], sv[:, k, 0:W], t["sxb"][:, 0:W],
                            ALU.mult)
            V.tensor_scalar(t["qtmp"][:, 0:W], t["qtmp"][:, 0:W], 127.0, -127.0,
                            ALU.min, ALU.max)
            V.tensor_scalar(t["qtmp"][:, 0:W], t["qtmp"][:, 0:W], MAGIC, -MAGIC,
                            ALU.add, ALU.add)
            self.rint_hop(W)
            V.tensor_copy(ktv[:, k, 0:W], t["qtmp2"][:, 0:W])
        self.bar()


def emit_wprep(p, wsrc, wqd, sc_key, nk, nm, tag):
    nc, t, V, S = p.nc, p.t, p.V, p.S
    ld, st = t["ld_w" + tag], t["st_w" + tag]
    M = nm * 128
    with p.stage("wprep" + tag):
        wtmp = p.sb("wtmp", [128, M])
        wtmp2 = p.sb("wtmp2", [128, M])
        wint = p.sb("wint", [128, M], I32)
        wbf = p.sb("wbf", [128, M], BF16)
        acc = p.sb("wacc", [128, 2])
        sc2 = p.sb("wsc2", [1, 2])
        swcol = p.sb("swcol", [128, 1])
        nc.vector.memset(acc[:, 0:1], 0.0)
        nc.vector.drain()
        p.bar()
        for k in range(nk):
            nc.gpsimd.dma_start(out=wtmp[:], in_=wsrc[k * 128:(k + 1) * 128, :]
                                ).then_inc(ld, 16)
            nc.vector.wait_ge(ld, 16 * (k + 1))
            p.bar()
            V.tensor_reduce(acc[:, 1:2], wtmp[:], mybir.AxisListType.X,
                            ALU.add, apply_absolute_value=True)
            V.tensor_tensor(acc[:, 0:1], acc[:, 0:1], acc[:, 1:2], ALU.add)
            p.bar()
        p.bar()
        nc.tensor.matmul(t["psB"][0:1, 0:1], acc[:, 0:1], t["onescf"][:],
                         start=True, stop=True)
        p.bar()
        V.tensor_scalar_mul(sc2[:, 0:1], t["psB"][0:1, 0:1], 1.0 / (nk * 128 * M))
        V.tensor_scalar_max(sc2[:, 0:1], sc2[:, 0:1], 1e-5)
        V.tensor_copy(t[sc_key][0:1, 0:1], sc2[:, 0:1])
        p.bar()
        S.activation(out=sc2[:, 1:2], in_=sc2[:, 0:1], func=AF.Ln)
        S.activation(out=sc2[:, 1:2], in_=sc2[:, 1:2], func=AF.Exp, scale=-1.0)
        p.bar()
        nc.tensor.matmul(t["psB"][:, 0:1], t["ones128f"][:], sc2[:, 1:2],
                         start=True, stop=True)
        nc.tensor.matmul(t["psB"][:, 512:513], t["ones128f"][:], sc2[:, 0:1],
                         start=True, stop=True)
        p.bar()
        V.tensor_copy(swcol[:], t["psB"][:, 0:1])
        V.tensor_copy(t[sc_key + "c"][:, 0:1], t["psB"][:, 512:513])
        p.bar()
        for k in range(nk):
            nc.gpsimd.dma_start(out=wtmp[:], in_=wsrc[k * 128:(k + 1) * 128, :]
                                ).then_inc(ld, 16)
            nc.vector.wait_ge(ld, 16 * (nk + k + 1))
            p.bar()
            V.tensor_single_scalar(wtmp[:], wtmp[:], swcol[:, 0:1], ALU.mult)
            V.tensor_scalar(wtmp[:], wtmp[:], 1.0, -1.0, ALU.min, ALU.max)
            V.tensor_scalar(wtmp[:], wtmp[:], MAGIC, -MAGIC, ALU.add, ALU.add)
            V.tensor_copy(wint[:], wtmp[:])
            V.tensor_copy(wtmp2[:], wint[:])
            V.tensor_tensor(wtmp[:], wtmp[:], wtmp2[:], ALU.subtract)
            V.tensor_scalar(wtmp[:], wtmp[:], 2.0, None, ALU.mult)
            V.tensor_copy(wint[:], wtmp[:])
            V.tensor_copy(wtmp[:], wint[:])
            V.tensor_tensor(wtmp2[:], wtmp2[:], wtmp[:], ALU.add)
            V.tensor_copy(wbf[:], wtmp2[:])
            p.bar()
            nc.sync.dma_start(
                out=wqd.rearrange("p (m k c) -> p m k c", k=nk, c=128)[:, :, k, :],
                in_=wbf.rearrange("p (m c) -> p m c", c=128)).then_inc(st, 16)
            nc.sync.wait_ge(st, 16 * (k + 1))
        p.bar()


def emit_s1(p):
    nc, t, V, S = p.nc, p.t, p.V, p.S
    ntile = p.ntok // TT
    ld, st = t["ld_s1"], t["st_s1"]
    psB = t["psB"]
    W = TT
    BCB = 768
    with p.stage("s1"):
        xt = p.sb("s1x", [128, 8 * W])
        ht = p.sb("s1h", [128, 8 * W])
        yt = p.sb("s1y", [128, 8 * W])
        stg = p.sb("s1stg", [128, 8 * W])
        nc.gpsimd.reg_mov(p.rG, 0)
        nc.vector.reg_mov(p.rV, 32)
        with p.tloop(ntile) as i:
            nc.gpsimd.wait_ge(st, p.rG)
            nc.gpsimd.reg_add(p.rG, p.rG, 16)
            nc.gpsimd.dma_start(out=xt.rearrange("p (k t) -> p k t", t=W),
                                in_=t["xb"].rearrange("(k p) t -> p k t", p=128)[
                                    :, :, ds(p.oG, TT)]).then_inc(ld, 16)
            nc.gpsimd.dma_start(out=ht.rearrange("p (k t) -> p k t", t=W),
                                in_=p.flatg(t["hd"], p.ntok, 8)).then_inc(ld, 16)
            nc.vector.wait_ge(ld, p.rV)
            nc.vector.reg_add(p.rV, p.rV, 32)
            p.bar()
            xv = xt.rearrange("p (k t) -> p k t", t=W)
            yv = yt.rearrange("p (k t) -> p k t", t=W)
            p.ln_stats(xv, 8, W)
            for k in range(8):
                V.tensor_tensor(t["qtmp"][:, 0:W], xv[:, k, :], t["nmuB"][:, 0:W], ALU.add)
                V.tensor_tensor(t["qtmp"][:, 0:W], t["qtmp"][:, 0:W],
                                t["invsB"][:, 0:W], ALU.mult)
                V.tensor_tensor(yv[:, k, :], t["qtmp"][:, 0:W],
                                ht[:, k * W:(k + 1) * W], ALU.add)
            p.bar()
            p.ln_stats(yv, 8, W)
            for k in range(8):
                V.tensor_tensor(t["qtmp"][:, 0:W], yv[:, k, :], t["nmuB"][:, 0:W], ALU.add)
                V.tensor_tensor(t["qtmp"][:, 0:W], t["qtmp"][:, 0:W],
                                t["invsB"][:, 0:W], ALU.mult)
                V.tensor_scalar(t["qtmp"][:, 0:W], t["qtmp"][:, 0:W],
                                t["gammab"][:, k:k + 1], t["betab"][:, k:k + 1],
                                ALU.mult, ALU.add)
                V.tensor_tensor(stg[:, k * W:(k + 1) * W], xv[:, k, :],
                                t["qtmp"][:, 0:W], ALU.add)
            p.bar()
            nc.sync.dma_start(out=p.chs(t["x1d"]),
                              in_=stg.rearrange("p (k t) -> p k t", t=W)
                              ).then_inc(st, 16)
        nc.sync.wait_ge(st, 16 * ntile)


def emit_ffn1(p, ind_key, tag):
    nc, t, V, S = p.nc, p.t, p.V, p.S
    ntile = p.ntok // TT
    ld, lw, st = t["ld_" + tag], t["lw_" + tag], t["st_" + tag]
    W = WF
    WC = TT
    psA, psB = t["psA"], t["psB"]
    BCB = 768
    with p.stage(tag):
        w1q = p.sb("w1qs", [128, 8 * DF], BF16)
        xt = p.sb("f1x", [128, 8 * W])
        s2t = xt
        k1 = p.sb("f1k", [128, 8 * W], BF16)
        hmid = p.sb("f1h", [128, 32 * W], BF16)
        hmv = hmid.rearrange("p (m t) -> p m t", t=W)
        oscrow = p.sb("f1osc", [1, WC])
        nc.gpsimd.dma_start(out=w1q[:], in_=t["w1qd"][:]).then_inc(lw, 16)
        nc.tensor.wait_ge(lw, 16)
        p.bar()
        nc.gpsimd.reg_mov(p.rG, 0)
        nc.vector.reg_mov(p.rV, 16)
        with p.tloop(ntile) as i:
            nc.gpsimd.wait_ge(st, p.rG)
            nc.gpsimd.reg_add(p.rG, p.rG, 32)
            nc.gpsimd.dma_start(out=xt.rearrange("p (k t) -> p k t", t=W),
                                in_=p.chg(t[ind_key], halo=HE_FFN)).then_inc(ld, 16)
            nc.vector.wait_ge(ld, p.rV)
            nc.vector.reg_add(p.rV, p.rV, 16)
            p.bar()
            xv = xt.rearrange("p (k t) -> p k t", t=W)
            sv = xv
            p.ln_stats(xv, 8, W)
            for k in range(8):
                V.tensor_tensor(t["qtmp"][:, 0:W], xv[:, k, :], t["nmuB"][:, 0:W], ALU.add)
                V.tensor_tensor(sv[:, k, :], t["qtmp"][:, 0:W],
                                t["invsB"][:, 0:W], ALU.mult)
            p.bar()
            p.act_quant(sv, 8, W, t["w1scc"][:, 0:1])
            p.quantize(sv, k1, 8, W)
            for m in range(32):
                pp = m % 2
                if pp == 0 and m > 0:
                    p.ewait(nc.tensor, p.tDPq, t["eDPq"])
                for k in range(8):
                    p.mmN(psA, pp * 1024,
                          w1q[:, (m * 8 + k) * 128:(m * 8 + k + 1) * 128],
                          k1[:, k * W:(k + 1) * W], W,
                          start=(k == 0), stop=(k == 7),
                          inc=(t["ePDq"] if (k == 7 and pp == 1) else None))
                if pp == 1:
                    p.ewait(nc.vector, p.vPDq, t["ePDq"])
                    for mm in (m - 1, m):
                        inst = V.tensor_tensor(
                            hmid[:, mm * W:(mm + 1) * W],
                            psA[:, (mm % 2) * 1024:(mm % 2) * 1024 + W],
                            t["oscb"][:, 0:W], ALU.mult)
                    if m < 31:
                        inst.then_inc(t["eDPq"], 1)
            p.bar()
            CV = W - 2
            sqb = t["sqb"]
            for m0 in range(0, 32, 4):
                for m in range(m0, m0 + 4):
                    sl = sqb[:, (m % 4) * 768:(m % 4) * 768 + CV]
                    V.tensor_scalar(sl, hmid[:, m * W:m * W + CV],
                                    t["convw"][:, 3 * m:3 * m + 1],
                                    t["convb"][:, m:m + 1], ALU.mult, ALU.add)
                    V.scalar_tensor_tensor(sl, hmid[:, m * W + 1:m * W + 1 + CV],
                                           t["convw"][:, 3 * m + 1:3 * m + 2],
                                           sl, ALU.mult, ALU.add)
                    V.scalar_tensor_tensor(sl, hmid[:, m * W + 2:m * W + 2 + CV],
                                           t["convw"][:, 3 * m + 2:3 * m + 3],
                                           sl, ALU.mult, ALU.add)
                p.bar()
                for m in range(m0, m0 + 4):
                    S.activation(out=hmv[:, m, 0:WC],
                                 in_=sqb[:, (m % 4) * 768 + HE_FFN - 1:
                                         (m % 4) * 768 + HE_FFN - 1 + WC],
                                 func=AF.Silu)
                p.bar()
            p.act_quant(hmv[:, :, 0:WC], 32, WC, t["w2scc"][:, 0:1])
            p.quantize2(hmv[:, :, 0:WC], hmv[:, :, 0:WC], 32, WC)
            V.tensor_copy(oscrow[:, 0:WC], t["qrow"][:, 768:768 + WC])
            p.bar()
            nc.sync.dma_start(
                out=p.flats(t["k2d"], p.ntok, 32),
                in_=hmv[:, :, 0:WC]).then_inc(st, 16)
            nc.sync.dma_start(
                out=t["oscd"][:, ds(p.oS, WC)], in_=oscrow[:, 0:WC]
                ).then_inc(st, 16)
        nc.sync.wait_ge(st, 32 * ntile)


def emit_ffn2(p, ind_key, outd_key, tag):
    nc, t, V, S = p.nc, p.t, p.V, p.S
    ntile = p.ntok // TT
    ld, lw, st = t["ld_" + tag], t["lw_" + tag], t["st_" + tag]
    WC = TT
    psA, psB = t["psA"], t["psB"]
    with p.stage(tag):
        w2q = p.sb("w2qs", [128, 32 * D], BF16)
        k2 = p.sb("f2k", [128, 32 * WC], BF16)
        xc = p.sb("f2x", [128, 8 * WC])
        stg = p.sb("f2stg", [128, 8 * WC])
        oscrow = p.sb("f2osc", [1, WC])
        nc.gpsimd.dma_start(out=w2q[:], in_=t["w2qd"][:]).then_inc(lw, 16)
        nc.tensor.wait_ge(lw, 16)
        p.bar()
        nc.gpsimd.reg_mov(p.rG, 0)
        nc.vector.reg_mov(p.rV, 48)
        with p.tloop(ntile) as i:
            nc.gpsimd.wait_ge(st, p.rG)
            nc.gpsimd.reg_add(p.rG, p.rG, 16)
            nc.gpsimd.dma_start(out=k2.rearrange("p (k t) -> p k t", t=WC),
                                in_=p.flatg(t["k2d"], p.ntok, 32)).then_inc(ld, 16)
            nc.gpsimd.dma_start(out=xc.rearrange("p (k t) -> p k t", t=WC),
                                in_=p.chg(t[ind_key])).then_inc(ld, 16)
            nc.gpsimd.dma_start(out=oscrow[:, 0:WC],
                                in_=t["oscd"][:, ds(p.oG, WC)]).then_inc(ld, 16)
            nc.vector.wait_ge(ld, p.rV)
            nc.vector.reg_add(p.rV, p.rV, 48)
            p.bar()
            p.mmN(psB, 0, t["ones128f"][:], oscrow[:, 0:WC], WC, start=True, stop=True)
            p.bar()
            V.tensor_copy(t["oscb"][:, 0:WC], psB[:, 0:WC])
            p.bar()
            for m in range(8):
                pp = m % 2
                if m >= 2:
                    p.ewait(nc.tensor, p.tDPq, t["eDPq"])
                for k in range(32):
                    p.mmN(psA, pp * 1024,
                          w2q[:, (m * 32 + k) * 128:(m * 32 + k + 1) * 128],
                          k2[:, k * WC:(k + 1) * WC], WC,
                          start=(k == 0), stop=(k == 31),
                          inc=(t["ePDq"] if k == 31 else None))
                p.ewait(nc.vector, p.vPDq, t["ePDq"])
                V.tensor_tensor(t["qtmp"][:, 0:WC], psA[:, pp * 1024:pp * 1024 + WC],
                                t["oscb"][:, 0:WC], ALU.mult)
                inst = V.tensor_tensor(stg[:, m * WC:(m + 1) * WC],
                                       t["qtmp"][:, 0:WC],
                                       xc[:, m * WC:(m + 1) * WC], ALU.add)
                if m < 6:
                    inst.then_inc(t["eDPq"], 1)
            p.bar()
            if outd_key == "outd":
                nc.sync.dma_start(out=p.flats(t[outd_key], p.ntok, 8),
                                  in_=stg.rearrange("p (k t) -> p k t", t=WC)
                                  ).then_inc(st, 16)
            else:
                nc.sync.dma_start(out=p.chs(t[outd_key]),
                                  in_=stg.rearrange("p (k t) -> p k t", t=WC)
                                  ).then_inc(st, 16)
        nc.sync.wait_ge(st, 16 * ntile)


def emit_s3a(p):
    nc, t, V, S = p.nc, p.t, p.V, p.S
    ntile = p.ntok // TT
    ld, lw, st = t["ld_s3a"], t["lw_s3a"], t["st_s3a"]
    W = WA
    WC = TT
    psA, psB, psD = t["psA"], t["psB"], t["psD"]
    eDP, ePD, ePA, eAP = t["eDP"], t["ePD"], t["ePA"], t["eAP"]
    eAD, eDA, eDS = t["eAD"], t["eDA"], t["eDS"]
    eDPq, ePDq = t["eDPq"], t["ePDq"]
    with p.stage("s3a"):
        qkvw = p.sb("qkvws", [128, 8 * 3 * D], BF16)
        xt = p.sb("a_x", [128, 8 * W])
        xnt = p.sb("a_xn", [128, 8 * W], BF16)
        qkvT = p.sb("a_qkv", [128, 24 * W], BF16)
        kst = p.sb("a_kst", [128, W], BF16)
        vst = p.sb("a_vst", [128, W], BF16)
        qst = p.sb("a_qst", [128, 512], BF16)
        vT = p.sb("a_vT", [128, 768], BF16)
        expS = p.sb("a_exp", [128, 4 * 384], BF16)
        rsA = p.sb("a_rsA", [1, 512])
        rsB = p.sb("a_rsB", [1, 512])
        rn = p.sb("a_rn", [64, 512])
        attnT = p.sb("a_at", [64, 2 * WC], BF16)
        nc.gpsimd.dma_start(out=qkvw[:], in_=t["qkvwd"][:]).then_inc(lw, 16)
        nc.tensor.wait_ge(lw, 16)
        p.bar()
        nc.gpsimd.reg_mov(p.rG, 0)
        nc.vector.reg_mov(p.rV, 16)
        nc.vector.sem_inc(eDA, 1)
        nc.tensor.sem_inc(ePA, 1)
        with p.tloop(ntile) as i:
            nc.gpsimd.wait_ge(st, p.rG)
            nc.gpsimd.reg_add(p.rG, p.rG, 128)
            nc.sync.reg_mov(p.rY2, 0)
            nc.gpsimd.dma_start(out=xt.rearrange("p (k t) -> p k t", t=W),
                                in_=p.chg(t["x2d"], halo=HE_ATT)).then_inc(ld, 16)
            nc.vector.wait_ge(ld, p.rV)
            nc.vector.reg_add(p.rV, p.rV, 16)
            p.bar()
            xv = xt.rearrange("p (k t) -> p k t", t=W)
            p.ln_stats(xv, 8, W)
            for k in range(8):
                V.tensor_tensor(t["qtmp"][:, 0:W], xv[:, k, :], t["nmuB"][:, 0:W], ALU.add)
                V.tensor_tensor(xv[:, k, :], t["qtmp"][:, 0:W],
                                t["invsB"][:, 0:W], ALU.mult)
            p.bar()
            p.ln_stats(xv, 8, W, zero_mean=True)
            for k in range(8):
                V.tensor_tensor(t["qtmp"][:, 0:W], xv[:, k, :],
                                t["invsB"][:, 0:W], ALU.mult)
                V.tensor_scalar(xnt[:, k * W:(k + 1) * W], t["qtmp"][:, 0:W],
                                t["ln1wb"][:, k:k + 1], t["ln1bb"][:, k:k + 1],
                                ALU.mult, ALU.add)
            p.bar()
            for m in range(24):
                pp = m % 2
                if pp == 0 and m > 0:
                    p.ewait(nc.tensor, p.tDPq, eDPq)
                for k in range(8):
                    p.mmN(psA, pp * 1024,
                          qkvw[:, (m * 8 + k) * 128:(m * 8 + k + 1) * 128],
                          xnt[:, k * W:(k + 1) * W], W,
                          start=(k == 0), stop=(k == 7),
                          inc=(ePDq if (k == 7 and pp == 1) else None))
                if pp == 1:
                    p.ewait(nc.vector, p.vPDq, ePDq)
                    for mm2 in (m - 1, m):
                        sc = 0.125 if mm2 < 8 else 1.0
                        inst = V.tensor_scalar(
                            qkvT[:, mm2 * W:(mm2 + 1) * W],
                            psA[:, (mm2 % 2) * 1024:(mm2 % 2) * 1024 + W],
                            t["qkvbb"][:, mm2:mm2 + 1], sc, ALU.add, ALU.mult)
                    if m < 23:
                        inst.then_inc(eDPq, 1)
            nc.vector.reg_mov(p.rVo, 0)
            expv = expS.rearrange("p (q c) -> p q c", c=384)
            psAs = psA.rearrange("p (q c) -> p q c", c=512)
            with nc.Fori(0, NH // 2) as hp:
                oV = nc.vector.snap(p.rVo, min_val=0, max_val=7 * W)
                oY2 = nc.sync.snap(p.rY2, min_val=0, max_val=14)
                p._stage_regs += [oV.val, oY2.val]
                p.ewait(nc.vector, p.vPD, ePD, k=0)
                V.tensor_copy(kst[:], qkvT[:, ds(oV + 8 * W, W)])
                V.tensor_copy(vst[:], qkvT[:, ds(oV + 16 * W, W)])
                inst = V.tensor_copy(qst[:], qkvT[:, ds(oV + 128, 512)])
                inst.then_inc(eDP, 1)
                for b6 in (0, 1):
                    base = b6 * 64
                    p.ewait(nc.tensor, p.tDP, eDP, k=1)
                    for b in range(6):
                        nc.tensor.transpose(
                            psD[:, b * 64:(b + 1) * 64],
                            vst[base:base + 64, b * 128:(b + 1) * 128],
                            t["identb"][base:base + 64, base:base + 64])
                    nc.tensor.drain()
                    nc.tensor.sem_inc(ePD, 1)
                    p.ewait(nc.vector, p.vPD, ePD, k=1)
                    inst = V.tensor_copy(vT[:, b6 * 384:(b6 + 1) * 384],
                                         psD[:, 0:384])
                    if b6 == 0:
                        inst.then_inc(eDP, 1)
                for b6 in (0, 1):
                    base = b6 * 64
                    rs = rsA if b6 == 0 else rsB
                    p.ewait(nc.tensor, p.tAP, eAP, k=0)
                    for qb in range(4):
                        for b in range(3):
                            inst = nc.tensor.matmul(
                                psA[:, qb * 512 + b * 128:qb * 512 + (b + 1) * 128],
                                kst[base:base + 64, (qb + b) * 128:(qb + b + 1) * 128],
                                qst[base:base + 64, qb * 128:(qb + 1) * 128],
                                start=True, stop=True)
                    inst.then_inc(ePA, 1)
                    p.ewait(nc.scalar, p.sPA, ePA, k=2)
                    p.ewait(nc.scalar, p.sDA, eDA)
                    inst = S.activation(out=expv[:, 0:4, 0:384], in_=psAs[:, 0:4, 0:384],
                                        func=AF.Exp)
                    inst.then_inc(eAD, 1)
                    nc.scalar.sem_inc(eAP, 1)
                    p.ewait(nc.vector, p.vAD, eAD)
                    with nc.vector.If(i == 0):
                        nc.vector.memset(expS[:, 0:128], 0.0)
                        nc.vector.drain()
                    with nc.vector.If(i == ntile - 1):
                        nc.vector.memset(expS[:, 3 * 384 + 256:4 * 384], 0.0)
                        nc.vector.drain()
                    nc.vector.sem_inc(eDP, 1)
                    nc.vector.sem_inc(eDA, 1)
                    p.ewait(nc.tensor, p.tDP, eDP)
                    p.ewait(nc.tensor, p.tAP, eAP, k=0)
                    for qb in range(4):
                        for b in range(3):
                            inst = nc.tensor.matmul(
                                psA[0:1, qb * 512 + 384:qb * 512 + 512],
                                t["onescol"][:],
                                expS[:, qb * 384 + b * 128:qb * 384 + (b + 1) * 128],
                                start=(b == 0), stop=(b == 2))
                    inst.then_inc(ePA, 1)
                    for qb in range(4):
                        for b in range(3):
                            inst = nc.tensor.matmul(
                                psB[0:64, qb * 128:(qb + 1) * 128],
                                vT[:, (b6 * 6 + qb + b) * 64:(b6 * 6 + qb + b + 1) * 64],
                                expS[:, qb * 384 + b * 128:qb * 384 + (b + 1) * 128],
                                start=(b == 0), stop=(b == 2))
                    inst.then_inc(ePA, 1)
                    p.ewait(nc.scalar, p.sPA, ePA)
                    S.activation(out=rs.rearrange("o (q c) -> o q c", c=128),
                                 in_=psAs[0:1, 0:4, 384:512], func=AF.Ln)
                    inst = S.activation(out=rs[:, 0:512], in_=rs[:, 0:512],
                                        func=AF.Exp, scale=-1.0)
                    inst.then_inc(eAP, 1)
                    nc.scalar.sem_inc(eAP, 1)
                    p.ewait(nc.tensor, p.tAP, eAP, k=3)
                    inst = nc.tensor.matmul(psB[0:64, 512:1024], t["ones64f"][:],
                                            rs[:, 0:512], start=True, stop=True)
                    inst.then_inc(ePD, 1)
                    p.ewait(nc.vector, p.vPD, ePD)
                    V.tensor_copy(rn[:, 0:512], psB[0:64, 512:1024])
                    if b6 == 0:
                        nc.vector.wait_ge(st, p.vSD)
                        nc.vector.reg_add(p.vSD, p.vSD, 16)
                    inst = V.tensor_tensor(attnT[0:64, b6 * WC:(b6 + 1) * WC],
                                           psB[0:64, 0:512], rn[:, 0:512], ALU.mult)
                    if b6 == 1:
                        nc.vector.sem_inc(eDS, 1)
                p.ewait(nc.sync, p.rY, eDS)
                nc.sync.dma_start(
                    out=t["attd"].rearrange("p (k t) -> p k t", t=p.ntok)[
                        :, ds(oY2, 2), ds(p.oS, TT)],
                    in_=attnT.rearrange("p (c t) -> p c t", t=WC)
                    ).then_inc(st, 16)
                nc.vector.reg_add(p.rVo, p.rVo, W)
                nc.sync.reg_add(p.rY2, p.rY2, 2)
        nc.sync.wait_ge(st, 128 * ntile)


def emit_s3a2(p):
    nc, t, V, S = p.nc, p.t, p.V, p.S
    ntile = p.ntok // TT
    ld, lw, st = t["ld_s3a2"], t["lw_s3a2"], t["st_s3a2"]
    WC = TT
    psA = t["psA"]
    with p.stage("s3a2"):
        projw = p.sb("projws", [64, 16 * D], BF16)
        attnT = p.sb("p_at", [64, 16 * WC], BF16)
        x2c = p.sb("p_x2", [128, 8 * WC])
        s3c = p.sb("p_s3", [128, 8 * WC])
        stg = p.sb("p_stg", [128, 8 * WC])
        nc.gpsimd.dma_start(out=projw[:], in_=t["projwd"][:]).then_inc(lw, 16)
        nc.tensor.wait_ge(lw, 16)
        p.bar()
        nc.gpsimd.reg_mov(p.rG, 0)
        nc.vector.reg_mov(p.rV, 32)
        with p.tloop(ntile) as i:
            nc.gpsimd.wait_ge(st, p.rG)
            nc.gpsimd.reg_add(p.rG, p.rG, 16)
            nc.gpsimd.dma_start(out=attnT.rearrange("p (c t) -> p c t", t=WC),
                                in_=p.flatg(t["attd"], p.ntok, 16)).then_inc(ld, 16)
            nc.gpsimd.dma_start(out=x2c.rearrange("p (k t) -> p k t", t=WC),
                                in_=p.chg(t["x2d"])).then_inc(ld, 16)
            nc.vector.wait_ge(ld, p.rV)
            nc.vector.reg_add(p.rV, p.rV, 32)
            p.bar()
            x2v = x2c.rearrange("p (k t) -> p k t", t=WC)
            p.ln_stats(x2v, 8, WC)
            for k in range(8):
                V.tensor_tensor(t["qtmp"][:, 0:WC], x2v[:, k, :],
                                t["nmuB"][:, 0:WC], ALU.add)
                V.tensor_tensor(s3c[:, k * WC:(k + 1) * WC], t["qtmp"][:, 0:WC],
                                t["invsB"][:, 0:WC], ALU.mult)
            p.bar()
            for m in range(8):
                pp = m % 2
                for kh in range(16):
                    p.mmN(psA, pp * 1024,
                          projw[:, (m * 16 + kh) * 128:(m * 16 + kh + 1) * 128],
                          attnT[:, kh * WC:(kh + 1) * WC], WC,
                          start=(kh == 0), stop=(kh == 15))
                p.bar()
                V.tensor_scalar(t["qtmp"][:, 0:WC], psA[:, pp * 1024:pp * 1024 + WC],
                                t["projbb"][:, m:m + 1], None, ALU.add)
                V.tensor_tensor(stg[:, m * WC:(m + 1) * WC], t["qtmp"][:, 0:WC],
                                s3c[:, m * WC:(m + 1) * WC], ALU.add)
                p.bar()
            nc.sync.dma_start(out=p.chs(t["xb1d"]),
                              in_=stg.rearrange("p (k t) -> p k t", t=WC)
                              ).then_inc(st, 16)
        nc.sync.wait_ge(st, 16 * ntile)


def emit_s3b1(p):
    nc, t, V, S = p.nc, p.t, p.V, p.S
    ntile = p.ntok // TT
    ld, lw, st = t["ld_s3b1"], t["lw_s3b1"], t["st_s3b1"]
    W = TT
    psA, psB = t["psA"], t["psB"]
    BCB = 768
    with p.stage("s3b1"):
        fc1w = p.sb("fc1ws", [128, 8 * DF], BF16)
        xt = p.sb("b1x", [128, 8 * W])
        y2t = p.sb("b1y", [128, 8 * W], BF16)
        glb = p.sb("b1g", [128, 32 * W], BF16)
        u8 = p.sb("b1u8", [128, 8 * W])
        xa = p.sb("b1xa", [128, W])
        xc = p.sb("b1xc", [128, W])
        nh = p.sb("b1nh", [128, W])
        dh = p.sb("b1dh", [128, W])
        eb = p.sb("b1e", [128, W])
        nc.gpsimd.dma_start(out=fc1w[:], in_=t["fc1wd"][:]).then_inc(lw, 16)
        nc.tensor.wait_ge(lw, 16)
        p.bar()
        nc.gpsimd.reg_mov(p.rG, 0)
        nc.vector.reg_mov(p.rV, 16)
        with p.tloop(ntile) as i:
            nc.gpsimd.wait_ge(st, p.rG)
            nc.gpsimd.reg_add(p.rG, p.rG, 16)
            nc.gpsimd.dma_start(out=xt.rearrange("p (k t) -> p k t", t=W),
                                in_=p.chg(t["xb1d"])).then_inc(ld, 16)
            nc.vector.wait_ge(ld, p.rV)
            nc.vector.reg_add(p.rV, p.rV, 16)
            p.bar()
            xv = xt.rearrange("p (k t) -> p k t", t=W)
            p.ln_stats(xv, 8, W)
            for k in range(8):
                V.tensor_tensor(t["qtmp"][:, 0:W], xv[:, k, :], t["nmuB"][:, 0:W], ALU.add)
                V.tensor_tensor(t["qtmp"][:, 0:W], t["qtmp"][:, 0:W],
                                t["invsB"][:, 0:W], ALU.mult)
                V.tensor_scalar(y2t[:, k * W:(k + 1) * W], t["qtmp"][:, 0:W],
                                t["ln2wb"][:, k:k + 1], t["ln2bb"][:, k:k + 1],
                                ALU.mult, ALU.add)
            p.bar()
            glv = glb.rearrange("p (m t) -> p m t", t=W)
            u8v = u8.rearrange("p (m t) -> p m t", t=W)
            nc.vector.reg_mov(p.rVo, 0)
            nc.scalar.reg_mov(p.rS, 0)
            for g in range(4):
                for m8 in range(8):
                    m = g * 8 + m8
                    for k in range(8):
                        p.mmN(psA, (m8 % 4) * 512,
                              fc1w[:, (m * 8 + k) * 128:(m * 8 + k + 1) * 128],
                              y2t[:, k * W:(k + 1) * W], W,
                              start=(k == 0), stop=(k == 7))
                    if m8 % 4 == 3:
                        p.bar()
                        for q4 in range(4):
                            V.tensor_scalar(u8v[:, m8 - 3 + q4, 0:W],
                                            psA[:, q4 * 512:q4 * 512 + W],
                                            t["fc1bb"][:, m - 3 + q4:m - 2 + q4],
                                            None, ALU.add)
                        p.bar()
                gs = p.regscope(); gs.__enter__()
                with nc.Fori(0, 8) as mi:
                    oD = nc.vector.snap(p.rVo, min_val=g * 8, max_val=g * 8 + 7)
                    oA = nc.scalar.snap(p.rS, min_val=0, max_val=7)
                    p._stage_regs += [oD.val, oA.val]
                    uS = u8v[:, ds(oA, 1), 0:W]
                    uV = u8v[:, ds(oD - g * 8, 1), 0:W]
                    S.activation(out=xa[:], in_=uS, func=AF.Abs)
                    S.activation(out=nh[:], in_=uS, func=AF.Square)
                    S.activation(out=eb[:], in_=nh[:], func=AF.Exp, scale=-0.5)
                    p.bar()
                    V.tensor_scalar(xc[:], xa[:], 16.0, None, ALU.min)
                    V.tensor_scalar(nh[:], xc[:], GN[4][0], GN[3][0], ALU.mult, ALU.add)
                    V.tensor_scalar(nh[:], nh[:], GN[3][1], None, ALU.add)
                    for jj in (2, 1):
                        V.tensor_tensor(nh[:], nh[:], xc[:], ALU.mult)
                        V.tensor_scalar(nh[:], nh[:], GN[jj][0], GN[jj][1], ALU.add, ALU.add)
                    V.tensor_tensor(nh[:], nh[:], xc[:], ALU.mult)
                    V.tensor_scalar(nh[:], nh[:], 1.0, None, ALU.add)
                    V.tensor_scalar(dh[:], xc[:], GD[5][0], GD[4][0], ALU.mult, ALU.add)
                    for jj in (3, 2, 1):
                        V.tensor_tensor(dh[:], dh[:], xc[:], ALU.mult)
                        V.tensor_scalar(dh[:], dh[:], GD[jj][0], GD[jj][1], ALU.add, ALU.add)
                    V.tensor_tensor(dh[:], dh[:], xc[:], ALU.mult)
                    V.tensor_scalar(dh[:], dh[:], 1.0, None, ALU.add)
                    V.reciprocal(xc[:], dh[:])
                    V.tensor_tensor(nh[:], nh[:], xc[:], ALU.mult)
                    V.tensor_tensor(eb[:], eb[:], nh[:], ALU.mult)
                    V.tensor_scalar(eb[:], eb[:], -0.5, 0.5, ALU.mult, ALU.add)
                    V.tensor_tensor(eb[:], eb[:], xa[:], ALU.mult)
                    V.tensor_scalar(nh[:], uV, 0.5, None, ALU.mult)
                    V.tensor_tensor(glv[:, ds(oD, 1), 0:W], nh[:], eb[:], ALU.add)
                    p.bar()
                    nc.vector.reg_add(p.rVo, p.rVo, 1)
                    nc.scalar.reg_add(p.rS, p.rS, 1)
                gs.__exit__(None, None, None)
                nc.scalar.reg_mov(p.rS, 0)
            nc.sync.dma_start(
                out=p.flats(t["gld"], p.ntok, 32),
                in_=glb.rearrange("p (k t) -> p k t", t=W)).then_inc(st, 16)
        nc.sync.wait_ge(st, 16 * ntile)


def emit_s3b2(p):
    nc, t, V, S = p.nc, p.t, p.V, p.S
    ntile = p.ntok // TT
    ld, lw, st = t["ld_s3b2"], t["lw_s3b2"], t["st_s3b2"]
    W = TT
    psA = t["psA"]
    with p.stage("s3b2"):
        fc2w = p.sb("fc2ws", [128, 32 * D], BF16)
        glb = p.sb("b2g", [128, 32 * W], BF16)
        xb1 = p.sb("b2xb1", [128, 8 * W])
        x2 = p.sb("b2x2", [128, 8 * W])
        stg = p.sb("b2stg", [128, 8 * W])
        nc.gpsimd.dma_start(out=fc2w[:], in_=t["fc2wd"][:]).then_inc(lw, 16)
        nc.tensor.wait_ge(lw, 16)
        p.bar()
        nc.gpsimd.reg_mov(p.rG, 0)
        nc.vector.reg_mov(p.rV, 48)
        with p.tloop(ntile) as i:
            nc.gpsimd.wait_ge(st, p.rG)
            nc.gpsimd.reg_add(p.rG, p.rG, 16)
            nc.gpsimd.dma_start(out=glb.rearrange("p (k t) -> p k t", t=W),
                                in_=p.flatg(t["gld"], p.ntok, 32)).then_inc(ld, 16)
            nc.gpsimd.dma_start(out=xb1.rearrange("p (k t) -> p k t", t=W),
                                in_=p.chg(t["xb1d"])).then_inc(ld, 16)
            nc.gpsimd.dma_start(out=x2.rearrange("p (k t) -> p k t", t=W),
                                in_=p.chg(t["x2d"])).then_inc(ld, 16)
            nc.vector.wait_ge(ld, p.rV)
            nc.vector.reg_add(p.rV, p.rV, 48)
            p.bar()
            for m in range(8):
                pp = m % 2
                for k in range(32):
                    p.mmN(psA, pp * 1024,
                          fc2w[:, (m * 32 + k) * 128:(m * 32 + k + 1) * 128],
                          glb[:, k * W:(k + 1) * W], W,
                          start=(k == 0), stop=(k == 31))
                p.bar()
                V.tensor_scalar(t["qtmp"][:, 0:W], psA[:, pp * 1024:pp * 1024 + W],
                                t["fc2bb"][:, m:m + 1], None, ALU.add)
                V.tensor_tensor(t["qtmp"][:, 0:W], t["qtmp"][:, 0:W],
                                xb1[:, m * W:(m + 1) * W], ALU.add)
                V.tensor_tensor(stg[:, m * W:(m + 1) * W], t["qtmp"][:, 0:W],
                                x2[:, m * W:(m + 1) * W], ALU.add)
                p.bar()
            nc.sync.dma_start(out=p.chs(t["x3d"]),
                              in_=stg.rearrange("p (k t) -> p k t", t=W)
                              ).then_inc(st, 16)
        nc.sync.wait_ge(st, 16 * ntile)


def emit_phase2(nc, t, ntok, stages=8):
    p = P2(nc, t, ntok)
    t["p2obj"] = p
    p.edges_init()
    emit_wprep(p, t["w1T"], t["w1qd"], "w1sc", 8, 32, "1")
    emit_wprep(p, t["w2T"], t["w2qd"], "w2sc", 32, 8, "2")
    if stages >= 1:
        emit_s1(p)
    if stages >= 2:
        emit_ffn1(p, "x1d", "s2a")
    if stages >= 3:
        emit_ffn2(p, "x1d", "x2d", "s2b")
    if stages >= 4:
        emit_s3a(p)
    if stages >= 5:
        emit_s3a2(p)
    if stages >= 6:
        emit_s3b1(p)
    if stages >= 7:
        emit_s3b2(p)
    if stages >= 8:
        emit_ffn1(p, "x3d", "s4a")
    if stages >= 9:
        emit_ffn2(p, "x3d", "outd", "s4b")



def emit_phase_a(nc, t, nstep):
    V, S = _wrap(nc.vector), _wrap(nc.scalar)
    ncha = nstep // CHA
    xt, wx, gd, biasb = t["xt"], t["wx"], t["gd"], t["biasb"]
    onescol, ones128 = t["onescol"], t["ones128"]
    wxb, xbuf, sqb, s1b, t8 = t["wxb"], t["xbuf"], t["sqb"], t["s1b"], t["t8"]
    trow, nmub, invsb, gstage = t["trow"], t["nmub"], t["invsb"], t["gstage"]
    psS, psB, psP = t["psS"], t["psB"], t["psP"]
    (a_wl, a_xl, a_sq, a_mm2, a_mmS, a_ew1, a_iv, a_rows, a_mmB, a_s1, a_pg,
     a_gact, a_st, a_pgc) = (
        t["a_wl"], t["a_xl"], t["a_sq"], t["a_mm2"], t["a_mmS"], t["a_ew1"],
        t["a_iv"], t["a_rows"], t["a_mmB"], t["a_s1"], t["a_pg"], t["a_gact"],
        t["a_st"], t["a_pgc"])

    rQ = nc.scalar.alloc_register("rQ")
    rTsq = nc.tensor.alloc_register("rTsq")
    rTg = nc.tensor.alloc_register("rTg")
    rVp = nc.vector.alloc_register("rVp")
    nc.scalar.reg_mov(rQ, 1)
    nc.tensor.reg_mov(rTsq, 1)
    nc.tensor.reg_mov(rTg, 1)
    nc.vector.reg_mov(rVp, 1)
    nc.vector.sem_inc(a_gact, 2)

    for k in range(8):
        nc.gpsimd.dma_start(
            out=wxb.rearrange("p (m k c) -> p m k c", k=8, c=128)[:, :, k, :],
            in_=wx[k * 128:(k + 1) * 128, :].rearrange("p (m c) -> p m c", c=128),
        ).then_inc(a_wl, 16)

    with nc.Fori(0, ncha) as c:
        nc.gpsimd.wait_ge(a_s1, 8 * c)
        nc.gpsimd.wait_ge(a_sq, 8 * c)
        nc.gpsimd.wait_ge(a_pgc, c)
        for k in range(8):
            nc.gpsimd.dma_start(
                out=xbuf[:, k * TCA:(k + 1) * TCA],
                in_=xt[k * 128:(k + 1) * 128, :][:, ds(c * TCA, TCA)],
            ).then_inc(a_xl, 16)
        nc.tensor.wait_ge(a_wl, 128)
        nc.tensor.wait_ge(a_xl, (c + 1) * 128)
        nc.tensor.wait_ge(a_rows, c)
        for k in range(8):
            nc.tensor.matmul(psS[:, 0:TCA], onescol[:], xbuf[:, k * TCA:(k + 1) * TCA],
                             start=(k == 0), stop=(k == 7))
        nc.scalar.wait_ge(a_xl, (c + 1) * 128)
        for k in range(8):
            pq = k % 2
            if k < 2:
                nc.scalar.wait_ge(a_mmS, c)
            else:
                nc.scalar.wait_ge(a_mm2, rQ)
                nc.scalar.reg_add(rQ, rQ, 1)
            S.activation(out=sqb[:, pq * TCA:(pq + 1) * TCA],
                         in_=xbuf[:, k * TCA:(k + 1) * TCA],
                         func=AF.Square).then_inc(a_sq, 1)
            nc.tensor.wait_ge(a_sq, rTsq)
            nc.tensor.reg_add(rTsq, rTsq, 1)
            inst = nc.tensor.matmul(psS[:, TCA:2 * TCA], onescol[:],
                                    sqb[:, pq * TCA:(pq + 1) * TCA],
                                    start=(k == 0), stop=(k == 7))
            inst.then_inc(a_mmS if k == 7 else a_mm2, 1)
        nc.scalar.reg_add(rQ, rQ, 1)
        MU, VAR, NMU = 0, TCA, 2 * TCA
        nc.vector.wait_ge(a_mmS, c + 1)
        V.tensor_scalar_mul(trow[:, MU:MU + TCA], psS[:, 0:TCA], 1.0 / D)
        V.tensor_scalar_mul(trow[:, VAR:VAR + TCA], psS[:, TCA:2 * TCA], 1.0 / D)
        V.tensor_scalar_mul(trow[:, NMU:NMU + TCA], trow[:, MU:MU + TCA], -1.0)
        V.tensor_copy(nmub[:], trow[:, NMU:NMU + TCA])
        V.tensor_tensor(trow[:, MU:MU + TCA], trow[:, MU:MU + TCA],
                        trow[:, MU:MU + TCA], ALU.mult)
        V.tensor_tensor(trow[:, VAR:VAR + TCA], trow[:, VAR:VAR + TCA],
                        trow[:, MU:MU + TCA], ALU.subtract).then_inc(a_ew1, 1)
        nc.scalar.wait_ge(a_ew1, c + 1)
        S.activation(out=trow[:, VAR:VAR + TCA], in_=trow[:, VAR:VAR + TCA],
                     func=AF.Ln, bias=t["epsb"][:, 0:1])
        S.activation(out=trow[:, VAR:VAR + TCA], in_=trow[:, VAR:VAR + TCA],
                     func=AF.Exp, scale=-0.5).then_inc(a_iv, 1)
        nc.vector.wait_ge(a_iv, c + 1)
        V.tensor_copy(invsb[:], trow[:, VAR:VAR + TCA]).then_inc(a_rows, 1)
        nc.tensor.wait_ge(a_rows, c + 1)
        nc.tensor.wait_ge(a_s1, 8 * c)
        nc.tensor.matmul(psB[:, 0:TCA], ones128[:], nmub[:], start=True, stop=True)
        nc.tensor.matmul(psB[:, TCA:2 * TCA], ones128[:], invsb[:],
                         start=True, stop=True).then_inc(a_mmB, 1)
        nc.vector.wait_ge(a_mmB, c + 1)
        for k in range(8):
            V.tensor_tensor(t8[:], xbuf[:, k * TCA:(k + 1) * TCA],
                            psB[:, 0:TCA], ALU.add)
            V.tensor_tensor(s1b[:, k * TCA:(k + 1) * TCA], t8[:],
                            psB[:, TCA:2 * TCA], ALU.mult).then_inc(a_s1, 1)
        nc.tensor.wait_ge(a_s1, 8 * (c + 1))
        for m in range(NMG):
            pp = m % 2
            nc.tensor.wait_ge(a_gact, rTg)
            nc.tensor.reg_add(rTg, rTg, 1)
            for k in range(8):
                inst = nc.tensor.matmul(psP[:, pp * TCA:(pp + 1) * TCA],
                                        wxb[:, (m * 8 + k) * 128:(m * 8 + k + 1) * 128],
                                        s1b[:, k * TCA:(k + 1) * TCA],
                                        start=(k == 0), stop=(k == 7))
            inst.then_inc(a_pgc if m == NMG - 1 else a_pg, 1)
            if m == NMG - 1:
                nc.vector.wait_ge(a_pgc, c + 1)
            else:
                nc.vector.wait_ge(a_pg, rVp)
                nc.vector.reg_add(rVp, rVp, 1)
            if m == 0:
                nc.vector.wait_ge(a_st, 16 * c)
            V.tensor_scalar_add(gstage[:, m * TCA:(m + 1) * TCA],
                                psP[:, pp * TCA:(pp + 1) * TCA],
                                biasb[:, m:m + 1]).then_inc(a_gact, 1)
        nc.sync.wait_ge(a_gact, 2 + (c + 1) * NMG)
        nc.sync.dma_start(
            out=gd.rearrange("p (m t) -> p m t", m=NMG)[:, :, ds(c * TCA, TCA)],
            in_=gstage.rearrange("p (m t) -> p m t", m=NMG),
        ).then_inc(a_st, 16)
    nc.sync.wait_ge(a_st, ncha * 16)



def emit_phase_b(nc, t, nstep, record_hs=True):
    V, S = _wrap(nc.vector), _wrap(nc.scalar)
    nbody = nstep // (2 * INNER)
    wh, weg, gd, begb, hsout = t["wh"], t["weg"], t["gd"], t["begb"], t["hsout"]
    whb, gbuf = t["whb"], t["gbuf"]
    hT, gT, stc, stn = t["hT"], t["gT"], t["stc"], t["stn"]
    zg, zifo, sifo, gf, ef, tmp, tmp2, rn, hf, hsb = (
        t["zg"], t["zifo"], t["sifo"], t["gf"], t["ef"], t["tmp"], t["tmp2"],
        t["rn"], t["hf"], t["hsb"])
    psG, psI, psW = t["psG"], t["psI"], t["psW"]
    (b_wl, b_gl, b_mg, b_mi, b_mw, b_eg, b_ag, b_gt, b_ei, b_ai, b_ae,
     b_ec, b_hs, b_mo, b_eo, b_ao) = (
        t["b_wl"], t["b_gl"], t["b_mg"], t["b_mi"], t["b_mw"], t["b_eg"],
        t["b_ag"], t["b_gt"], t["b_ei"], t["b_ai"], t["b_ae"], t["b_ec"],
        t["b_hs"], t["b_mo"], t["b_eo"], t["b_ao"])

    rT = nc.tensor.alloc_register("rT")
    rV = nc.vector.alloc_register("rV")
    rS = nc.scalar.alloc_register("rS")
    nc.tensor.reg_mov(rT, 1)
    nc.vector.reg_mov(rV, 1)
    nc.scalar.reg_mov(rS, 1)

    for k in range(8):
        nc.gpsimd.dma_start(
            out=whb.rearrange("p (m k c) -> p m k c", k=8, c=128)[:, 0:NMG, k, :],
            in_=wh[k * 128:(k + 1) * 128, :].rearrange("p (m c) -> p m c", c=128),
        ).then_inc(b_wl, 16)
        nc.gpsimd.dma_start(
            out=whb.rearrange("p (m k c) -> p m k c", k=8, c=128)[:, NMG:NT, k, :],
            in_=weg[k * 128:(k + 1) * 128, :].rearrange("p (m c) -> p m c", c=128),
        ).then_inc(b_wl, 16)
    nc.vector.memset(hT[:], 0.0)
    nc.vector.drain()
    nc.vector.memset(stc[:], 0.0)
    nc.vector.drain()
    nc.vector.memset(stn[:], 0.0)
    nc.vector.drain()
    nc.vector.sem_inc(b_ec, 1)
    nc.tensor.wait_ge(b_wl, 256)

    with nc.Fori(0, nbody) as cb:
        nc.gpsimd.wait_ge(b_eo, 16 * cb)
        for H in range(2):
            nc.gpsimd.dma_start(
                out=gbuf[:, H * INNER * GSC:(H + 1) * INNER * GSC].rearrange(
                    "p (m t) -> p m t", m=NMG),
                in_=gd.rearrange("p (m t) -> p m t", m=NMG)[
                    :, :, ds(cb * 2 * INNER * B + H * INNER * B, INNER * B)],
            ).then_inc(b_gl, 16)
        for j in range(2 * INNER):
            H = j // INNER
            pa = j % 2
            gt_ap = gbuf.rearrange("p (h m t) -> p h m t", h=2, m=NMG)[
                :, H, :, (j % INNER) * 8:(j % INNER + 1) * 8]
            nc.tensor.wait_ge(b_ec, rT)
            for m in range(8):
                for k in range(8):
                    inst = nc.tensor.matmul(
                        psG[:, pa * 64 + m * 8:pa * 64 + (m + 1) * 8],
                        whb[:, (m * 8 + k) * 128:(m * 8 + k + 1) * 128],
                        hT[:, k * 8:(k + 1) * 8],
                        start=(k == 0), stop=(k == 7))
            inst.then_inc(b_mg, 1)
            for m in range(8, 24):
                for k in range(8):
                    inst = nc.tensor.matmul(
                        psI[:, pa * 192 + (m - 8) * 8:pa * 192 + (m - 7) * 8],
                        whb[:, (m * 8 + k) * 128:(m * 8 + k + 1) * 128],
                        hT[:, k * 8:(k + 1) * 8],
                        start=(k == 0), stop=(k == 7))
            inst.then_inc(b_mi, 1)
            nc.vector.wait_ge(b_mg, rV)
            if j == 0:
                nc.vector.wait_ge(b_gl, (cb + 1) * 32)
            V.tensor_tensor(
                zg.rearrange("p (m t) -> p m t", m=8),
                psG[:, pa * 64:pa * 64 + 64].rearrange("p (m t) -> p m t", m=8),
                gt_ap[:, 0:8, :], ALU.add).then_inc(b_eg, 1)
            nc.scalar.wait_ge(b_eg, rS)
            S.activation(out=gf[:], in_=zg[:], func=AF.Tanh).then_inc(b_ag, 1)
            nc.vector.wait_ge(b_ag, rV)
            V.tensor_copy(gT[:], gf[:]).then_inc(b_gt, 1)
            nc.tensor.wait_ge(b_gt, rT)
            nc.tensor.reg_add(rT, rT, 1)
            for m in range(NMG, NT):
                mo = m - NMG
                for k in range(8):
                    inst = nc.tensor.matmul(
                        psW[:, pa * 64 + mo * 8:pa * 64 + (mo + 1) * 8],
                        whb[:, (m * 8 + k) * 128:(m * 8 + k + 1) * 128],
                        gT[:, k * 8:(k + 1) * 8],
                        start=(k == 0), stop=(k == 7))
            inst.then_inc(b_mw, 1)
            for m in range(24, 32):
                for k in range(8):
                    inst = nc.tensor.matmul(
                        psI[:, pa * 192 + (m - 8) * 8:pa * 192 + (m - 7) * 8],
                        whb[:, (m * 8 + k) * 128:(m * 8 + k + 1) * 128],
                        hT[:, k * 8:(k + 1) * 8],
                        start=(k == 0), stop=(k == 7))
            inst.then_inc(b_mo, 1)
            nc.vector.wait_ge(b_mi, rV)
            V.tensor_tensor(
                zifo.rearrange("p (m t) -> p m t", m=24)[:, 0:16, :],
                psI[:, pa * 192:pa * 192 + 128].rearrange("p (m t) -> p m t", m=16),
                gt_ap[:, 8:24, :], ALU.add).then_inc(b_ei, 1)
            nc.scalar.wait_ge(b_ei, rS)
            S.activation(out=sifo[:, 0:128], in_=zifo[:, 0:128],
                         func=AF.Sigmoid).then_inc(b_ai, 1)
            nc.scalar.wait_ge(b_mw, rS)
            for mo in range(NMW):
                inst = S.activation(
                    out=ef[:, mo * 8:(mo + 1) * 8],
                    in_=psW[:, pa * 64 + mo * 8:pa * 64 + (mo + 1) * 8],
                    func=AF.Exp, bias=begb[:, mo:mo + 1])
            inst.then_inc(b_ae, 1)
            SI, SF, SO = 0, 64, 128
            nc.vector.wait_ge(b_ai, rV)
            nc.vector.wait_ge(b_ae, rV)
            if record_hs and j == 0:
                nc.vector.wait_ge(b_hs, 32 * cb)
            V.tensor_tensor(stc[:], sifo[:, SF:SF + 64], stc[:], ALU.mult)
            V.tensor_tensor(tmp[:], sifo[:, SI:SI + 64], gf[:], ALU.mult)
            V.tensor_tensor(stc[:], stc[:], tmp[:], ALU.add)
            V.tensor_tensor(stn[:], sifo[:, SF:SF + 64], stn[:], ALU.mult)
            V.tensor_tensor(tmp2[:], sifo[:, SI:SI + 64], ef[:], ALU.mult)
            V.tensor_tensor(stn[:], stn[:], tmp2[:], ALU.add)
            V.reciprocal(rn[:], stn[:])
            V.tensor_tensor(tmp[:], stc[:], rn[:], ALU.mult)
            nc.vector.wait_ge(b_mo, rV)
            V.tensor_tensor(
                zifo.rearrange("p (m t) -> p m t", m=24)[:, 16:24, :],
                psI[:, pa * 192 + 128:pa * 192 + 192].rearrange(
                    "p (m t) -> p m t", m=8),
                gt_ap[:, 24:32, :], ALU.add).then_inc(b_eo, 1)
            nc.scalar.wait_ge(b_eo, rS)
            nc.scalar.reg_add(rS, rS, 1)
            S.activation(out=sifo[:, SO:SO + 64], in_=zifo[:, 128:192],
                         func=AF.Sigmoid).then_inc(b_ao, 1)
            nc.vector.wait_ge(b_ao, rV)
            nc.vector.reg_add(rV, rV, 1)
            V.tensor_tensor(hf[:], sifo[:, SO:SO + 64], tmp[:], ALU.mult)
            if record_hs:
                V.tensor_copy(
                    hsb.rearrange("p (m s) -> p m s", s=2 * INNER)[:, :, j],
                    hf.rearrange("p (m b) -> p m b", b=8)[:, :, 0])
            V.tensor_copy(hT[:], hf[:]).then_inc(b_ec, 1)
            if record_hs and (j == INNER - 1 or j == 2 * INNER - 1):
                nc.sync.wait_ge(b_ec, 16 * cb + (H + 1) * INNER + 1)
                nc.sync.dma_start(
                    out=hsout.rearrange("p (m t) -> p m t", m=8)[
                        :, :, ds(16 * cb + H * INNER, INNER)],
                    in_=hsb.rearrange("p (m s) -> p m s", s=2 * INNER)[
                        :, :, H * INNER:(H + 1) * INNER],
                ).then_inc(b_hs, 16)
    if record_hs:
        nc.sync.wait_ge(b_hs, nbody * 32)


def host_tile_lhsT(w):
    K, M = w.shape
    nk, nm = K // 128, M // 128
    out = np.empty((128, nm * nk * 128), np.float32)
    for m in range(nm):
        for k in range(nk):
            out[:, (m * nk + k) * 128:(m * nk + k + 1) * 128] = \
                w[k * 128:(k + 1) * 128, m * 128:(m + 1) * 128]
    return out


def host_tile_lhsT64(w):
    K, M = w.shape
    nk, nm = K // 64, M // 128
    out = np.empty((64, nm * nk * 128), np.float32)
    for m in range(nm):
        for kh in range(nk):
            out[:, (m * nk + kh) * 128:(m * nk + kh + 1) * 128] = \
                w[kh * 64:(kh + 1) * 64, m * 128:(m + 1) * 128]
    return out


def perp(v):
    C = v.shape[0]
    return np.ascontiguousarray(v.reshape(C // 128, 128).T)



def build_full(nstep, ntok, stages=9):
    nc = bacc_mod.Bacc("TRN2", target_bir_lowering=False)
    t = {}
    TP = ntok + 2 * PT
    def din(name, shape):
        t[name] = nc.dram_tensor(name, shape, F32, kind="ExternalInput").ap()
    def dint(name, shape, dtype=F32):
        t[name] = nc.dram_tensor(name, shape, dtype, kind="Internal").ap()
    din("xb", [D, ntok])
    dint("hd", [128, 8 * ntok])
    din("sxt", [D, nstep * B]); din("wx", [D, 4 * SL]); din("wh", [D, 4 * SL])
    din("weg", [D, SL]); din("biasT", [128, NMG]); din("begT", [128, NMW])
    dint("gd", [128, NMG * (nstep + 2 * INNER) * B])
    din("identin", [128, 128])
    din("w1T", [D, DF]); din("w2T", [DF, D])
    din("qkvwin", [128, 24 * 8 * 128]); din("projwin", [64, 8 * 16 * 128])
    din("fc1win", [128, 32 * 8 * 128]); din("fc2win", [128, 8 * 32 * 128])
    din("vecsin", [128, 8 + 8 + 8 + 8 + 8 + 8 + 24 + 8 + 32 + 8 + 96 + 32])
    t["outd"] = nc.dram_tensor("outd", [128, 8 * ntok], F32, kind="ExternalOutput").ap()
    for nmx in ("x1d", "x2d", "x3d", "xb1d"):
        dint(nmx, [128, 8 * TP])
    dint("k2d", [128, 32 * ntok], BF16)
    dint("attd", [64, 16 * ntok], BF16)
    dint("gld", [128, 32 * ntok], BF16)
    dint("oscd", [1, ntok])
    dint("w1qd", [128, 8 * DF], BF16)
    dint("w2qd", [128, 32 * D], BF16)

    ts = {"xt": t["sxt"], "wx": t["wx"], "wh": t["wh"], "weg": t["weg"],
          "biasT": t["biasT"], "begT": t["begT"], "hsout": t["hd"],
          "gd": t["gd"]}
    scan_sems = []
    def ssem(name):
        h = nc.alloc_semaphore(name="sc_" + name)
        ts[name] = h
        scan_sems.append(h)
    esta = contextlib.ExitStack()
    estb = contextlib.ExitStack()
    estc = contextlib.ExitStack()
    def ssb(est_, name, shape, dtype=F32):
        ts[name] = est_.enter_context(nc.sbuf_tensor("sc_" + name, shape, dtype))
    def spsum(est_, name, shape):
        ts[name] = est_.enter_context(nc.psum_tensor("sc_" + name, shape, F32))
    for s in ["s_init", "a_wl", "a_xl", "a_sq", "a_mm2", "a_mmS", "a_ew1",
              "a_iv", "a_rows", "a_mmB", "a_s1", "a_pg", "a_gact", "a_st",
              "a_pgc", "b_wl", "b_gl", "b_mg", "b_mi", "b_mw", "b_eg", "b_ag",
              "b_gt", "b_ei", "b_ai", "b_ae", "b_ec", "b_hs", "b_mo", "b_eo",
              "b_ao"]:
        ssem(s)
    ssb(estc, "ones128", [1, 128], BF16); ssb(estc, "onescol", [128, 1], BF16)
    ssb(estc, "epsb", [1, 1])
    ssb(estc, "biasb", [128, NMG]); ssb(estc, "begb", [128, NMW])
    ssb(esta, "wxb", [128, 8 * 4 * SL], BF16)
    ssb(esta, "xbuf", [128, 8 * TCA], BF16)
    ssb(esta, "sqb", [128, 2 * TCA], BF16)
    ssb(esta, "s1b", [128, 8 * TCA], BF16)
    ssb(esta, "t8", [128, TCA])
    ssb(esta, "trow", [1, 3 * TCA])
    ssb(esta, "nmub", [1, TCA], BF16); ssb(esta, "invsb", [1, TCA], BF16)
    ssb(esta, "gstage", [128, NMG * TCA])
    spsum(esta, "psS", [1, 2 * TCA]); spsum(esta, "psB", [128, 2 * TCA])
    spsum(esta, "psP", [128, 2 * TCA])
    Vs = _wrap(nc.vector)
    nc.vector.memset(ts["epsb"][:], LN_EPS); nc.vector.drain()
    nc.vector.memset(ts["ones128"][:], 1.0); nc.vector.drain()
    nc.vector.memset(ts["onescol"][:], 1.0); nc.vector.drain()
    nc.sync.dma_start(out=ts["biasb"][:], in_=ts["biasT"][:]).then_inc(ts["s_init"], 16)
    nc.sync.dma_start(out=ts["begb"][:], in_=ts["begT"][:]).then_inc(ts["s_init"], 16)
    nc.vector.wait_ge(ts["s_init"], 32)
    nc.all_engine_barrier()
    emit_phase_a(nc, ts, nstep)
    nc.all_engine_barrier()
    esta.close()
    ssb(estb, "whb", [128, 8 * NT * 128], BF16)
    ssb(estb, "gbuf", [128, 2 * INNER * GSC])
    ssb(estb, "hT", [128, 64], BF16); ssb(estb, "gT", [128, 64], BF16)
    ssb(estb, "stc", [128, 64]); ssb(estb, "stn", [128, 64])
    ssb(estb, "zg", [128, 64]); ssb(estb, "zifo", [128, 192])
    ssb(estb, "sifo", [128, 192])
    ssb(estb, "gf", [128, 64]); ssb(estb, "ef", [128, 64])
    ssb(estb, "tmp", [128, 64]); ssb(estb, "tmp2", [128, 64])
    ssb(estb, "rn", [128, 64]); ssb(estb, "hf", [128, 64])
    ssb(estb, "hsb", [128, 2 * INNER * 8])
    spsum(estb, "psG", [128, 2 * 64]); spsum(estb, "psI", [128, 2 * 192])
    spsum(estb, "psW", [128, 2 * 64])
    emit_phase_b(nc, ts, nstep, record_hs=True)
    nc.all_engine_barrier()
    estb.close()
    estc.close()
    nc.clear_and_free_semaphores(scan_sems)
    nc.all_engine_barrier()
    nc.all_engine_barrier()
    est = contextlib.ExitStack()
    def sb(name, shape, dtype=F32):
        t[name] = est.enter_context(nc.sbuf_tensor(name, shape, dtype))
    def psum(name, shape, dtype=F32):
        t[name] = est.enter_context(nc.psum_tensor(name, shape, dtype))
    def sem(name):
        t[name] = est.enter_context(nc.semaphore(name))
    sb("ones128", [1, 128], BF16); sb("ones64", [1, 64], BF16)
    sb("onescol", [128, 1], BF16); sb("onescf", [128, 1]); sb("ones128f", [1, 128]); sb("ones64f", [1, 64]); sb("epsb", [1, 1])
    sb("ident", [128, 128]); sb("identb", [128, 128], BF16)
    sb("sqb", [128, 4 * 768])
    sb("rowb", [1, 3 * 768])
    sb("qtmp", [128, 768]); sb("qtmp2", [128, 768])
    sb("kint", [128, 768], I32)
    sb("fold", [128, 768])
    sb("colcl", [128, 8]); sb("cl2", [128, 8]); sb("clb", [128, 16], BF16)
    sb("qrow", [1, 2 * 768])
    sb("oscb", [128, 768]); sb("sxb", [128, 768])
    sb("nmuB", [128, 768]); sb("invsB", [128, 768])
    sb("zrow", [128, 8 * PT])
    sb("gammab", [128, 8]); sb("betab", [128, 8])
    sb("ln1wb", [128, 8]); sb("ln1bb", [128, 8])
    sb("ln2wb", [128, 8]); sb("ln2bb", [128, 8])
    sb("qkvbb", [128, 24]); sb("projbb", [128, 8])
    sb("fc1bb", [128, 32]); sb("fc2bb", [128, 8])
    sb("convw", [128, 96]); sb("convb", [128, 32])
    sb("w1sc", [1, 1]); sb("w2sc", [1, 1])
    sb("w1scc", [128, 1]); sb("w2scc", [128, 1])
    sb("vecs", [128, 256])
    psum("psA", [128, 2048]); psum("psB", [128, 1536])
    psum("psD", [128, 384], BF16)
    for s in ["s_init", "ld_w1", "st_w1", "ld_w2", "st_w2",
              "ld_s1", "st_s1",
              "ld_s2a", "lw_s2a", "st_s2a", "ld_s2b", "lw_s2b", "st_s2b",
              "ld_s3a", "lw_s3a", "st_s3a", "ld_s3a2", "lw_s3a2", "st_s3a2",
              "ld_s3b1", "lw_s3b1", "st_s3b1", "ld_s3b2", "lw_s3b2", "st_s3b2",
              "ld_s4a", "lw_s4a", "st_s4a", "ld_s4b", "lw_s4b", "st_s4b"]:
        sem(s)
    t["qkvwd"] = t["qkvwin"]; t["projwd"] = t["projwin"]
    t["fc1wd"] = t["fc1win"]; t["fc2wd"] = t["fc2win"]

    si = t["s_init"]
    nc.sync.dma_start(out=t["ident"][:], in_=t["identin"][:]).then_inc(si, 16)
    nc.sync.dma_start(out=t["vecs"][:, 0:248], in_=t["vecsin"][:, 0:248]).then_inc(si, 16)
    nc.vector.wait_ge(si, 32)
    V = _wrap(nc.vector)
    V.tensor_copy(t["identb"][:], t["ident"][:])
    nc.vector.memset(t["ones128"][:], 1.0); nc.vector.drain()
    nc.vector.memset(t["ones64"][:], 1.0); nc.vector.drain()
    nc.vector.memset(t["onescol"][:], 1.0); nc.vector.drain()
    nc.vector.memset(t["onescf"][:], 1.0); nc.vector.drain()
    nc.vector.memset(t["ones128f"][:], 1.0); nc.vector.drain()
    nc.vector.memset(t["ones64f"][:], 1.0); nc.vector.drain()
    nc.vector.memset(t["epsb"][:], LN_EPS); nc.vector.drain()
    nc.vector.memset(t["zrow"][:], 0.0); nc.vector.drain()
    off = 0
    for name, w in [("gammab", 8), ("betab", 8), ("ln1wb", 8), ("ln1bb", 8),
                    ("ln2wb", 8), ("ln2bb", 8), ("qkvbb", 24), ("projbb", 8),
                    ("fc1bb", 32), ("fc2bb", 8), ("convw", 96), ("convb", 32)]:
        V.tensor_copy(t[name][:], t["vecs"][:, off:off + w])
        off += w
    nc.vector.sem_inc(si, 1)
    nc.sync.wait_ge(si, 33)
    for nmx in ("x1d", "x2d", "x3d"):
        dv = t[nmx].rearrange("p (k t) -> p k t", t=ntok + 2 * PT)
        nc.sync.dma_start(out=dv[:, :, 0:PT],
                          in_=t["zrow"].rearrange("p (k t) -> p k t", t=PT)
                          ).then_inc(si, 16)
        nc.sync.dma_start(out=dv[:, :, ntok + PT:ntok + 2 * PT],
                          in_=t["zrow"].rearrange("p (k t) -> p k t", t=PT)
                          ).then_inc(si, 16)
    nc.sync.wait_ge(si, 33 + 96)
    nc.all_engine_barrier()
    emit_phase2(nc, t, ntok, stages)
    nc.all_engine_barrier()
    est.close()
    return nc




NSTEP_FULL = 4096
_CACHE = {}


def _get_nc():
    if "nc" not in _CACHE:
        nc = build_full(NSTEP_FULL, NSTEP_FULL, 9)
        nc.compile()
        _CACHE["nc"] = nc
    return _CACHE["nc"]


def kernel(**inp):
    f32 = np.float32
    g = lambda k: np.ascontiguousarray(np.asarray(inp[k], dtype=f32))
    x = g("L0_attn_a_x")
    Wi, Wf = g("L0_attn_a_Wi"), g("L0_attn_a_Wf")
    Wg, Wo = g("L0_attn_a_Wg"), g("L0_attn_a_Wo")
    Weg = g("L0_attn_a_Weg")
    bi, bf = g("L0_attn_a_bi"), g("L0_attn_a_bf")
    bg, bo = g("L0_attn_a_bg"), g("L0_attn_a_bo")
    beg = g("L0_attn_a_beg")
    wx = np.ascontiguousarray(np.concatenate([Wg[:D], Wi[:D], Wf[:D], Wo[:D]], 1))
    wh = np.ascontiguousarray(np.concatenate([Wg[D:], Wi[D:], Wf[D:], Wo[D:]], 1))
    biasT = np.ascontiguousarray(np.concatenate([bg, bi, bf, bo]).reshape(NMG, 128).T)
    begT = perp(beg)
    Wd = {
        "gamma": g("L0_attn_a_gamma"), "beta": g("L0_attn_a_beta"),
        "ln1_w": g("L0_attn_b_ln1_w"), "ln1_b": g("L0_attn_b_ln1_b"),
        "ln2_w": g("L0_attn_b_ln2_w"), "ln2_b": g("L0_attn_b_ln2_b"),
        "qkv_w": g("L0_attn_b_qkv_w"), "qkv_b": g("L0_attn_b_qkv_b"),
        "proj_w": g("L0_attn_b_proj_w"), "proj_b": g("L0_attn_b_proj_b"),
        "fc1_w": g("L0_attn_b_fc1_w"), "fc1_b": g("L0_attn_b_fc1_b"),
        "fc2_w": g("L0_attn_b_fc2_w"), "fc2_b": g("L0_attn_b_fc2_b"),
        "w1": g("L0_ffn_w1"), "w2": g("L0_ffn_w2"),
        "conv_w": g("L0_ffn_conv_w"), "conv_b": g("L0_ffn_conv_b"),
    }
    vecs = np.concatenate([
        perp(Wd["gamma"]), perp(Wd["beta"]), perp(Wd["ln1_w"]), perp(Wd["ln1_b"]),
        perp(Wd["ln2_w"]), perp(Wd["ln2_b"]), perp(Wd["qkv_b"]), perp(Wd["proj_b"]),
        perp(Wd["fc1_b"]), perp(Wd["fc2_b"]),
        np.ascontiguousarray(Wd["conv_w"].reshape(32, 128, 3).transpose(1, 0, 2)
                             .reshape(128, 96)),
        perp(Wd["conv_b"])], axis=1).astype(np.float32)
    shared = {
        "wx": wx, "wh": wh, "weg": np.ascontiguousarray(Weg),
        "biasT": biasT, "begT": begT,
        "identin": np.eye(128, dtype=f32),
        "w1T": np.ascontiguousarray(Wd["w1"].T),
        "w2T": np.ascontiguousarray(Wd["w2"].T),
        "qkvwin": host_tile_lhsT(Wd["qkv_w"].T),
        "projwin": host_tile_lhsT64(Wd["proj_w"].T),
        "fc1win": host_tile_lhsT(Wd["fc1_w"].T),
        "fc2win": host_tile_lhsT(Wd["fc2_w"].T),
        "vecsin": vecs,
    }
    S = x.shape[1]
    in_maps = []
    for c in range(8):
        perm = [(c + j) % 8 for j in range(8)]
        m = dict(shared)
        m["sxt"] = np.ascontiguousarray(x[perm].transpose(2, 1, 0).reshape(D, S * B))
        m["xb"] = np.ascontiguousarray(x[c].T)
        in_maps.append(m)
    res = run_bass_kernel_spmd(_get_nc(), in_maps, core_ids=list(range(8)))
    out = np.empty((8, S, D), f32)
    for c in range(8):
        out[c] = (res.results[c]["outd"].reshape(128, 8, S)
                  .transpose(2, 1, 0).reshape(S, D))
    return out
